# version 07
# baseline (speedup 1.0000x reference)
import numpy as np
import concourse.bass as bass
import concourse.mybir as mybir
from concourse.bass import IndirectOffsetOnAxis
from concourse.bass_utils import run_bass_kernel_spmd
from contextlib import ExitStack

F32 = mybir.dt.float32
BF16 = mybir.dt.bfloat16
I32 = mybir.dt.int32
AF = mybir.ActivationFunctionType
ALU = mybir.AluOpType
AXX = mybir.AxisListType.X

D = 1024
TC = 2048
T = 8192
HD = 64
NH = 16
E = 16
BEXP = 20.0
MAGIC = float(np.float32(1.5 * 2.0 ** 23))
LN10K = float(np.log(10000.0))
TWO_PI = float(2.0 * np.pi)

AR_COLS = 53248
OFF_TAB = 0
OFF_XN = 8192
OFF_WB = 24576
OFF_SC = 49152
WB0 = OFF_WB
WB1 = OFF_WB + 8192
WB2 = OFF_WB + 16384
A_QDUP = OFF_XN + 0
A_KDUP = OFF_XN + 2048
A_VAUG = OFF_XN + 10240
A_CTX = OFF_XN + 14464
A_VFM = WB0
A_WO = WB1
B_QDUP = OFF_XN + 0
B_KDUP = OFF_XN + 8192
B_VFM = WB0
B_VAUG = WB1
B_CTX = WB1 + 4224
B_EI = WB1 + 5248
U0 = OFF_SC
U1 = OFF_SC + 2048
VA_STRIDE = 66
SER_FFN = False
CAP = 1280
NCH = CAP // 128
WINS = [(0, 512), (512, 512), (1024, 256)]
NWIN = len(WINS)
CHWIN = [0, 0, 0, 0, 1, 1, 1, 1, 2, 2]


def host_prepare(inputs):
    f = np.ascontiguousarray
    x = np.asarray(inputs["L0_attn_a_hidden_states"])
    xT = f(np.asarray(x).transpose(0, 2, 1))
    wqT = f(np.asarray(inputs["L0_attn_a_Wq"]).T)
    wkT = f(np.asarray(inputs["L0_attn_a_Wk"]).T)
    wvT = f(np.asarray(inputs["L0_attn_a_Wv"]).T)
    woT = f(np.asarray(inputs["L0_attn_a_Wo"]).T)
    bq = np.asarray(inputs["L0_attn_a_bq"]); bk = np.asarray(inputs["L0_attn_a_bk"])
    bv = np.asarray(inputs["L0_attn_a_bv"]); bo = np.asarray(inputs["L0_attn_a_bo"])
    abias = np.zeros((128, 20), np.float32)
    for oc in range(8):
        abias[:, oc] = bq[oc * 128:(oc + 1) * 128]
        abias[:, 12 + oc] = bo[oc * 128:(oc + 1) * 128]
    for oc in range(2):
        abias[:, 8 + oc] = bk[oc * 128:(oc + 1) * 128]
        abias[:, 10 + oc] = bv[oc * 128:(oc + 1) * 128]
    qkvT = f(np.asarray(inputs["L0_attn_b_qkv_w"]).T)
    hperm = [r + 4 * m for r in range(4) for m in range(4)]
    projP = np.asarray(inputs["L0_attn_b_proj_w"]).reshape(D, NH, HD)[:, hperm, :].reshape(D, D)
    projT = f(projP.T)
    pb = np.asarray(inputs["L0_attn_b_proj_b"])
    pbias = np.zeros((128, 8), np.float32)
    for oc in range(8):
        pbias[:, oc] = pb[oc * 128:(oc + 1) * 128]
    slopes = np.asarray(inputs["L0_attn_b_slopes"])
    wgT = f(np.asarray(inputs["L0_ffn_Wg"]).T)
    w1T = f(np.asarray(inputs["L0_ffn_W1"]).transpose(0, 2, 1))
    w3T = f(np.asarray(inputs["L0_ffn_W3"]).transpose(0, 2, 1))
    w2T = f(np.asarray(inputs["L0_ffn_W2"]).transpose(0, 2, 1))
    b1c = f(np.asarray(inputs["L0_ffn_b1"]).reshape(E, 8, 128).transpose(0, 2, 1))
    b2c = f(np.asarray(inputs["L0_ffn_b2"]).reshape(E, 8, 128).transpose(0, 2, 1))
    b3c = f(np.asarray(inputs["L0_ffn_b3"]).reshape(E, 8, 128).transpose(0, 2, 1))
    b2s = np.asarray(inputs["L0_ffn_b2"]).astype(np.float32)
    shared = dict(wqT=wqT, wkT=wkT, wvT=wvT, woT=woT, abias=abias,
                  qkvT=qkvT, projT=projT, pbias=pbias,
                  wgT=wgT, w1T=w1T, w3T=w3T, w2T=w2T, b1c=b1c, b2c=b2c, b3c=b3c,
                  b2s=b2s)
    in_maps = []
    p = np.arange(128)
    for c in range(8):
        seq, g = c // 4, c % 4
        m = dict(shared)
        m["xin"] = f(xT[seq][:, g * TC:(g + 1) * TC]).astype(np.float32)
        m["posrow"] = np.arange(g * TC, (g + 1) * TC, dtype=np.float32).reshape(1, TC)
        m["slopes_rep"] = np.broadcast_to(
            slopes[[g, g + 4, g + 8, g + 12]].astype(np.float32), (128, 4)).copy()
        idx = np.zeros((128, 56), np.int32)
        col = 0
        for hh in range(4):
            for r in range(4):
                idx[:, col] = r * 3072 + g * 768 + 256 + hh * 64 + (p % 64); col += 1
        for hh in range(4):
            for r in range(4):
                idx[:, col] = r * 3072 + g * 768 + 0 + hh * 64 + (p % 64); col += 1
        for hh in range(4):
            for r in range(4):
                idx[:, col] = r * 3072 + g * 768 + 512 + hh * 64 + (p % 64); col += 1
        for pt in range(8):
            r = pt // 2
            mem = 2 * (pt % 2) + p // 64
            idx[:, col] = r * 1024 + g * 256 + mem * 64 + (p % 64); col += 1
        m["idxc"] = idx
        in_maps.append(m)
    return in_maps


def host_assemble(results):
    out = np.zeros((2, T, D), np.float32)
    for c, r in enumerate(results):
        seq, g = c // 4, c % 4
        out[seq, g * TC:(g + 1) * TC, :] = np.asarray(r["xout"]).T
    return out



def v3(ap2d, b):
    return ap2d.rearrange("p (a b) -> p a b", b=b)


class Sem:
    def __init__(self, nc, ctx, name, is_group=False):
        self.h = ctx.enter_context(nc.semaphore(name))
        self.n = 0
        self.is_group = is_group


class Bld:
    ENGS = ("pe", "act", "dve", "pool", "sp")
    DRAIN = {"act", "dve", "pool"}

    def __init__(self, nc, ctx):
        self.nc = nc
        self.ctx = ctx
        self.prog = {e: [] for e in self.ENGS}
        self.spe = Sem(nc, ctx, "spe")
        self.sact = Sem(nc, ctx, "sact")
        self.sdve = Sem(nc, ctx, "sdve")
        self.spool = Sem(nc, ctx, "spool")
        self.groups = {}

    def grp(self, name):
        if name not in self.groups:
            self.groups[name] = Sem(self.nc, self.ctx, "g_" + name, is_group=True)
        return self.groups[name]

    def on(self, eng, fn, waits=(), inc=None, inc_amt=1, drain=None):
        if drain is None:
            drain = eng in self.DRAIN
        for sem, val in waits:
            assert not getattr(sem, "is_group", False) or val == sem.n, \
                f"stale group wait: {val} != {sem.n}"
        if inc is not None:
            inc.n += inc_amt
        self.prog[eng].append((fn, tuple(waits), inc, inc_amt, drain))

    def dsp(self, grp, out, in_, waits=(), **kw):
        g = self.grp(grp)
        fn = lambda e, out=out, in_=in_, kw=kw: e.dma_start(out=out, in_=in_, **kw)
        self.on("sp", fn, waits, inc=g, inc_amt=16, drain=False)
        return (g, g.n)

    def dgp(self, grp, out, in_, waits=(), **kw):
        g = self.grp(grp)
        fn = lambda e, out=out, in_=in_, kw=kw: e.dma_start(out=out, in_=in_, **kw)
        self.on("pool", fn, waits, inc=g, inc_amt=16, drain=False)
        return (g, g.n)

    def dgi(self, grp, out, in_, idx_ap, waits=()):
        g = self.grp(grp)
        fn = lambda e, out=out, in_=in_, idx_ap=idx_ap: e.indirect_dma_start(
            out, None, in_, IndirectOffsetOnAxis(idx_ap, 0))
        self.on("pool", fn, waits, inc=g, inc_amt=16, drain=False)
        return (g, g.n)

    def dsi(self, grp, out, in_, idx_ap, waits=()):
        g = self.grp(grp)
        fn = lambda e, out=out, in_=in_, idx_ap=idx_ap: e.indirect_dma_start(
            out, None, in_, IndirectOffsetOnAxis(idx_ap, 0))
        self.on("pool", fn, waits, inc=g, inc_amt=16, drain=False)
        return (g, g.n)

    def dsis(self, grp, out, idx_ap, in_, waits=()):
        g = self.grp(grp)
        fn = lambda e, out=out, in_=in_, idx_ap=idx_ap: e.indirect_dma_start(
            out, IndirectOffsetOnAxis(idx_ap, 0), in_, None)
        self.on("pool", fn, waits, inc=g, inc_amt=16, drain=False)
        return (g, g.n)

    def coll(self, name, groups, ins, outs, waits=()):
        g = self.grp("cc_" + name)
        fn = lambda e, groups=groups, ins=ins, outs=outs: e.collective_compute(
            "AllGather", ALU.bypass, replica_groups=groups, ins=ins, outs=outs)
        self.on("pool", fn, waits, inc=g, inc_amt=1, drain=False)
        return (g, g.n)

    def pe_mark(self):
        return (self.spe, self.mark("pe"))

    def pool_mark(self):
        return (self.spool, self.mark("pool"))

    def tpe(self):
        return (self.spe, self.spe.n)

    def tact(self):
        return (self.sact, self.sact.n)

    def tdve(self):
        return (self.sdve, self.sdve.n)

    def tpool(self):
        return (self.spool, self.spool.n)

    def pe(self, fn, waits=(), inc=False):
        self.on("pe", fn, waits, inc=self.spe if inc else None, drain=False)
        return self.spe.n

    def act(self, fn, waits=(), inc=False, drain=None):
        self.on("act", fn, waits, inc=self.sact if inc else None, drain=drain)
        return self.sact.n

    def dve(self, fn, waits=(), inc=False):
        self.on("dve", fn, waits, inc=self.sdve if inc else None)
        return self.sdve.n

    def pool(self, fn, waits=(), inc=False):
        self.on("pool", fn, waits, inc=self.spool if inc else None)
        return self.spool.n

    def mark(self, eng):
        sems = {"pe": self.spe, "act": self.sact, "dve": self.sdve,
                "pool": self.spool}
        sem = sems[eng]
        fn, waits, inc, amt, drain = self.prog[eng][-1]
        assert inc is None, "last entry already incs"
        sem.n += 1
        self.prog[eng][-1] = (fn, waits, sem, amt if inc else 1, drain)
        return sem.n

    def replay(self, key, engine):
        waited = {}
        for fn, waits, inc, inc_amt, drain in self.prog[key]:
            for sem, val in waits:
                if val > 0 and val > waited.get(id(sem), -1):
                    engine.wait_ge(sem.h, val)
                    waited[id(sem)] = val
            if drain:
                engine.drain()
            inst = fn(engine)
            if inc is not None:
                inst.then_inc(inc.h, inc_amt)


class MK:
    def __init__(self, stage=99, taps=()):
        self.stage = stage
        self.taps = set(taps)
        self.nc = bass.Bass("TRN2", target_bir_lowering=False)
        self.ctx = ExitStack()
        self._declare()
        self.b = Bld(self.nc, self.ctx)
        self.tap_toks = []
        self.ptr_last = None

    def _declare(self):
        nc, ctx = self.nc, self.ctx
        din = lambda n, s: nc.dram_tensor(n, s, F32, kind="ExternalInput").ap()
        self.xin = din("xin", [D, TC])
        self.posrow_d = din("posrow", [1, TC])
        self.slopes_d = din("slopes_rep", [128, 4])
        self.idxc_d = nc.dram_tensor("idxc", [128, 56], I32, kind="ExternalInput").ap()
        self.abias_d = din("abias", [128, 20])
        self.pbias_d = din("pbias", [128, 8])
        self.wqT = din("wqT", [D, D]); self.wkT = din("wkT", [D, 256])
        self.wvT = din("wvT", [D, 256]); self.woT = din("woT", [D, D])
        self.qkvT = din("qkvT", [D, 3 * D]); self.projT = din("projT", [D, D])
        self.wgT_d = din("wgT", [D, E])
        self.w1T = din("w1T", [E, D, D]); self.w3T = din("w3T", [E, D, D])
        self.w2T = din("w2T", [E, D, D])
        self.b1c_d = din("b1c", [E, 128, 8]); self.b2c_d = din("b2c", [E, 128, 8])
        self.b3c_d = din("b3c", [E, 128, 8])
        self.b2s_d = din("b2s", [E, D])
        self.xout = nc.dram_tensor("xout", [D, TC], F32, kind="ExternalOutput").ap()
        dbf = lambda n, s: nc.dram_tensor(n, s, BF16).ap()
        self.ag_a_in = dbf("ag_a_in", [512, TC])
        self.ag_a_out = dbf("ag_a_out", [2048, TC])
        self.ag_b_in = dbf("ag_b_in", [3072, TC])
        self.ag_b_out = dbf("ag_b_out", [12288, TC])
        self.ag_c_in = dbf("ag_c_in", [1024, TC])
        self.ag_c_out = dbf("ag_c_out", [4096, TC])
        self.qa_dram = dbf("qa_dram", [D, TC])
        self.ctxa_dram = dbf("ctxa_dram", [D, TC])
        self.xtok_d = dbf("xtok", [TC, D])
        self.ycomp_d = dbf("ycomp", [E * CAP, D])
        self.invmap_d = nc.dram_tensor("invmap", [E * CAP, 1], I32).ap()
        self.wcomp_d = nc.dram_tensor("wcomp", [E * CAP, 1], F32).ap()
        sb = lambda n, s, d: ctx.enter_context(nc.sbuf_tensor(n, s, d))
        self.R = sb("R", [128, 8 * TC], F32)
        self.MS = sb("MS", [128, 4096], F32)
        self.AR = sb("AR", [128, AR_COLS], BF16)
        self.WET = sb("WET", [128, 2048], BF16)
        self.AB = sb("AB", [128, 20], F32)
        self.PBc = sb("PBc", [128, 8], F32)
        self.B1 = sb("B1", [128, 128], F32)
        self.B2 = sb("B2", [128, 128], F32)
        self.B3 = sb("B3", [128, 128], F32)
        self.SLP = sb("SLP", [128, 12], F32)
        self.WGT = sb("WGT", [128, 128], BF16)
        self.IDX = sb("IDX", [128, 56], I32)
        self.IDENT = sb("IDENT", [128, 128], BF16)
        self.PERMA = sb("PERMA", [128, 128], BF16)
        self.PERMB = sb("PERMB", [128, 128], BF16)
        self.ONESB = sb("ONESB", [128, 128], BF16)
        self.ONESF = sb("ONESF", [128, 128], F32)
        self.ONESC = sb("ONESC", [128, 1], BF16)
        self.CONE = sb("CONE", [128, 8], F32)
        self.PF = sb("PF", [128, 128], F32)
        self.JG = sb("JG", [128, 64], F32)
        self.IOC = sb("IOC", [128, 1], F32)
        self.CONSTF = sb("CONSTF", [128, 2], F32)
        nc.const_aps.register(F32, -BEXP, self.CONSTF[:, 0:1])
        self.QRAW = sb("QRAW", [128, 2048], BF16)
        self.QSTG = sb("QSTG", [128, 2048], BF16)
        self.LTRI = sb("LTRI", [128, 128], BF16)
        self.SLOTI = sb("SLOTI", [128, 64], I32)
        self.IDXS = sb("IDXS", [128, 2 * NCH], I32)
        self.TOKID = sb("TOKID", [128, 16], I32)
        self.ZI32 = sb("ZI32", [128, E * CAP // 128], I32)
        self.WSTR = sb("WSTR", [128, 2 * NCH], F32)
        self.ZF32 = sb("ZF32", [128, E * CAP // 128], F32)
        self.B2S = sb("B2S", [16, D], BF16)
        ps = lambda n, s, d: ctx.enter_context(nc.psum_tensor(n, s, d))
        self.pA = ps("pA", [128, 2048], F32)
        self.pB = ps("pB", [128, 1024], F32)
        self.pb6 = ps("pb6", [128, 512], F32)
        self.ptr = ps("ptr", [128, 1024], BF16)
        self.dbg = {}
        shapes = {"xn1": [D, TC], "qa": [D, TC], "aga": [512, TC], "r2": [D, TC],
                  "wfull": [128, 256], "r3": [D, TC], "agb": [3072, TC],
                  "r4": [D, TC], "ctxa": [D, TC], "ctxb": [1024, TC]}
        for t in self.taps:
            self.dbg[t] = self.nc.dram_tensor(
                "dbg_" + t, shapes[t], F32, kind="ExternalOutput").ap()

    def ar(self, off, cols, rows=128, r0=0):
        return self.AR[r0:r0 + rows, off:off + cols]

    def xnp(self, pt):
        return self.ar(OFF_XN + pt * 2048, 2048)

    def tap_sbuf(self, name, src_ap, waits=()):
        if name in self.dbg:
            self.tap_toks.append(self.b.dgp(
                "tap", self.dbg[name].rearrange("(pt p) t -> p pt t", p=128),
                v3(src_ap, 2048), waits=waits))

    def tap_dram(self, name, src_ap, waits=()):
        if name in self.dbg:
            self.tap_toks.append(self.b.dgp("tap", self.dbg[name], src_ap,
                                            waits=waits))

    def dbg_tap_R(self, name):
        self.tap_toks.append(self.b.dsp(
            "tap2", self.dbg[name].rearrange("(pt p) t -> p pt t", p=128),
            v3(self.R[:, :], 2048), waits=[self.b.tdve()]))

    def preamble(self):
        b = self.b
        self.tok_R = b.dsp("in", v3(self.R[:, :], 2048),
                           self.xin.rearrange("(pt p) t -> p pt t", p=128))
        b.dsp("in", self.AB[:, :], self.abias_d)
        b.dsp("in", self.PBc[:, :], self.pbias_d)
        b.dsp("in", self.SLP[:, 0:4], self.slopes_d)
        b.dsp("in", self.IDX[:, :], self.idxc_d)
        b.dsp("in", self.MS[0:1, 0:TC], self.posrow_d)
        b.dsp("in", v3(self.B1[:, :], 8), self.b1c_d.rearrange("e p c -> p e c"))
        b.dsp("in", v3(self.B2[:, :], 8), self.b2c_d.rearrange("e p c -> p e c"))
        self.tok_in = b.dsp("in", v3(self.B3[:, :], 8),
                            self.b3c_d.rearrange("e p c -> p e c"))
        self.tok_R = self.tok_in
        b.dgp("wg", self.B2S[:, :], self.b2s_d)
        self.tok_wg = b.dgp("wg", v3(self.WGT[:, :], 16),
                            self.wgT_d.rearrange("(dc p) e -> p dc e", p=128))
        P = b.pool
        P(lambda e: e.memset(self.IDENT[:, :], 1.0))
        P(lambda e: e.affine_select(self.IDENT[:, :], self.IDENT[:, :], [[1, 128]],
                                    ALU.is_equal, 0.0, base=0, channel_multiplier=-1))
        P(lambda e: e.memset(self.ONESB[:, :], 1.0))
        P(lambda e: e.memset(self.ONESF[:, :], 1.0))
        P(lambda e: e.memset(self.ONESC[:, :], 1.0))
        P(lambda e: e.memset(self.CONE[:, :], 1.0))
        P(lambda e: e.memset(self.CONSTF[:, 0:2], -BEXP))
        P(lambda e: e.iota(self.PF[:, :], [[-1, 128]], base=0, channel_multiplier=1,
                           allow_small_or_imprecise_dtypes=True))
        P(lambda e: e.iota(self.JG[:, :], [[128, 64]], base=0, channel_multiplier=1,
                           allow_small_or_imprecise_dtypes=True))
        P(lambda e: e.iota(self.IOC[:, :], [[1, 1]], base=0, channel_multiplier=1,
                           allow_small_or_imprecise_dtypes=True))
        P(lambda e: e.memset(self.LTRI[:, :], 1.0))
        P(lambda e: e.affine_select(self.LTRI[:, :], self.LTRI[:, :], [[1, 128]],
                                    ALU.is_gt, 0.0, base=0, channel_multiplier=-1))
        P(lambda e: e.memset(self.ZI32[:, :], 0.0))
        P(lambda e: e.memset(self.ZF32[:, :], 0.0))
        P(lambda e: e.iota(self.MS[:, 3000:3016], [[128, 16]], base=0,
                           channel_multiplier=1,
                           allow_small_or_imprecise_dtypes=True))
        P(lambda e: e.memset(self.PERMA[:, :], 0.0))
        P(lambda e: e.memset(self.PERMB[:, :], 0.0))
        for bp in (0, 64):
            for (mat, half, st, bs) in (
                    (self.PERMA, 0, -2, -1), (self.PERMA, 1, -2, 0),
                    (self.PERMB, 0, -1, -32), (self.PERMB, 1, -1, 0)):
                sl = mat[bp:bp + 64, bp + half * 32: bp + half * 32 + 32]
                P(lambda e, sl=sl: e.memset(sl, 1.0))
                P(lambda e, sl=sl, st=st, bs=bs: e.affine_select(
                    sl, sl, [[st, 32]], ALU.is_equal, 0.0, base=bs,
                    channel_multiplier=1))
        self.tok_consts = b.pool_mark()
        d = b.dve
        d(lambda e: e.tensor_scalar(self.SLP[:, 4:8], self.SLP[:, 0:4], 8.0, None,
                                    op0=ALU.mult), waits=[self.tok_in])
        d(lambda e: e.tensor_scalar(self.SLP[:, 8:12], self.SLP[:, 0:4], -8.0, None,
                                    op0=ALU.mult), inc=True)
        self.tok_slp = b.tdve()
        d(lambda e: e.tensor_scalar(self.TOKID[:, :], self.MS[:, 3000:3016], 1.0,
                                    None, op0=ALU.mult), waits=[self.tok_consts],
          inc=True)
        self.tok_tokid = b.tdve()

    def build_tables(self, reload_pos=False):
        b = self.b
        MS, pA = self.MS, self.pA
        wpos = [self.tok_in]
        if reload_pos:
            wpos = [b.dsp("pos", MS[0:1, 0:TC], self.posrow_d,
                          waits=[b.tpe()])]
        b.pool(lambda e: e.iota(MS[0:1, 2048:2112], [[0, 2], [1, 32]], base=0,
                                channel_multiplier=0,
                                allow_small_or_imprecise_dtypes=True),
               waits=wpos, inc=True)
        tp = b.tpool()
        b.act(lambda e: e.activation(MS[0:1, 2112:2176], MS[0:1, 2048:2112],
                                     AF.Exp, bias=0.0, scale=-LN10K / 32.0),
              waits=[tp] + wpos, inc=True)
        ta = b.tact()
        wA = [ta, b.tdve(), b.tpe()]
        for ch in range(4):
            for bp in (0, 64):
                b.pe(lambda e, ch=ch, bp=bp: e.matmul(
                    pA[bp:bp + 64, ch * 512:(ch + 1) * 512],
                    MS[0:1, 2112:2176],
                    MS[0:1, ch * 512:(ch + 1) * 512],
                    start=True, stop=True, tile_position=(0, bp)),
                    waits=wA if (ch == 0 and bp == 0) else ())
        tpe = b.pe_mark()
        d = b.dve
        d(lambda e: e.tensor_scalar(MS[:, 0:2048], pA[:, :], 1.0 / TWO_PI, None,
                                    op0=ALU.mult), waits=[tpe], inc=True)
        self.tok_tabu = b.tdve()
        d(lambda e: e.tensor_scalar(MS[:, 2048:4096], MS[:, 0:2048],
                                    0.25, MAGIC, op0=ALU.add, op1=ALU.add))
        d(lambda e: e.tensor_scalar(MS[:, 2048:4096], MS[:, 2048:4096], -MAGIC,
                                    None, op0=ALU.add))
        d(lambda e: e.scalar_tensor_tensor(MS[:, 2048:4096], MS[:, 0:2048], 0.25,
                                           MS[:, 2048:4096], op0=ALU.add,
                                           op1=ALU.subtract), inc=True)
        tcos = b.tdve()
        cosk = self.ar(OFF_TAB, 2048)
        b.act(lambda e: e.activation(cosk, MS[:, 2048:4096], AF.Sin, bias=0.0,
                                     scale=TWO_PI), waits=[tcos], inc=True)
        tck = b.tact()
        d(lambda e: e.tensor_scalar(MS[:, 2048:4096], MS[:, 0:2048], MAGIC, None,
                                    op0=ALU.add), waits=[tck])
        d(lambda e: e.tensor_scalar(MS[:, 2048:4096], MS[:, 2048:4096], -MAGIC,
                                    None, op0=ALU.add))
        d(lambda e: e.tensor_sub(MS[:, 0:2048], MS[:, 0:2048], MS[:, 2048:4096]),
          inc=True)
        tsin = b.tdve()
        sink = self.ar(OFF_TAB + 2048, 2048)
        b.act(lambda e: e.activation(sink, MS[:, 0:2048], AF.Sin, bias=0.0,
                                     scale=TWO_PI), waits=[tsin], inc=True)
        tsk = b.tact()
        d(lambda e: e.tensor_scalar(self.AR[0:32, 2048:4096],
                                    self.AR[0:32, 2048:4096], -1.0, None,
                                    op0=ALU.mult), waits=[tsk])
        d(lambda e: e.tensor_scalar(self.AR[64:96, 2048:4096],
                                    self.AR[64:96, 2048:4096], -1.0, None,
                                    op0=ALU.mult))
        d(lambda e: e.tensor_scalar(self.ar(OFF_TAB + 4096, 2048), cosk, 0.125,
                                    None, op0=ALU.mult))
        d(lambda e: e.tensor_scalar(self.ar(OFF_TAB + 6144, 2048), sink, 0.125,
                                    None, op0=ALU.mult), inc=True)
        self.tok_tab = b.tdve()

    def layernorm(self, sq_off, first=False, war_pe_tok=None):
        b = self.b
        MS, pA = self.MS, self.pA
        d, p = b.dve, b.pe
        pe_done = [None] * 8
        for pt in range(8):
            w = []
            if first:
                w.append(self.tok_R)
            if war_pe_tok is not None:
                w.append(war_pe_tok)
            d(lambda e, pt=pt: e.tensor_copy(self.xnp(pt),
                                             self.R[:, pt * 2048:(pt + 1) * 2048]),
              waits=w, inc=True)
            wsq = [pe_done[pt - 2]] if pt >= 2 else []
            sqb = self.ar(sq_off + (pt % 2) * 2048, 2048)
            d(lambda e, pt=pt, sqb=sqb: e.tensor_mul(sqb, self.xnp(pt), self.xnp(pt)),
              waits=wsq, inc=True)
            tdve = b.tdve()
            for ch in range(4):
                p(lambda e, pt=pt, ch=ch: e.matmul(
                    pA[0:1, ch * 512:(ch + 1) * 512], self.ONESC[:, 0:1],
                    self.xnp(pt)[:, ch * 512:(ch + 1) * 512],
                    start=(pt == 0), stop=(pt == 7)),
                    waits=[tdve, self.tok_consts] if ch == 0 else ())
                p(lambda e, ch=ch, sqb=sqb, pt=pt: e.matmul(
                    pA[64:65, ch * 512:(ch + 1) * 512], self.ONESC[:, 0:1],
                    sqb[:, ch * 512:(ch + 1) * 512],
                    start=(pt == 0), stop=(pt == 7), tile_position=(0, 64)))
            pe_done[pt] = b.pe_mark()
        tstat = pe_done[7]
        self.last_ln_pe = tstat
        d(lambda e: e.tensor_scalar(MS[0:1, 0:2048], pA[0:1, :], 1.0 / D, None,
                                    op0=ALU.mult), waits=[tstat])
        d(lambda e: e.tensor_mul(MS[0:1, 2048:4096], MS[0:1, 0:2048],
                                 MS[0:1, 0:2048]))
        d(lambda e: e.scalar_tensor_tensor(MS[0:1, 2048:4096], pA[64:65, :],
                                           1.0 / D, MS[0:1, 2048:4096],
                                           op0=ALU.mult, op1=ALU.subtract))
        d(lambda e: e.tensor_scalar(MS[0:1, 2048:4096], MS[0:1, 2048:4096], 1e-5,
                                    None, op0=ALU.add), inc=True)
        tvar = b.tdve()
        b.act(lambda e: e.activation(MS[0:1, 2048:4096], MS[0:1, 2048:4096], AF.Ln,
                                     bias=0.0, scale=1.0), waits=[tvar])
        b.act(lambda e: e.activation(MS[0:1, 2048:4096], MS[0:1, 2048:4096], AF.Exp,
                                     bias=0.0, scale=-0.5), inc=True)
        trstd = b.tact()
        mb = self.ar(OFF_SC, 2048)
        rb = self.ar(OFF_SC + 2048, 2048)
        prev = tvar
        for (col0, dst) in ((0, mb), (2048, rb)):
            for ch in range(4):
                wl = [trstd, prev]
                p(lambda e, col0=col0, ch=ch: e.matmul(
                    pA[:, ch * 512:(ch + 1) * 512], self.ONESF[0:1, :],
                    MS[0:1, col0 + ch * 512:col0 + (ch + 1) * 512],
                    start=True, stop=True), waits=wl if ch == 0 else ())
            tb = b.pe_mark()
            d(lambda e, dst=dst: e.tensor_copy(dst, pA[:, :]), waits=[tb], inc=True)
            prev = b.tdve()
        for pt in range(8):
            d(lambda e, pt=pt: e.tensor_sub(self.xnp(pt), self.xnp(pt), mb))
            d(lambda e, pt=pt: e.tensor_mul(self.xnp(pt), self.xnp(pt), rb),
              inc=(pt == 7))
        return b.tdve()

    def proj(self, wslot_off, wstride, oc_list, rope, sink, extra_wait=(),
             stg_war=()):
        b = self.b
        pA = self.pA
        d, p = b.dve, b.pe
        half_free = [None, None]
        stg_done = [None, None]
        stg_free = None
        stg_war = list(stg_war)
        last_evac = None
        for oc, bias in oc_list:
            rope_tok = None
            for half in range(2):
                w = list(extra_wait)
                if half_free[half] is not None:
                    w.append(half_free[half])
                for ch2 in range(2):
                    for dc in range(8):
                        lhsT = self.ar(wslot_off + dc * wstride + oc * 128, 128)
                        p(lambda e, half=half, ch2=ch2, dc=dc, lhsT=lhsT: e.matmul(
                            pA[:, half * 1024 + ch2 * 512:
                               half * 1024 + (ch2 + 1) * 512], lhsT,
                            self.xnp(dc)[:, half * 1024 + ch2 * 512:
                                         half * 1024 + (ch2 + 1) * 512],
                            start=(dc == 0), stop=(dc == 7)),
                            waits=w if (ch2 == 0 and dc == 0) else ())
                tmm = b.pe_mark()
                hs = slice(half * 1024, (half + 1) * 1024)
                pa_h = pA[:, hs]
                if rope is None:
                    wv = [tmm]
                    if half == 0:
                        if stg_free is not None:
                            wv.append(stg_free)
                        elif stg_war:
                            wv += stg_war
                    if bias is not None:
                        d(lambda e, bias=bias, hs=hs, pa_h=pa_h: e.tensor_scalar(
                            self.QSTG[:, hs], pa_h, bias, None, op0=ALU.add),
                          waits=wv, inc=True)
                    else:
                        d(lambda e, hs=hs, pa_h=pa_h: e.tensor_copy(
                            self.QSTG[:, hs], pa_h), waits=wv, inc=True)
                    half_free[half] = b.tdve()
                else:
                    perm, cos_off, sin_off = rope
                    if bias is not None:
                        d(lambda e, bias=bias, hs=hs, pa_h=pa_h: e.tensor_scalar(
                            self.QRAW[:, hs], pa_h, bias, None, op0=ALU.add),
                          waits=[tmm], inc=True)
                    else:
                        d(lambda e, hs=hs, pa_h=pa_h: e.tensor_copy(
                            self.QRAW[:, hs], pa_h), waits=[tmm], inc=True)
                    half_free[half] = b.tdve()
                    w2 = [half_free[half], self.tok_consts]
                    if last_evac is not None:
                        w2.append(last_evac)
                    for q2 in range(2):
                        p(lambda e, half=half, q2=q2, perm=perm: e.matmul(
                            self.pB[:, q2 * 512:(q2 + 1) * 512], perm,
                            self.QRAW[:, half * 1024 + q2 * 512:
                                      half * 1024 + (q2 + 1) * 512],
                            start=True, stop=True), waits=w2 if q2 == 0 else ())
                    tperm = b.pe_mark()
                    wv2 = [tperm]
                    if stg_done[half] is not None:
                        wv2.append(stg_done[half])
                    elif stg_war:
                        wv2 += stg_war
                    b.act(lambda e, hs=hs: e.activation(
                        self.QSTG[:, hs], self.pB[:, :], AF.Copy, bias=0.0,
                        scale=1.0), waits=wv2, inc=True)
                    evac_tok = b.tact()
                    last_evac = evac_tok
                    cos_ap = self.ar(cos_off + half * 1024, 1024)
                    sin_ap = self.ar(sin_off + half * 1024, 1024)
                    d(lambda e, hs=hs, cos_ap=cos_ap: e.tensor_mul(
                        self.QRAW[:, hs], self.QRAW[:, hs], cos_ap),
                      waits=[tperm])
                    d(lambda e, hs=hs, sin_ap=sin_ap: e.tensor_mul(
                        self.QSTG[:, hs], self.QSTG[:, hs], sin_ap),
                      waits=[evac_tok])
                    d(lambda e, hs=hs: e.tensor_add(
                        self.QSTG[:, hs], self.QSTG[:, hs], self.QRAW[:, hs]),
                      inc=True)
                    rope_tok = b.tdve()
                    half_free[half] = rope_tok
            if rope is None:
                stg_free = sink(oc, self.QSTG, half_free[1])
            else:
                tok = sink(oc, self.QSTG, rope_tok)
                stg_done = [tok, tok]
        return b.tpe()

    def vaug_build(self, vfm_off, vaug_off, tok_v):
        b = self.b
        p, d = b.pe, b.dve
        evac_tok = b.tdve()
        for bt in range(4):
            wt = [tok_v, evac_tok, self.tok_consts]
            for i in range(16):
                cc = bt * 16 + i
                p(lambda e, cc=cc, i=i, vfm_off=vfm_off: e.transpose(
                    self.ptr[:, i * 64:(i + 1) * 64],
                    self.ar(vfm_off + cc * 128, 128, rows=64),
                    self.IDENT[0:64, 0:64]),
                  waits=wt if i == 0 else ())
            tt = b.pe_mark()
            dst = self.AR[:, vaug_off + bt * 16 * VA_STRIDE:
                          vaug_off + (bt + 1) * 16 * VA_STRIDE].rearrange(
                "p (c v) -> p c v", v=VA_STRIDE)[:, :, 0:64]
            d(lambda e, dst=dst: e.tensor_copy(dst, self.ptr[:, :].rearrange(
                "p (c v) -> p c v", v=64)), waits=[tt], inc=True)
            evac_tok = b.tdve()
        ones = self.AR[:, vaug_off:vaug_off + 64 * VA_STRIDE].rearrange(
            "p (c v) -> p c v", v=VA_STRIDE)[:, :, 64:65]
        d(lambda e, ones=ones: e.memset(ones, 1.0), inc=True)
        return b.tdve(), b.tpe()

    def flash_head(self, kdup_off, qdup_off, vaug_off, parity, Q0,
                   wait_loads, exp_emit, ctx_sink, war_ctx_tok, war_store,
                   chunks=None):
        b = self.b
        p, d = b.pe, b.dve
        pA, pB = self.pA, self.pB
        if chunks is None:
            chunks = list(range(64))
        n = len(chunks)
        t_exp = [None] * n
        t_av = [None] * n

        def emit_S(i):
            tc = chunks[i]
            half = i % 2
            w = []
            if i < 2:
                w += list(wait_loads)
                if i == 0 and war_ctx_tok is not None:
                    w.append(war_ctx_tok)
            else:
                w.append(t_exp[i - 2])
            for hv in range(2):
                lhsT = self.AR[hv * 64:hv * 64 + 64,
                               kdup_off + tc * 128:kdup_off + (tc + 1) * 128]
                rhs = self.AR[hv * 64:hv * 64 + 64,
                              qdup_off + Q0 + hv * 512:qdup_off + Q0 + (hv + 1) * 512]
                p(lambda e, half=half, hv=hv, lhsT=lhsT, rhs=rhs: e.matmul(
                    pA[:, half * 1024 + hv * 512:half * 1024 + (hv + 1) * 512],
                    lhsT, rhs, start=True, stop=True, tile_position=(hv * 64, 0)),
                    waits=w if hv == 0 else ())
            return b.pe_mark()

        ts_tok = [None] * n
        ts_tok[0] = emit_S(0)
        if n > 1:
            ts_tok[1] = emit_S(1)
        for i in range(n):
            tc = chunks[i]
            half = i % 2
            ubuf = U0 if half == 0 else U1
            wav = t_av[i - 2] if i >= 2 else None
            ta, td = exp_emit(tc, ts_tok[i], wav, ubuf, half)
            t_exp[i] = ta
            if i + 2 < n:
                ts_tok[i + 2] = emit_S(i + 2)
            for hv in range(2):
                lhsT = self.AR[:, vaug_off + tc * VA_STRIDE:
                               vaug_off + tc * VA_STRIDE + 65]
                wv = [ta] + ([td] if td is not None else [])
                p(lambda e, hv=hv, i=i, lhsT=lhsT, ubuf=ubuf: e.matmul(
                    pB[0:65, hv * 512:(hv + 1) * 512], lhsT,
                    self.ar(ubuf, 1024)[:, hv * 512:(hv + 1) * 512],
                    start=(i == 0), stop=(i == n - 1)),
                    waits=wv if hv == 0 else ())
            t_av[i] = b.pe_mark()
        MS = self.MS
        slot = MS[0:65, 1024 + parity * 1024:2048 + parity * 1024]
        wm = [t_av[n - 1]] + ([war_store] if war_store is not None else [])
        d(lambda e, slot=slot: e.tensor_copy(slot[:, 0:1024], pB[0:65, :]),
          waits=wm, inc=True)
        tfree = b.tdve()
        d(lambda e, slot=slot: e.reciprocal(MS[64:65, 0:1024],
                                            slot[64:65, 0:1024]), inc=True)
        prev = b.tdve()
        for half in range(2):
            b.pe(lambda e, half=half: e.matmul(
                self.pb6[0:64, :], self.ONESF[64:65, 0:64],
                MS[64:65, half * 512:(half + 1) * 512], start=True, stop=True),
                waits=[prev])
            tb = b.pe_mark()
            d(lambda e, half=half: e.tensor_copy(
                MS[0:64, half * 512:(half + 1) * 512], self.pb6[0:64, :]),
              waits=[tb], inc=True)
            prev = b.tdve()
        ctxt = slot[0:64, 0:1024]
        d(lambda e, ctxt=ctxt: e.tensor_mul(ctxt, ctxt, MS[0:64, 0:1024]),
          inc=True)
        tctx = b.tdve()
        tok = ctx_sink(ctxt, tctx)
        return tfree, tok, b.tpe()

    def attn_a(self):
        b = self.b
        d, p = b.dve, b.pe
        tok_wq = b.dgp("wq", v3(self.ar(WB0, 8192), 1024),
                       self.wqT.rearrange("(dc p) o -> p dc o", p=128))
        tok_wk = b.dgp("wkv", v3(self.ar(WB1, 2048), 256),
                       self.wkT.rearrange("(dc p) o -> p dc o", p=128),
                       waits=[self.tok_ln1_pe])
        tok_wv = b.dgp("wkv", v3(self.ar(WB1 + 2048, 2048), 256),
                       self.wvT.rearrange("(dc p) o -> p dc o", p=128))

        def sink_q(oc, tile, tok):
            return b.dsp("qa%d" % (oc % 2),
                         self.qa_dram[oc * 128:(oc + 1) * 128, :], tile[:, :],
                         waits=[tok])

        def sink_k(oc, tile, tok):
            return b.dsp("aga", self.ag_a_in[oc * 128:(oc + 1) * 128, :],
                         tile[:, :], waits=[tok])

        def sink_v(oc, tile, tok):
            return b.dsp("aga", self.ag_a_in[256 + oc * 128:256 + (oc + 1) * 128, :],
                         tile[:, :], waits=[tok])

        rope_k = (self.PERMA[:, :], OFF_TAB, OFF_TAB + 2048)
        rope_q = (self.PERMA[:, :], OFF_TAB + 4096, OFF_TAB + 6144)
        self.proj(WB0, 1024, [(oc, self.AB[:, oc:oc + 1]) for oc in range(8)],
                  rope_q, sink_q,
                  extra_wait=[tok_wq, self.tok_xn1, self.tok_tab])
        gq0, gq1 = b.grp("qa0"), b.grp("qa1")
        self.proj(WB1, 256, [(oc, self.AB[:, 8 + oc:9 + oc]) for oc in range(2)],
                  rope_k, sink_k, extra_wait=[tok_wv],
                  stg_war=[(gq0, gq0.n), (gq1, gq1.n)])
        ga = b.grp("aga")
        self.proj(WB1 + 2048, 256,
                  [(oc, self.AB[:, 10 + oc:11 + oc]) for oc in range(2)],
                  None, sink_v, extra_wait=[], stg_war=[(ga, ga.n)])
        g_aga = b.grp("aga")
        tok_contrib = (g_aga, g_aga.n)
        g_qa0, g_qa1 = b.grp("qa0"), b.grp("qa1")
        tok_qa = [(g_qa0, g_qa0.n), (g_qa1, g_qa1.n)]
        self.tap_dram("qa", self.qa_dram, waits=tok_qa)
        self.tap_dram("aga", self.ag_a_in, waits=[tok_contrib])
        tok_aga = b.coll("a", [[0, 1, 2, 3], [4, 5, 6, 7]],
                         [self.ag_a_in], [self.ag_a_out],
                         waits=[tok_contrib])
        tok_wo = b.dgp("wo", v3(self.ar(A_WO, 8192), 1024),
                       self.woT.rearrange("(dc p) o -> p dc o", p=128))

        def exp_plain(tc, wpe, wav, ubuf, half):
            w = [wpe] + ([wav] if wav is not None else [])
            b.act(lambda e, ubuf=ubuf, half=half: e.activation(
                self.ar(ubuf, 1024), self.pA[:, half * 1024:(half + 1) * 1024],
                AF.Exp, bias=-BEXP, scale=1.0), waits=w, inc=True, drain=False)
            return b.tact(), None

        kload_war = None
        ctx_war = None
        ctx_tok = None
        self.ctx_store = [None, None]
        src = self.ag_a_out.rearrange("(r c) t -> c r t", c=512)
        for kv in range(4):
            w_k = [tok_aga]
            if kload_war is not None:
                w_k.append(kload_war)
            gkv = "kva%d" % (kv % 2)
            b.dsp(gkv, v3(self.ar(A_KDUP, 8192, rows=64), 2048),
                  src[kv * 64:(kv + 1) * 64, :, :], waits=w_k)
            b.dsp(gkv, v3(self.AR[64:128, A_KDUP:A_KDUP + 8192], 2048),
                  src[kv * 64:(kv + 1) * 64, :, :])
            tok_kv = b.dsp(gkv, v3(self.ar(A_VFM, 8192, rows=64), 2048),
                           src[256 + kv * 64:256 + (kv + 1) * 64, :, :])
            tok_vaug, _ = self.vaug_build(A_VFM, A_VAUG, tok_kv)
            for hi in range(4):
                h = kv * 4 + hi
                wq1 = [kload_war] if kload_war is not None else []
                gqd = "qda%d" % (h % 2)
                b.dsp(gqd, self.ar(A_QDUP, 2048, rows=64),
                      self.qa_dram[h * 64:(h + 1) * 64, :],
                      waits=wq1 + tok_qa)
                tq = b.dsp(gqd, self.AR[64:128, A_QDUP:A_QDUP + 2048],
                           self.qa_dram[h * 64:(h + 1) * 64, :])
                for qt in range(2):
                    wl = [tok_kv, tq, tok_vaug, self.tok_consts]
                    par = (2 * h + qt) % 2

                    def sink(ctxt, tok, h=h, qt=qt, par=par):
                        return b.dgp(
                            "ctx%d" % par,
                            self.ctxa_dram[h * 64:(h + 1) * 64,
                                           qt * 1024:(qt + 1) * 1024],
                            ctxt, waits=[tok])

                    ctx_war, ctx_tok, kload_war = self.flash_head(
                        A_KDUP, A_QDUP, A_VAUG, par, qt * 1024,
                        wl, exp_plain, sink, ctx_war, self.ctx_store[par])
                    self.ctx_store[par] = ctx_tok
        g_c0, g_c1 = b.grp("ctx0"), b.grp("ctx1")
        tok_ctxall = [(g_c0, g_c0.n), (g_c1, g_c1.n)]
        tok_ctx = b.dsp("ctxld", v3(self.AR[:, OFF_XN:OFF_XN + 16384], 2048),
                        self.ctxa_dram.rearrange("(pt p) t -> p pt t", p=128),
                        waits=tok_ctxall + [kload_war])
        self.tap_dram("ctxa", self.ctxa_dram, waits=tok_ctxall)
        pa_free = None
        for oc in range(8):
            w = [tok_wo, tok_ctx]
            if pa_free is not None:
                w.append(pa_free)
            for ch in range(4):
                for dc in range(8):
                    lhsT = self.ar(A_WO + dc * 1024 + oc * 128, 128)
                    p(lambda e, ch=ch, dc=dc, lhsT=lhsT: e.matmul(
                        self.pA[:, ch * 512:(ch + 1) * 512], lhsT,
                        self.xnp(dc)[:, ch * 512:(ch + 1) * 512],
                        start=(dc == 0), stop=(dc == 7)),
                        waits=w if (ch == 0 and dc == 0) else ())
            tmm = b.pe_mark()
            gt2 = b.grp("tap2")
            d(lambda e, oc=oc: e.scalar_tensor_tensor(
                self.R[:, oc * 2048:(oc + 1) * 2048], self.pA[:, :],
                self.AB[:, 12 + oc:13 + oc],
                self.R[:, oc * 2048:(oc + 1) * 2048],
                op0=ALU.add, op1=ALU.add),
              waits=[tmm] + ([(gt2, gt2.n)] if (oc == 0 and self.dbg) else []),
              inc=True)
            pa_free = b.tdve()
        self.tok_oproj_pe = b.tpe()
        return b.tdve()

    def routing(self, tok_xn, fi, qstg_war=()):
        b = self.b
        MS, pb6, pB = self.MS, self.pb6, self.pB
        d, p = b.dve, b.pe
        sM = lambda a, bb: MS[:, a:bb]
        sv = self.MS[:, 0:16].rearrange("p (a b) -> p a b", b=2)
        t1v = self.MS[:, 32:40].rearrange("p (a b) -> p a b", b=1)
        t_exp_prev = None
        tr_evac = None
        qstg_war = list(qstg_war)
        grps = [b.grp(n) for n in ("ix0", "ix1", "ws0", "ws1", "xg0", "xg1")]
        scg = b.grp("sc")
        wzi = [self.tok_consts] + ([(g, g.n) for g in grps] + [(scg, scg.n)]
                                   if fi == 2 else [])
        b.dsp("zi", self.invmap_d.rearrange("(a bb) o -> a (bb o)", a=128),
              self.ZI32[:, :], waits=wzi)
        tok_zi = b.dsp("zi",
                       self.wcomp_d.rearrange("(a bb) o -> a (bb o)", a=128),
                       self.ZF32[:, :])
        b.pool(lambda e: e.iota(MS[0:1, 3600:3616], [[1, 16]], base=0,
                                channel_multiplier=0,
                                allow_small_or_imprecise_dtypes=True),
               waits=[tok_xn], inc=True)
        tbp = b.tpool()
        d(lambda e: e.tensor_scalar(MS[0:1, 3600:3616], MS[0:1, 3600:3616],
                                    float(CAP), None, op0=ALU.mult), waits=[tbp])
        d(lambda e: e.tensor_scalar(MS[0:1, 3584:3600], self.ONESF[0:1, 0:16],
                                    0.0, None, op0=ALU.mult), inc=True)
        tpb = b.tdve()
        p(lambda e: e.matmul(pB[:, 48:64], self.ONESF[0:1, 0:128],
                             MS[0:1, 3600:3616], start=True, stop=True),
          waits=[tpb])
        tec = b.pe_mark()
        d(lambda e: e.tensor_copy(MS[:, 3616:3632], pB[:, 48:64]), waits=[tec],
          inc=True)
        tpb = b.tdve()
        for tc in range(16):
            w = [self.tok_wg, tok_xn]
            if t_exp_prev is not None:
                w.append(t_exp_prev)
            for dc in range(8):
                p(lambda e, tc=tc, dc=dc: e.matmul(
                    pb6[:, 0:16], self.xnp(dc)[:, tc * 128:(tc + 1) * 128],
                    self.WGT[:, dc * 16:(dc + 1) * 16],
                    start=(dc == 0), stop=(dc == 7)),
                    waits=w if dc == 0 else ())
            tmm = b.pe_mark()
            b.act(lambda e: e.activation(sM(16, 32), pb6[:, 0:16], AF.Exp,
                                         bias=0.0, scale=-1.0),
                  waits=[tmm], inc=True)
            t_exp_prev = b.tact()
            d(lambda e: e.tensor_scalar(sM(16, 32), sM(16, 32), 1.0, None,
                                        op0=ALU.add), waits=[t_exp_prev])
            d(lambda e: e.reciprocal(sM(0, 16), sM(16, 32)))
            d(lambda e: e.tensor_max(t1v[:, :, 0:1], sv[:, :, 0:1], sv[:, :, 1:2]))
            gv = self.MS[:, 32:40].rearrange("p (a b) -> p a b", b=2)
            d(lambda e: e.tensor_max(
                self.MS[:, 40:44].rearrange("p (a b) -> p a b", b=1),
                gv[:, :, 0:1], gv[:, :, 1:2]))
            d(lambda e: e.reduce_max(sM(44, 45), sM(40, 44), axis=AXX))
            d(lambda e: e.tensor_scalar(sM(48, 52), sM(40, 44), MS[:, 44:45], None,
                                        op0=ALU.is_ge))
            d(lambda e: e.scalar_tensor_tensor(sM(52, 56), sM(48, 52), -2.0,
                                               sM(40, 44), op0=ALU.mult, op1=ALU.add))
            d(lambda e: e.reduce_max(sM(44, 45), sM(52, 56), axis=AXX))
            d(lambda e: e.tensor_scalar(sM(112, 116), sM(52, 56), MS[:, 44:45], None,
                                        op0=ALU.is_ge))
            d(lambda e: e.tensor_add(sM(56, 60), sM(48, 52), sM(112, 116)))
            for gi in range(4):
                d(lambda e, gi=gi: e.tensor_scalar(
                    sM(64 + 4 * gi, 68 + 4 * gi), self.CONE[:, 0:4],
                    MS[:, 56 + gi:57 + gi], None, op0=ALU.mult))
            d(lambda e: e.tensor_scalar(sM(80, 96), sM(0, 16), 1.0, None, op0=ALU.add))
            d(lambda e: e.tensor_mul(sM(80, 96), sM(80, 96), sM(64, 80)))
            d(lambda e: e.tensor_scalar(sM(80, 96), sM(80, 96), -1.0, None,
                                        op0=ALU.add))
            for k in range(4):
                d(lambda e: e.reduce_max(sM(44, 45), sM(80, 96), axis=AXX))
                d(lambda e: e.tensor_scalar(sM(112, 128), sM(80, 96), MS[:, 44:45],
                                            None, op0=ALU.is_ge))
                if k == 0:
                    d(lambda e: e.tensor_copy(sM(96, 112), sM(112, 128)))
                else:
                    d(lambda e: e.tensor_add(sM(96, 112), sM(96, 112), sM(112, 128)))
                if k < 3:
                    d(lambda e: e.scalar_tensor_tensor(
                        sM(80, 96), sM(112, 128), -3.0, sM(80, 96),
                        op0=ALU.mult, op1=ALU.add))
            d(lambda e: e.tensor_mul(sM(112, 128), sM(0, 16), sM(96, 112)))
            d(lambda e: e.reduce_sum(sM(44, 45), sM(112, 128), axis=AXX))
            d(lambda e: e.reciprocal(sM(45, 46), sM(44, 45)))
            d(lambda e, tc=tc: e.tensor_scalar(
                sM(256 + 16 * tc, 272 + 16 * tc), sM(112, 128),
                MS[:, 45:46], 2.5, op0=ALU.mult, op1=ALU.mult))
            wq = qstg_war if tc == 0 else []
            d(lambda e, tc=tc: e.tensor_copy(
                self.QSTG[:, tc * 16:(tc + 1) * 16],
                sM(256 + 16 * tc, 272 + 16 * tc)), waits=wq, inc=True)
            tstg = b.tdve()
            selb = self.QSTG[:, 256 + 16 * tc:272 + 16 * tc]
            d(lambda e, selb=selb: e.tensor_copy(selb, sM(96, 112)), inc=True)
            tselb = b.tdve()
            d(lambda e: e.tensor_copy(sM(512, 528), sM(96, 112)))
            for (ksh, sa, da_) in ((1, 512, 528), (2, 528, 512),
                                   (4, 512, 528), (8, 528, 512)):
                d(lambda e, k=ksh, sa=sa, da_=da_: e.tensor_copy(
                    MS[:, da_:da_ + k], MS[:, sa:sa + k]))
                d(lambda e, k=ksh, sa=sa, da_=da_: e.tensor_add(
                    MS[:, da_ + k:da_ + 16], MS[:, sa + k:sa + 16],
                    MS[:, sa:sa + 16 - k]))
            d(lambda e: e.tensor_sub(sM(544, 560), sM(512, 528), sM(96, 112)),
              inc=True)
            texcl = b.tdve()
            p(lambda e, selb=selb: e.matmul(pB[:, 0:16], self.LTRI[:, :], selb,
                                            start=True, stop=True),
              waits=[tselb, tpb])
            p(lambda e: e.matmul(pB[:, 16:32], self.ONESF[0:1, 0:128],
                                 MS[0:1, 3584:3600], start=True, stop=True))
            p(lambda e, selb=selb: e.matmul(pB[0:1, 32:48], self.ONESB[:, 0:1],
                                            selb, start=True, stop=True))
            trk = b.pe_mark()
            d(lambda e: e.tensor_copy(sM(576, 592), pB[:, 0:16]), waits=[trk])
            d(lambda e: e.tensor_add(sM(576, 592), sM(576, 592), pB[:, 16:32]))
            d(lambda e: e.tensor_scalar(sM(576, 592), sM(576, 592),
                                        float(CAP - 1), None, op0=ALU.min))
            d(lambda e: e.tensor_add(MS[0:1, 3584:3600], MS[0:1, 3584:3600],
                                     pB[0:1, 32:48]), inc=True)
            tpb = b.tdve()
            d(lambda e: e.tensor_scalar(sM(600, 616), sM(576, 592),
                                        1.0 / 128.0, -63.5 / 128.0,
                                        op0=ALU.mult, op1=ALU.add))
            d(lambda e: e.tensor_scalar(sM(600, 616), sM(600, 616), MAGIC, None,
                                        op0=ALU.add))
            d(lambda e: e.tensor_scalar(sM(600, 616), sM(600, 616), -MAGIC, None,
                                        op0=ALU.add))
            d(lambda e: e.scalar_tensor_tensor(sM(616, 632), sM(600, 616),
                                               -128.0, sM(576, 592),
                                               op0=ALU.mult, op1=ALU.add))
            d(lambda e: e.scalar_tensor_tensor(sM(576, 592), sM(616, 632),
                                               float(NCH), sM(600, 616),
                                               op0=ALU.mult, op1=ALU.add))
            d(lambda e: e.tensor_add(sM(576, 592), sM(576, 592),
                                     MS[:, 3616:3632]))
            for s in range(4):
                d(lambda e, s=s: e.tensor_scalar(sM(560, 576), sM(544, 560),
                                                 float(s), None,
                                                 op0=ALU.is_equal),
                  waits=[texcl] if s == 0 else ())
                d(lambda e: e.tensor_mul(sM(560, 576), sM(560, 576), sM(96, 112)))
                d(lambda e: e.tensor_mul(sM(648, 664), sM(560, 576), sM(576, 592)))
                d(lambda e, s=s: e.reduce_sum(MS[:, 592 + s:593 + s], sM(648, 664),
                                              axis=AXX))
                d(lambda e, tc=tc: e.tensor_mul(sM(648, 664), sM(560, 576),
                                                sM(256 + 16 * tc, 272 + 16 * tc)))
                d(lambda e, s=s, tc=tc: e.reduce_sum(
                    MS[:, 3200 + 4 * tc + s:3201 + 4 * tc + s], sM(648, 664),
                    axis=AXX))
            d(lambda e, tc=tc: e.tensor_scalar(
                self.SLOTI[:, 4 * tc:4 * tc + 4], MS[:, 592:596], 1.0, None,
                op0=ALU.mult), inc=True)
            tslot = b.tdve()
            for s in range(4):
                w = [tslot] if s == 0 else []
                if tc == 0 and s == 0:
                    ws0, ws1 = b.grp("ws0"), b.grp("ws1")
                    w = [tslot, self.tok_tokid, tok_zi,
                         (ws0, ws0.n), (ws1, ws1.n)]
                b.dsis("sc", self.invmap_d,
                       self.SLOTI[:, 4 * tc + s:4 * tc + s + 1],
                       self.TOKID[:, tc:tc + 1], waits=w)
                b.dsis("sc", self.wcomp_d,
                       self.SLOTI[:, 4 * tc + s:4 * tc + s + 1],
                       MS[:, 3200 + 4 * tc + s:3201 + 4 * tc + s])
            wtr = [tstg, self.tok_consts]
            if tc == 0:
                wtr.append(self.tok_pl_ev)
            if tr_evac is not None:
                wtr.append(tr_evac)
            p(lambda e, tc=tc: e.transpose(
                self.ptr[0:16, (tc % 8) * 128:((tc % 8) + 1) * 128],
                self.QSTG[:, tc * 16:(tc + 1) * 16], self.IDENT[:, :]),
                waits=wtr)
            ttr = b.pe_mark()
            if tc % 8 == 7:
                d(lambda e, tc=tc: e.tensor_copy(
                    self.WET[0:16, (tc // 8) * 1024:(tc // 8 + 1) * 1024],
                    self.ptr[0:16, 0:1024]), waits=[ttr], inc=True)
                tr_evac = b.tdve()
        self.tap_dram("wfull", self.MS[:, 256:512], waits=[b.tdve()])
        self.tok_wet = tr_evac

    def payload(self, fi, tok_xn, ptr_war, qstg_war=()):
        b = self.b
        p = b.pe
        xstg = self.QSTG[:, 1024:2048]
        plg = b.grp("pl")
        pl_tok = None
        ev_prev = None
        for tc in range(16):
            for j in range(8):
                w = ()
                if j == 0:
                    w = [tok_xn] + ptr_war if tc == 0 else [ev_prev]
                p(lambda e, tc=tc, j=j: e.transpose(
                    self.ptr[:, j * 128:(j + 1) * 128],
                    self.xnp(j)[:, tc * 128:(tc + 1) * 128], self.IDENT[:, :]),
                  waits=w)
            tmm = b.pe_mark()
            wev = [tmm]
            if tc == 0:
                wev += list(qstg_war)
            if pl_tok is not None:
                wev.append(pl_tok)
            b.act(lambda e, xstg=xstg: e.activation(
                xstg, self.ptr[:, :], AF.Identity, bias=0.0, scale=1.0),
                waits=wev, inc=True, drain=False)
            ev_prev = b.tact()
            pl_tok = b.dsp("pl", self.xtok_d[tc * 128:(tc + 1) * 128, :], xstg,
                           waits=[ev_prev])
        self.tok_pl = pl_tok
        self.tok_pl_ev = ev_prev

    def ffn(self, tok_xn, fi, qstg_war=()):
        b = self.b
        MS, pB, pA = self.MS, self.pB, self.pA
        d, p = b.dve, b.pe
        ptr_war = [self.ptr_last] if self.ptr_last is not None else []
        self.payload(fi, tok_xn, ptr_war, qstg_war)
        self.routing(tok_xn, fi, qstg_war)
        wsrc = lambda w, e_: w[e_].rearrange("(dc p) i -> p dc i", p=128)
        g13 = lambda e_: "w13_%d" % (e_ % 2)
        g2 = lambda e_: "w2_%d" % (e_ % 2)
        war = [b.tpe()]
        b.dgp(g13(0), v3(self.ar(WB0, 8192), 1024), wsrc(self.w1T, 0), waits=war)
        tok_w13 = b.dgp(g13(0), v3(self.ar(WB1, 8192), 1024), wsrc(self.w3T, 0))
        tok_w2 = b.dgp(g2(0), v3(self.ar(WB2, 8192), 1024), wsrc(self.w2T, 0))
        gt = b.grp("tap")
        tap_war = [(gt, gt.n)] if self.dbg else []
        gt2 = b.grp("tap2")
        rtap_war = [(gt2, gt2.n)] if self.dbg else []
        first_radd = [True]
        scg = b.grp("sc")
        plg = b.grp("pl")
        ysg = [b.grp("ys0"), b.grp("ys1")]
        h_free = [None, None]
        silu_free = [None, None]
        h_last = [None, None]
        hwar = [None, None]
        pb_free = [None, None]
        ystg_war = [None, None]
        ring_pl = [False, True]
        ystg_ring = [0]
        tr_done = [None, None]
        ptr_free = [self.tok_wet]
        winready = {}
        awin_pe = {}
        idx_tok = {}
        wst_tok = {}
        gather_last = {}
        prev_b_dve = [None]

        def emit_idx_we(e_):
            par = e_ % 2
            xg0, xg1 = b.grp("xg0"), b.grp("xg1")
            w = []
            if e_ == 0:
                w.append((scg, scg.n))
            else:
                w += [(xg0, xg0.n), (xg1, xg1.n)]
            if prev_b_dve[0] is not None:
                w.append(prev_b_dve[0])
            idx_tok[e_] = b.dsp(
                "ix%d" % par, self.IDXS[:, par * NCH:(par + 1) * NCH],
                self.invmap_d[e_ * CAP:(e_ + 1) * CAP, :].rearrange(
                    "(a bb) o -> a (bb o)", bb=NCH), waits=w)
            wst_tok[e_] = b.dsp(
                "ws%d" % par, self.WSTR[:, par * NCH:(par + 1) * NCH],
                self.wcomp_d[e_ * CAP:(e_ + 1) * CAP, :].rearrange(
                    "(a bb) o -> a (bb o)", bb=NCH))

        gtok = {}

        def emit_gather(e_, c8):
            par = e_ % 2
            rc = c8 % 2
            w = []
            if c8 == 0:
                w.append(idx_tok[e_])
                if e_ == 0:
                    w.append(self.tok_pl)
            if tr_done[rc] is not None:
                w.append(tr_done[rc])
            gtok[(e_, c8)] = b.dsi(
                "xg%d" % rc, self.ar(OFF_SC + 1024 + rc * 1024, 1024),
                self.xtok_d,
                self.IDXS[:, par * NCH + c8:par * NCH + c8 + 1], waits=w)

        def emit_tr_chunk(e_, c8, gtok):
            wi = CHWIN[c8]
            gw = e_ * NWIN + wi
            wslot = gw % 3
            cc = c8 - WINS[wi][0] // 128
            rc = c8 % 2
            src = self.ar(OFF_SC + 1024 + rc * 1024, 1024)
            for j in range(8):
                w = [gtok, ptr_free[0]] if j == 0 else ()
                p(lambda e, j=j, src=src: e.transpose(
                    self.ptr[:, j * 128:(j + 1) * 128],
                    src[:, j * 128:(j + 1) * 128], self.IDENT[:, :]), waits=w)
            tmm = b.pe_mark()
            tr_done[rc] = tmm
            dst = self.ar(OFF_XN + wslot * 4096, 4096).rearrange(
                "p (a bb) -> p a bb", bb=512)[:, :, cc * 128:(cc + 1) * 128]
            wev = [tmm]
            if cc == 0 and (gw - 3) in awin_pe:
                wev.append(awin_pe[gw - 3])
            b.act(lambda e, dst=dst: e.activation(
                dst, v3(self.ptr[:, :], 128), AF.Identity, bias=0.0, scale=1.0),
                waits=wev, inc=True, drain=False)
            ptr_free[0] = b.tact()
            if c8 in (3, 7, NCH - 1):
                winready[gw] = b.tact()

        def a_window(e_, wi):
            ew = e_ * 8
            if True:
                ws_, wd = WINS[wi]
                gw = e_ * NWIN + wi
                hh = gw % 2
                for ic in range(8):
                    par = ic % 2
                    base = par * 1024
                    w = [tok_w13, winready[gw]] + tap_war
                    if h_free[par] is not None:
                        w.append(h_free[par])
                    if ic == 0 and hwar[hh] is not None:
                        w.append(hwar[hh])
                    for m3, woff in ((0, WB0), (1, WB1)):
                        for dc in range(8):
                            lhsT = self.ar(woff + dc * 1024 + ic * 128, 128)
                            rhs = self.ar(OFF_XN + (gw % 3) * 4096 + dc * 512, wd)
                            p(lambda e, base=base, m3=m3, lhsT=lhsT, rhs=rhs,
                              dc=dc, wd=wd: e.matmul(
                                  self.pA[:, base + m3 * 512:base + m3 * 512 + wd],
                                  lhsT, rhs, start=(dc == 0), stop=(dc == 7)),
                              waits=w if (m3 == 0 and dc == 0) else ())
                    tA = b.pe_mark()
                    hdst = self.ar(OFF_TAB + ic * 1024 + hh * 512, wd)
                    silu_bf = self.ar(OFF_SC + par * 512, wd)
                    ws2 = [tA]
                    if silu_free[par] is not None:
                        ws2.append(silu_free[par])
                    b.act(lambda e, ic=ic, ew=ew, base=base, silu_bf=silu_bf,
                          wd=wd: e.activation(
                              silu_bf, self.pA[:, base:base + wd], AF.Silu,
                              bias=self.B1[:, ew + ic:ew + ic + 1], scale=1.0),
                          waits=ws2, drain=False)
                    b.act(lambda e, ic=ic, ew=ew, base=base, hdst=hdst,
                          wd=wd: e.activation(
                              hdst, self.pA[:, base + 512:base + 512 + wd],
                              AF.Identity,
                              bias=self.B3[:, ew + ic:ew + ic + 1], scale=1.0),
                          inc=True, drain=False)
                    tsl = b.tact()
                    d(lambda e, hdst=hdst, silu_bf=silu_bf: e.tensor_mul(
                        hdst, hdst, silu_bf), waits=[tsl], inc=True)
                    h_free[par] = tsl
                    silu_free[par] = b.tdve()
                    h_last[hh] = b.tdve()
                awin_pe[gw] = b.tpe()

        def b_phase(e_, wi, pend_tr):
            gw = e_ * NWIN + wi
            hh = gw % 2
            ws_, wd = WINS[wi]
            for cc in range(wd // 128):
                c8 = ws_ // 128 + cc
                ring = ystg_ring[0] % 2
                for dh in (0, 1):
                    w = [h_last[hh], tok_w2]
                    if pb_free[dh] is not None:
                        w.append(pb_free[dh])
                    for k in range(8):
                        lhsT = self.ar(OFF_TAB + k * 1024 + hh * 512 + cc * 128,
                                       128)
                        rhs = self.ar(WB2 + k * 1024 + dh * 512, 512)
                        p(lambda e, dh=dh, lhsT=lhsT, rhs=rhs, k=k: e.matmul(
                            self.pB[:, dh * 512:(dh + 1) * 512], lhsT, rhs,
                            start=(k == 0), stop=(k == 7)),
                          waits=w if k == 0 else ())
                    tB = b.pe_mark()
                    wev = [tB]
                    if ystg_war[ring] is not None:
                        wev.append(ystg_war[ring])
                    if ring_pl[ring]:
                        wev.append((plg, plg.n))
                        ring_pl[ring] = False
                    if wi == 0 and cc == 0 and dh == 0:
                        wev.append(wst_tok[e_])
                    wecol = self.WSTR[:, (e_ % 2) * NCH + c8:
                                      (e_ % 2) * NCH + c8 + 1]
                    d(lambda e, ring=ring, dh=dh, wecol=wecol: e.tensor_scalar(
                        self.QSTG[:, ring * 1024 + dh * 512:
                                  ring * 1024 + dh * 512 + 512],
                        self.pB[:, dh * 512:(dh + 1) * 512], wecol, None,
                        op0=ALU.mult), waits=wev, inc=True)
                    pb_free[dh] = b.tdve()
                ydst = self.ycomp_d[e_ * CAP:(e_ + 1) * CAP, :].rearrange(
                    "(a bb) o -> a bb o", bb=NCH)[:, c8:c8 + 1, :]
                ystg_war[ring] = b.dsp(
                    "ys%d" % ring, ydst,
                    v3(self.QSTG[:, ring * 1024:ring * 1024 + 1024], 1024),
                    waits=[pb_free[1] if pb_free[1] is not None else pb_free[0]])
                ystg_ring[0] += 1
                if pend_tr:
                    en, c8n = pend_tr.pop(0)
                    emit_tr_chunk(en, c8n, gtok[(en, c8n)])
                    if c8n + 2 < NCH:
                        emit_gather(en, c8n + 2)
            hwar[hh] = b.tpe()

        emit_idx_we(0)
        emit_gather(0, 0)
        emit_gather(0, 1)
        for c8 in range(NCH):
            emit_tr_chunk(0, c8, gtok[(0, c8)])
            if c8 + 2 < NCH:
                emit_gather(0, c8 + 2)
        for e_ in range(E):
            a_window(e_, 0)
            pend_tr = []
            if e_ + 1 < E:
                emit_idx_we(e_ + 1)
                emit_gather(e_ + 1, 0)
                emit_gather(e_ + 1, 1)
                pend_tr = [(e_ + 1, c8) for c8 in range(NCH)]
            for wi in range(NWIN):
                if wi + 1 < NWIN:
                    a_window(e_, wi + 1)
                if wi == 1 and e_ + 1 < E:
                    wlast = [awin_pe[e_ * NWIN + 2]]
                    b.dgp(g13(e_ + 1), v3(self.ar(WB0, 8192), 1024),
                          wsrc(self.w1T, e_ + 1), waits=wlast)
                    tok_w13_next = b.dgp(g13(e_ + 1),
                                         v3(self.ar(WB1, 8192), 1024),
                                         wsrc(self.w3T, e_ + 1))
                b_phase(e_, wi, pend_tr)
            assert not pend_tr
            prev_b_dve[0] = b.tdve()
            if e_ + 1 < E:
                tok_w2 = b.dgp(g2(e_ + 1), v3(self.ar(WB2, 8192), 1024),
                               wsrc(self.w2T, e_ + 1), waits=[b.tpe()])
                tok_w13 = tok_w13_next
        fold_free = [None, None]
        for dc in range(8):
            for t2 in range(4):
                par = (dc * 4 + t2) % 2
                w = []
                if dc == 0 and t2 == 0:
                    w = [self.tok_wet, self.tok_wg, pb_free[0], pb_free[1]]
                if fold_free[par] is not None:
                    w = list(w) + [fold_free[par]]
                p(lambda e, dc=dc, t2=t2, par=par: e.matmul(
                    self.pB[:, par * 512:(par + 1) * 512],
                    self.B2S[0:16, dc * 128:(dc + 1) * 128],
                    self.WET[0:16, t2 * 512:(t2 + 1) * 512],
                    start=True, stop=True), waits=w)
                tf = b.pe_mark()
                wadd = [tf] + (rtap_war if first_radd[0] else [])
                first_radd[0] = False
                d(lambda e, dc=dc, t2=t2, par=par: e.tensor_add(
                    self.R[:, dc * 2048 + t2 * 512:dc * 2048 + (t2 + 1) * 512],
                    self.R[:, dc * 2048 + t2 * 512:dc * 2048 + (t2 + 1) * 512],
                    self.pB[:, par * 512:(par + 1) * 512]), waits=wadd, inc=True)
                fold_free[par] = b.tdve()
        awin15 = b.tpe()
        ygg = [b.grp("yg0"), b.grp("yg1")]
        ring_war = [None, None]
        radd_prev = None
        for tc in range(16):
            par = tc % 2
            w0 = []
            if tc == 0:
                w0 = [(ysg[0], ysg[0].n), (ysg[1], ysg[1].n), awin15]
            if ring_war[par] is not None:
                w0.append(ring_war[par])
            yt = None
            for s in range(4):
                yt = b.dsi("yg%d" % par,
                           self.ar(OFF_XN + par * 4096 + s * 1024, 1024),
                           self.ycomp_d,
                           self.SLOTI[:, 4 * tc + s:4 * tc + s + 1],
                           waits=w0 if s == 0 else ())
            ya = lambda s, par=par: self.ar(OFF_XN + par * 4096 + s * 1024, 1024)
            d(lambda e, ya=ya: e.tensor_add(self.ar(OFF_SC + 1024, 1024),
                                            ya(0), ya(1)), waits=[yt])
            d(lambda e, ya=ya: e.tensor_add(self.ar(OFF_SC + 2048, 1024),
                                            ya(2), ya(3)))
            d(lambda e, par=par: e.tensor_add(
                self.QSTG[:, par * 1024:(par + 1) * 1024],
                self.ar(OFF_SC + 1024, 1024), self.ar(OFF_SC + 2048, 1024)),
              inc=True)
            ysum_t = b.tdve()
            ring_war[par] = ysum_t
            for j in range(8):
                w = [ysum_t, ptr_free[0]] if j == 0 else ()
                p(lambda e, par=par, j=j: e.transpose(
                    self.ptr[:, j * 128:(j + 1) * 128],
                    self.QSTG[:, par * 1024 + j * 128:par * 1024 + (j + 1) * 128],
                    self.IDENT[:, :]), waits=w)
            tmm = b.pe_mark()
            for j in range(8):
                d(lambda e, j=j, tc=tc: e.tensor_add(
                    self.R[:, j * 2048 + tc * 128:j * 2048 + (tc + 1) * 128],
                    self.R[:, j * 2048 + tc * 128:j * 2048 + (tc + 1) * 128],
                    self.ptr[:, j * 128:(j + 1) * 128]),
                  waits=[tmm] if j == 0 else (), inc=(j == 7))
            radd_prev = b.tdve()
            ptr_free[0] = radd_prev
        self.tok_ffn_pe = b.tpe()
        self.ptr_last = radd_prev
        return b.tdve()

    def attn_b(self, sub=3):
        b = self.b
        MS = self.MS
        d, p = b.dve, b.pe
        tok_q = b.dgp("qkvq", v3(self.ar(WB0, 8192), 1024),
                      self.qkvT[:, 0:1024].rearrange("(dc p) o -> p dc o", p=128),
                      waits=[b.tpe()])
        tok_k = b.dgp("qkvk", v3(self.ar(WB1, 8192), 1024),
                      self.qkvT[:, 1024:2048].rearrange("(dc p) o -> p dc o", p=128),
                      waits=[self.last_ln_pe])

        def mk_sink(base):
            def sink(oc, tile, tok):
                h1, h2 = 2 * oc, 2 * oc + 1
                r1 = (h1 % 4) * 768 + base + (h1 // 4) * 64
                r2 = (h2 % 4) * 768 + base + (h2 // 4) * 64
                b.dsp("agb", self.ag_b_in[r1:r1 + 64, :], tile[0:64, :],
                      waits=[tok])
                return b.dsp("agb", self.ag_b_in[r2:r2 + 64, :], tile[64:128, :])
            return sink

        rope_q = (self.PERMB[:, :], OFF_TAB + 4096, OFF_TAB + 6144)
        rope_k = (self.PERMB[:, :], OFF_TAB, OFF_TAB + 2048)
        tq_pe = self.proj(WB0, 1024, [(oc, None) for oc in range(8)], rope_q,
                          mk_sink(0),
                          extra_wait=[tok_q, self.tok_xn3, self.tok_tab])
        tok_v = b.dgp("qkvw2", v3(self.ar(WB0, 8192), 1024),
                      self.qkvT[:, 2048:3072].rearrange("(dc p) o -> p dc o", p=128),
                      waits=[tq_pe])
        gab = b.grp("agb")
        self.proj(WB1, 1024, [(oc, None) for oc in range(8)], rope_k,
                  mk_sink(256), extra_wait=[tok_k],
                  stg_war=[(gab, gab.n)])
        self.proj(WB0, 1024, [(oc, None) for oc in range(8)], None,
                  mk_sink(512), extra_wait=[tok_v], stg_war=[(gab, gab.n)])
        g_agb = b.grp("agb")
        tok_contrib = (g_agb, g_agb.n)
        self.tap_dram("agb", self.ag_b_in, waits=[tok_contrib])
        tok_agb = b.coll("b", [[0, 1, 2, 3], [4, 5, 6, 7]],
                         [self.ag_b_in], [self.ag_b_out], waits=[tok_contrib])
        if sub < 2:
            self.tok_oprojb_pe = b.tpe()
            return b.tdve()
        EI = self.ar(B_EI, 1024)
        kload_war = None
        ctx_war = None
        self.ctx_store = [None, None]
        it = 0
        for hh in range(4):
            wk = [tok_agb]
            if kload_war is not None:
                wk.append(kload_war)
            ghb = "hb%d" % (hh % 2)
            b.dgi(ghb, v3(self.AR[:, B_KDUP:B_KDUP + 8192], 2048)[:, 0:1, :][:, 0, :]
                  if False else self.AR[:, B_KDUP + 0 * 2048:B_KDUP + 2048],
                  self.ag_b_out, self.IDX[:, hh * 4:hh * 4 + 1], waits=wk)
            for r in range(1, 4):
                b.dgi(ghb, self.AR[:, B_KDUP + r * 2048:B_KDUP + (r + 1) * 2048],
                      self.ag_b_out, self.IDX[:, hh * 4 + r:hh * 4 + r + 1])
            for r in range(4):
                b.dgi(ghb, self.AR[:, B_QDUP + r * 2048:B_QDUP + (r + 1) * 2048],
                      self.ag_b_out,
                      self.IDX[:, 16 + hh * 4 + r:17 + hh * 4 + r])
            for r in range(4):
                tok_load = b.dgi(
                    ghb, self.AR[0:64, B_VFM + r * 2048:B_VFM + (r + 1) * 2048],
                    self.ag_b_out, self.IDX[0:64, 32 + hh * 4 + r:33 + hh * 4 + r])
            tok_vaug, _ = self.vaug_build(B_VFM, B_VAUG, tok_load)
            b.pool(lambda e: e.iota(MS[64:65, 0:1024], [[1, 1024]], base=0,
                                    channel_multiplier=0,
                                    allow_small_or_imprecise_dtypes=True),
                   waits=[b.tpe(), b.tdve()], inc=True)
            tio = b.tpool()
            b.act(lambda e, hh=hh: e.activation(
                MS[64:65, 0:1024], MS[64:65, 0:1024], AF.Exp, bias=0.0,
                scale=self.SLP[64:65, 8 + hh:9 + hh]),
                waits=[tio, self.tok_slp], inc=True)
            tei = b.tact()
            tev = None
            for half in range(2):
                web = [tei, b.tdve()]
                if tev is not None:
                    web.append(tev)
                p(lambda e, half=half: e.matmul(
                    self.pb6[:, :], self.ONESF[64:65, :],
                    MS[64:65, half * 512:(half + 1) * 512],
                    start=True, stop=True), waits=web)
                tbm = b.pe_mark()
                d(lambda e, half=half: e.tensor_copy(
                    EI[:, half * 512:(half + 1) * 512], self.pb6[:, :]),
                  waits=[tbm], inc=True)
                tev = b.tdve()
            d(lambda e, hh=hh: e.tensor_scalar(
                MS[:, 3200:3327], self.PF[:, 0:127], 0.0,
                self.SLP[:, 4 + hh:5 + hh], op0=ALU.min, op1=ALU.mult))
            d(lambda e, hh=hh: e.tensor_scalar(
                MS[:, 3330:3331], self.IOC[:, 0:1], -127.0,
                self.SLP[:, 4 + hh:5 + hh], op0=ALU.add, op1=ALU.mult))
            d(lambda e: e.tensor_scalar(MS[:, 3331:3332], MS[:, 3330:3331], -BEXP,
                                        None, op0=ALU.add), inc=True)
            tok_hbias = b.tdve()
            s8_min = 8.0 * (0.7071067811865476 ** (4 + 4 * hh))
            wcap = (BEXP + 150.0) / s8_min
            for qt in range(8):
                Q0 = 1024 * qt
                chunks = [tc for tc in range(64)
                          if tc * 128 + 128 > Q0 - wcap or tc * 128 + 128 > Q0]
                d(lambda e, Q0=Q0: e.tensor_scalar(
                    MS[:, 3072:3136], self.JG[:, :], -float(Q0), None, op0=ALU.add),
                  waits=[b.tact()])
                d(lambda e, hh=hh: e.tensor_scalar(
                    MS[:, 3136:3200], MS[:, 3072:3136],
                    self.SLP[:, 4 + hh:5 + hh], -BEXP, op0=ALU.mult, op1=ALU.add),
                  inc=True)
                tok_cols = b.tdve()

                def expb(sc, wpe, wav, ubuf, Q0=Q0, tok_cols=tok_cols):
                    base_w = [wpe, tok_cols] + ([wav] if wav is not None else [])
                    tdve = None
                    ci = 0
                    while ci < 2:
                        tc = 2 * sc + ci
                        Bu = tc * 128
                        U = self.ar(ubuf, 2048)
                        if Bu >= Q0 + 1024:
                            if ci == 0 and (2 * sc + 1) * 128 >= Q0 + 1024:
                                b.act(lambda e, U=U: e.activation(
                                    U[:, 0:2048], self.pA[:, :], AF.Exp,
                                    bias=-BEXP, scale=1.0),
                                    waits=base_w, inc=True)
                                ci = 2
                                continue
                            b.act(lambda e, U=U, ci=ci: e.activation(
                                U[:, ci * 1024:(ci + 1) * 1024],
                                self.pA[:, ci * 1024:(ci + 1) * 1024], AF.Exp,
                                bias=-BEXP, scale=1.0), waits=base_w, inc=True)
                        elif Bu + 128 <= Q0:
                            b.act(lambda e, U=U, ci=ci, tc=tc: e.activation(
                                U[:, ci * 1024:(ci + 1) * 1024],
                                self.pA[:, ci * 1024:(ci + 1) * 1024], AF.Exp,
                                bias=MS[:, 3136 + tc:3137 + tc], scale=1.0),
                                waits=base_w, inc=True)
                            tpen = b.tact()
                            d(lambda e, U=U, ci=ci: e.tensor_mul(
                                U[:, ci * 1024:(ci + 1) * 1024],
                                U[:, ci * 1024:(ci + 1) * 1024], EI[:, :]),
                              waits=[tpen], inc=True)
                            tdve = b.tdve()
                        else:
                            fz = Bu - Q0
                            if fz > 0:
                                b.act(lambda e, U=U, ci=ci, fz=fz: e.activation(
                                    U[:, ci * 1024:ci * 1024 + fz],
                                    self.pA[:, ci * 1024:ci * 1024 + fz], AF.Exp,
                                    bias=-BEXP, scale=1.0), waits=base_w, inc=True)
                            sc0 = 3456 + ci * 128
                            d(lambda e, ci=ci, fz=fz, sc0=sc0: e.tensor_add(
                                MS[:, sc0:sc0 + 127],
                                self.pA[:, ci * 1024 + fz:ci * 1024 + fz + 127],
                                MS[:, 3200:3327]),
                              waits=base_w + [b.tact()], inc=True)
                            tband = b.tdve()
                            b.act(lambda e, U=U, ci=ci, fz=fz, sc0=sc0: e.activation(
                                U[:, ci * 1024 + fz:ci * 1024 + fz + 127],
                                MS[:, sc0:sc0 + 127], AF.Exp, bias=-BEXP, scale=1.0),
                                waits=[tband] + base_w, inc=True)
                            ps_ = fz + 127
                            if ps_ < 1024:
                                b.act(lambda e, U=U, ci=ci, ps_=ps_: e.activation(
                                    U[:, ci * 1024 + ps_:(ci + 1) * 1024],
                                    self.pA[:, ci * 1024 + ps_:(ci + 1) * 1024],
                                    AF.Exp, bias=MS[:, 3331:3332], scale=1.0),
                                    waits=base_w, inc=True)
                                tps = b.tact()
                                d(lambda e, U=U, ci=ci, ps_=ps_: e.tensor_mul(
                                    U[:, ci * 1024 + ps_:(ci + 1) * 1024],
                                    U[:, ci * 1024 + ps_:(ci + 1) * 1024],
                                    EI[:, 0:1024 - ps_]),
                                  waits=[tps], inc=True)
                                tdve = b.tdve()
                        ci += 1
                    return b.tact(), tdve

                wl = [tok_load, tok_vaug, tok_hbias, tev, self.tok_consts]
                par = it % 2
                it += 1

                def sinkb(ctxt, tok, hh=hh, qt=qt, par=par):
                    return b.dgp(
                        "bctx%d" % par,
                        self.ag_c_in[(qt // 2) * 256 + hh * 64:
                                     (qt // 2) * 256 + hh * 64 + 64,
                                     (qt % 2) * 1024:((qt % 2) + 1) * 1024],
                        ctxt, waits=[tok])

                def expb(tc, wpe, wav, ubuf, half, Q0=Q0, tok_cols=tok_cols):
                    base_w = [wpe, tok_cols] + ([wav] if wav is not None else [])
                    tdve = None
                    Bu = tc * 128
                    U = self.ar(ubuf, 1024)
                    pa_h = self.pA[:, half * 1024:(half + 1) * 1024]
                    if Bu >= Q0 + 1024:
                        b.act(lambda e, U=U, pa_h=pa_h: e.activation(
                            U[:, :], pa_h, AF.Exp, bias=-BEXP, scale=1.0),
                            waits=base_w, inc=True, drain=False)
                    elif Bu + 128 <= Q0:
                        b.act(lambda e, U=U, pa_h=pa_h, tc=tc: e.activation(
                            U[:, :], pa_h, AF.Exp,
                            bias=MS[:, 3136 + tc:3137 + tc], scale=1.0),
                            waits=base_w, inc=True, drain=False)
                        tpen = b.tact()
                        d(lambda e, U=U: e.tensor_mul(U[:, :], U[:, :], EI[:, :]),
                          waits=[tpen], inc=True)
                        tdve = b.tdve()
                    else:
                        fz = Bu - Q0
                        if fz > 0:
                            b.act(lambda e, U=U, pa_h=pa_h, fz=fz: e.activation(
                                U[:, 0:fz], pa_h[:, 0:fz], AF.Exp, bias=-BEXP,
                                scale=1.0), waits=base_w, inc=True, drain=False)
                        sc0 = 3456 + half * 128
                        d(lambda e, pa_h=pa_h, fz=fz, sc0=sc0: e.tensor_add(
                            MS[:, sc0:sc0 + 127], pa_h[:, fz:fz + 127],
                            MS[:, 3200:3327]), waits=base_w + [b.tact()], inc=True)
                        tband = b.tdve()
                        b.act(lambda e, U=U, fz=fz, sc0=sc0: e.activation(
                            U[:, fz:fz + 127], MS[:, sc0:sc0 + 127], AF.Exp,
                            bias=-BEXP, scale=1.0), waits=[tband] + base_w, inc=True)
                        ps_ = fz + 127
                        if ps_ < 1024:
                            b.act(lambda e, U=U, pa_h=pa_h, ps_=ps_: e.activation(
                                U[:, ps_:1024], pa_h[:, ps_:1024], AF.Exp,
                                bias=MS[:, 3331:3332], scale=1.0),
                                waits=base_w, inc=True, drain=False)
                            tps = b.tact()
                            d(lambda e, U=U, ps_=ps_: e.tensor_mul(
                                U[:, ps_:1024], U[:, ps_:1024],
                                EI[:, 0:1024 - ps_]), waits=[tps], inc=True)
                            tdve = b.tdve()
                    return b.tact(), tdve

                ctx_war, ctx_tok, kload_war = self.flash_head(
                    B_KDUP, B_QDUP, B_VAUG, par, Q0, wl, expb, sinkb,
                    ctx_war, self.ctx_store[par], chunks=chunks)
                self.ctx_store[par] = ctx_tok
        g_b0, g_b1 = b.grp("bctx0"), b.grp("bctx1")
        tok_bctx = [(g_b0, g_b0.n), (g_b1, g_b1.n)]
        self.tap_dram("ctxb", self.ag_c_in, waits=tok_bctx)
        tok_agc = b.coll("c", [[0, 1, 2, 3], [4, 5, 6, 7]],
                         [self.ag_c_in], [self.ag_c_out], waits=tok_bctx)
        if sub < 3:
            self.tok_oprojb_pe = b.tpe()
            return b.tdve()
        tok_proj = b.dgp("pw", v3(self.ar(WB0, 8192), 1024),
                         self.projT.rearrange("(dc p) o -> p dc o", p=128),
                         waits=[kload_war])
        for pt in range(8):
            wg = [tok_agc, kload_war] if pt == 0 else ()
            tok_ctx = b.dgi("cg", self.AR[:, OFF_XN + pt * 2048:
                                          OFF_XN + (pt + 1) * 2048],
                            self.ag_c_out, self.IDX[:, 48 + pt:49 + pt], waits=wg)
        pa_free = None
        for oc in range(8):
            w = [tok_proj, tok_ctx]
            if pa_free is not None:
                w.append(pa_free)
            for ch in range(4):
                for dc in range(8):
                    lhsT = self.ar(WB0 + dc * 1024 + oc * 128, 128)
                    p(lambda e, ch=ch, dc=dc, lhsT=lhsT: e.matmul(
                        self.pA[:, ch * 512:(ch + 1) * 512], lhsT,
                        self.xnp(dc)[:, ch * 512:(ch + 1) * 512],
                        start=(dc == 0), stop=(dc == 7)),
                        waits=w if (ch == 0 and dc == 0) else ())
            tmm = b.pe_mark()
            gt2b = b.grp("tap2")
            d(lambda e, oc=oc: e.scalar_tensor_tensor(
                self.R[:, oc * 2048:(oc + 1) * 2048], self.pA[:, :],
                self.PBc[:, oc:oc + 1], self.R[:, oc * 2048:(oc + 1) * 2048],
                op0=ALU.add, op1=ALU.add),
              waits=[tmm] + ([(gt2b, gt2b.n)] if (oc == 0 and self.dbg) else []),
              inc=True)
            pa_free = b.tdve()
        self.tok_oprojb_pe = b.tpe()
        return b.tdve()

    def build(self):
        st = self.stage
        b = self.b
        self.preamble()
        self.build_tables()
        self.tok_xn1 = self.layernorm(WB1, first=True)
        self.tok_ln1_pe = self.last_ln_pe
        if st >= 1:
            self.tap_sbuf("xn1", self.AR[:, OFF_XN:OFF_XN + 16384],
                          waits=[self.tok_xn1])
        if st >= 2:
            self.attn_a()
            if "r2" in self.dbg:
                self.dbg_tap_R("r2")
        if st >= 4:
            tok_xn2 = self.layernorm(OFF_TAB, war_pe_tok=self.tok_oproj_pe)
            qw = [(g, g.n) for g in (b.grp("qa0"), b.grp("qa1"), b.grp("aga"))]
            self.ffn(tok_xn2, 1, qstg_war=qw)
            if "r3" in self.dbg:
                self.dbg_tap_R("r3")
        if st >= 42:
            self.tok_xn3 = self.layernorm(OFF_TAB, war_pe_tok=self.tok_ffn_pe)
            self.build_tables(reload_pos=True)
            if st >= 45:
                self.attn_b(sub=(1 if st < 48 else (2 if st < 59 else 3)))
            if "r4" in self.dbg:
                self.dbg_tap_R("r4")
        if st >= 60:
            tok_xn4 = self.layernorm(OFF_TAB, war_pe_tok=self.tok_oprojb_pe)
            qw = [(b.grp("agb"), b.grp("agb").n)]
            self.ffn(tok_xn4, 2, qstg_war=qw)
        tok_out = b.dsp("out", self.xout.rearrange("(pt p) t -> p pt t", p=128),
                        v3(self.R[:, :], 2048), waits=[b.tdve()])
        self.tap_toks.append(tok_out)
        fin = {}
        for sem, val in self.tap_toks:
            if val > fin.get(id(sem), (None, -1))[1]:
                fin[id(sem)] = (sem, val)
        b.on("sp", lambda e: e.nop(), waits=list(fin.values()), drain=False)
        return self.finalize()

    def finalize(self):
        b = self.b
        with self.ctx:
            block = self.ctx.enter_context(self.nc.Block())
            block.sync(lambda e: b.replay("sp", e))
            block.tensor(lambda e: b.replay("pe", e))
            block.scalar(lambda e: b.replay("act", e))
            block.vector(lambda e: b.replay("dve", e))
            block.gpsimd(lambda e: b.replay("pool", e))
        return self.nc


def kernel(**inputs):
    in_maps = host_prepare(inputs)
    mk = MK(stage=99)
    nc = mk.build()
    res = run_bass_kernel_spmd(nc, in_maps, core_ids=list(range(8)))
    return host_assemble(res.results)


# version 08 (final)
# speedup vs baseline: 1.0076x; 1.0076x over previous
import numpy as np
import concourse.bass as bass
import concourse.mybir as mybir
from concourse.bass import IndirectOffsetOnAxis
from concourse.bass_utils import run_bass_kernel_spmd
from contextlib import ExitStack

F32 = mybir.dt.float32
BF16 = mybir.dt.bfloat16
I32 = mybir.dt.int32
AF = mybir.ActivationFunctionType
ALU = mybir.AluOpType
AXX = mybir.AxisListType.X

D = 1024
TC = 2048
T = 8192
HD = 64
NH = 16
E = 16
BEXP = 20.0
MAGIC = float(np.float32(1.5 * 2.0 ** 23))
LN10K = float(np.log(10000.0))
TWO_PI = float(2.0 * np.pi)

AR_COLS = 53248
OFF_TAB = 0
OFF_XN = 8192
OFF_WB = 24576
OFF_SC = 49152
WB0 = OFF_WB
WB1 = OFF_WB + 8192
WB2 = OFF_WB + 16384
A_QDUP = OFF_XN + 0
A_KDUP = OFF_XN + 2048
A_VAUG = OFF_XN + 10240
A_CTX = OFF_XN + 14464
A_VFM = WB0
A_WO = WB1
B_QDUP = OFF_XN + 0
B_KDUP = OFF_XN + 8192
B_VFM = WB0
B_VAUG = WB1
B_CTX = WB1 + 4224
B_EI = WB1 + 5248
U0 = OFF_SC
U1 = OFF_SC + 2048
VA_STRIDE = 66
SER_FFN = False
CAP = 1280
NCH = CAP // 128
WINS = [(0, 512), (512, 512), (1024, 256)]
NWIN = len(WINS)
CHWIN = [0, 0, 0, 0, 1, 1, 1, 1, 2, 2]


def host_prepare(inputs):
    f = np.ascontiguousarray
    x = np.asarray(inputs["L0_attn_a_hidden_states"])
    xT = f(np.asarray(x).transpose(0, 2, 1))
    wqT = f(np.asarray(inputs["L0_attn_a_Wq"]).T)
    wkT = f(np.asarray(inputs["L0_attn_a_Wk"]).T)
    wvT = f(np.asarray(inputs["L0_attn_a_Wv"]).T)
    woT = f(np.asarray(inputs["L0_attn_a_Wo"]).T)
    bq = np.asarray(inputs["L0_attn_a_bq"]); bk = np.asarray(inputs["L0_attn_a_bk"])
    bv = np.asarray(inputs["L0_attn_a_bv"]); bo = np.asarray(inputs["L0_attn_a_bo"])
    abias = np.zeros((128, 20), np.float32)
    for oc in range(8):
        abias[:, oc] = bq[oc * 128:(oc + 1) * 128]
        abias[:, 12 + oc] = bo[oc * 128:(oc + 1) * 128]
    for oc in range(2):
        abias[:, 8 + oc] = bk[oc * 128:(oc + 1) * 128]
        abias[:, 10 + oc] = bv[oc * 128:(oc + 1) * 128]
    qkvT = f(np.asarray(inputs["L0_attn_b_qkv_w"]).T)
    hperm = [r + 4 * m for r in range(4) for m in range(4)]
    projP = np.asarray(inputs["L0_attn_b_proj_w"]).reshape(D, NH, HD)[:, hperm, :].reshape(D, D)
    projT = f(projP.T)
    pb = np.asarray(inputs["L0_attn_b_proj_b"])
    pbias = np.zeros((128, 8), np.float32)
    for oc in range(8):
        pbias[:, oc] = pb[oc * 128:(oc + 1) * 128]
    slopes = np.asarray(inputs["L0_attn_b_slopes"])
    wgT = f(np.asarray(inputs["L0_ffn_Wg"]).T)
    w1T = f(np.asarray(inputs["L0_ffn_W1"]).transpose(0, 2, 1))
    w3T = f(np.asarray(inputs["L0_ffn_W3"]).transpose(0, 2, 1))
    w2T = f(np.asarray(inputs["L0_ffn_W2"]).transpose(0, 2, 1))
    b1c = f(np.asarray(inputs["L0_ffn_b1"]).reshape(E, 8, 128).transpose(0, 2, 1))
    b2c = f(np.asarray(inputs["L0_ffn_b2"]).reshape(E, 8, 128).transpose(0, 2, 1))
    b3c = f(np.asarray(inputs["L0_ffn_b3"]).reshape(E, 8, 128).transpose(0, 2, 1))
    b2s = np.asarray(inputs["L0_ffn_b2"]).astype(np.float32)
    shared = dict(wqT=wqT, wkT=wkT, wvT=wvT, woT=woT, abias=abias,
                  qkvT=qkvT, projT=projT, pbias=pbias,
                  wgT=wgT, w1T=w1T, w3T=w3T, w2T=w2T, b1c=b1c, b2c=b2c, b3c=b3c,
                  b2s=b2s)
    in_maps = []
    p = np.arange(128)
    for c in range(8):
        seq, g = c // 4, c % 4
        m = dict(shared)
        m["xin"] = f(xT[seq][:, g * TC:(g + 1) * TC]).astype(np.float32)
        m["posrow"] = np.arange(g * TC, (g + 1) * TC, dtype=np.float32).reshape(1, TC)
        m["slopes_rep"] = np.broadcast_to(
            slopes[[g, g + 4, g + 8, g + 12]].astype(np.float32), (128, 4)).copy()
        idx = np.zeros((128, 56), np.int32)
        col = 0
        for hh in range(4):
            for r in range(4):
                idx[:, col] = r * 3072 + g * 768 + 256 + hh * 64 + (p % 64); col += 1
        for hh in range(4):
            for r in range(4):
                idx[:, col] = r * 3072 + g * 768 + 0 + hh * 64 + (p % 64); col += 1
        for hh in range(4):
            for r in range(4):
                idx[:, col] = r * 3072 + g * 768 + 512 + hh * 64 + (p % 64); col += 1
        for pt in range(8):
            r = pt // 2
            mem = 2 * (pt % 2) + p // 64
            idx[:, col] = r * 1024 + g * 256 + mem * 64 + (p % 64); col += 1
        m["idxc"] = idx
        in_maps.append(m)
    return in_maps


def host_assemble(results):
    out = np.zeros((2, T, D), np.float32)
    for c, r in enumerate(results):
        seq, g = c // 4, c % 4
        out[seq, g * TC:(g + 1) * TC, :] = np.asarray(r["xout"]).T
    return out



def v3(ap2d, b):
    return ap2d.rearrange("p (a b) -> p a b", b=b)


class Sem:
    def __init__(self, nc, ctx, name, is_group=False):
        self.h = ctx.enter_context(nc.semaphore(name))
        self.n = 0
        self.is_group = is_group


class Bld:
    ENGS = ("pe", "act", "dve", "pool", "sp")
    DRAIN = {"act", "dve", "pool"}

    def __init__(self, nc, ctx):
        self.nc = nc
        self.ctx = ctx
        self.prog = {e: [] for e in self.ENGS}
        self.spe = Sem(nc, ctx, "spe")
        self.sact = Sem(nc, ctx, "sact")
        self.sdve = Sem(nc, ctx, "sdve")
        self.spool = Sem(nc, ctx, "spool")
        self.groups = {}

    def grp(self, name):
        if name not in self.groups:
            self.groups[name] = Sem(self.nc, self.ctx, "g_" + name, is_group=True)
        return self.groups[name]

    def on(self, eng, fn, waits=(), inc=None, inc_amt=1, drain=None):
        if drain is None:
            drain = eng in self.DRAIN
        for sem, val in waits:
            assert not getattr(sem, "is_group", False) or val == sem.n, \
                f"stale group wait: {val} != {sem.n}"
        if inc is not None:
            inc.n += inc_amt
        self.prog[eng].append((fn, tuple(waits), inc, inc_amt, drain))

    def dsp(self, grp, out, in_, waits=(), **kw):
        g = self.grp(grp)
        fn = lambda e, out=out, in_=in_, kw=kw: e.dma_start(out=out, in_=in_, **kw)
        self.on("sp", fn, waits, inc=g, inc_amt=16, drain=False)
        return (g, g.n)

    def dgp(self, grp, out, in_, waits=(), **kw):
        g = self.grp(grp)
        fn = lambda e, out=out, in_=in_, kw=kw: e.dma_start(out=out, in_=in_, **kw)
        self.on("pool", fn, waits, inc=g, inc_amt=16, drain=False)
        return (g, g.n)

    def dgi(self, grp, out, in_, idx_ap, waits=()):
        g = self.grp(grp)
        fn = lambda e, out=out, in_=in_, idx_ap=idx_ap: e.indirect_dma_start(
            out, None, in_, IndirectOffsetOnAxis(idx_ap, 0))
        self.on("pool", fn, waits, inc=g, inc_amt=16, drain=False)
        return (g, g.n)

    def dsi(self, grp, out, in_, idx_ap, waits=()):
        g = self.grp(grp)
        fn = lambda e, out=out, in_=in_, idx_ap=idx_ap: e.indirect_dma_start(
            out, None, in_, IndirectOffsetOnAxis(idx_ap, 0))
        self.on("pool", fn, waits, inc=g, inc_amt=16, drain=False)
        return (g, g.n)

    def dsis(self, grp, out, idx_ap, in_, waits=()):
        g = self.grp(grp)
        fn = lambda e, out=out, in_=in_, idx_ap=idx_ap: e.indirect_dma_start(
            out, IndirectOffsetOnAxis(idx_ap, 0), in_, None)
        self.on("pool", fn, waits, inc=g, inc_amt=16, drain=False)
        return (g, g.n)

    def coll(self, name, groups, ins, outs, waits=()):
        g = self.grp("cc_" + name)
        fn = lambda e, groups=groups, ins=ins, outs=outs: e.collective_compute(
            "AllGather", ALU.bypass, replica_groups=groups, ins=ins, outs=outs)
        self.on("pool", fn, waits, inc=g, inc_amt=1, drain=False)
        return (g, g.n)

    def pe_mark(self):
        return (self.spe, self.mark("pe"))

    def pool_mark(self):
        return (self.spool, self.mark("pool"))

    def tpe(self):
        return (self.spe, self.spe.n)

    def tact(self):
        return (self.sact, self.sact.n)

    def tdve(self):
        return (self.sdve, self.sdve.n)

    def tpool(self):
        return (self.spool, self.spool.n)

    def pe(self, fn, waits=(), inc=False):
        self.on("pe", fn, waits, inc=self.spe if inc else None, drain=False)
        return self.spe.n

    def act(self, fn, waits=(), inc=False, drain=None):
        self.on("act", fn, waits, inc=self.sact if inc else None, drain=drain)
        return self.sact.n

    def dve(self, fn, waits=(), inc=False):
        self.on("dve", fn, waits, inc=self.sdve if inc else None)
        return self.sdve.n

    def pool(self, fn, waits=(), inc=False):
        self.on("pool", fn, waits, inc=self.spool if inc else None)
        return self.spool.n

    def mark(self, eng):
        sems = {"pe": self.spe, "act": self.sact, "dve": self.sdve,
                "pool": self.spool}
        sem = sems[eng]
        fn, waits, inc, amt, drain = self.prog[eng][-1]
        assert inc is None, "last entry already incs"
        sem.n += 1
        self.prog[eng][-1] = (fn, waits, sem, amt if inc else 1, drain)
        return sem.n

    def replay(self, key, engine):
        waited = {}
        for fn, waits, inc, inc_amt, drain in self.prog[key]:
            for sem, val in waits:
                if val > 0 and val > waited.get(id(sem), -1):
                    engine.wait_ge(sem.h, val)
                    waited[id(sem)] = val
            if drain:
                engine.drain()
            inst = fn(engine)
            if inc is not None:
                inst.then_inc(inc.h, inc_amt)


class MK:
    def __init__(self, stage=99, taps=()):
        self.stage = stage
        self.taps = set(taps)
        self.nc = bass.Bass("TRN2", target_bir_lowering=False)
        self.ctx = ExitStack()
        self._declare()
        self.b = Bld(self.nc, self.ctx)
        self.tap_toks = []
        self.ptr_last = None

    def _declare(self):
        nc, ctx = self.nc, self.ctx
        din = lambda n, s: nc.dram_tensor(n, s, F32, kind="ExternalInput").ap()
        self.xin = din("xin", [D, TC])
        self.posrow_d = din("posrow", [1, TC])
        self.slopes_d = din("slopes_rep", [128, 4])
        self.idxc_d = nc.dram_tensor("idxc", [128, 56], I32, kind="ExternalInput").ap()
        self.abias_d = din("abias", [128, 20])
        self.pbias_d = din("pbias", [128, 8])
        self.wqT = din("wqT", [D, D]); self.wkT = din("wkT", [D, 256])
        self.wvT = din("wvT", [D, 256]); self.woT = din("woT", [D, D])
        self.qkvT = din("qkvT", [D, 3 * D]); self.projT = din("projT", [D, D])
        self.wgT_d = din("wgT", [D, E])
        self.w1T = din("w1T", [E, D, D]); self.w3T = din("w3T", [E, D, D])
        self.w2T = din("w2T", [E, D, D])
        self.b1c_d = din("b1c", [E, 128, 8]); self.b2c_d = din("b2c", [E, 128, 8])
        self.b3c_d = din("b3c", [E, 128, 8])
        self.b2s_d = din("b2s", [E, D])
        self.xout = nc.dram_tensor("xout", [D, TC], F32, kind="ExternalOutput").ap()
        dbf = lambda n, s: nc.dram_tensor(n, s, BF16).ap()
        self.ag_a_in = dbf("ag_a_in", [512, TC])
        self.ag_a_out = dbf("ag_a_out", [2048, TC])
        self.ag_b_in = dbf("ag_b_in", [3072, TC])
        self.ag_b_out = dbf("ag_b_out", [12288, TC])
        self.ag_c_in = dbf("ag_c_in", [1024, TC])
        self.ag_c_out = dbf("ag_c_out", [4096, TC])
        self.qa_dram = dbf("qa_dram", [D, TC])
        self.ctxa_dram = dbf("ctxa_dram", [D, TC])
        self.xtok_d = dbf("xtok", [TC, D])
        self.ycomp_d = dbf("ycomp", [E * CAP, D])
        self.invmap_d = nc.dram_tensor("invmap", [E * CAP, 1], I32).ap()
        self.wcomp_d = nc.dram_tensor("wcomp", [E * CAP, 1], F32).ap()
        sb = lambda n, s, d: ctx.enter_context(nc.sbuf_tensor(n, s, d))
        self.R = sb("R", [128, 8 * TC], F32)
        self.MS = sb("MS", [128, 4096], F32)
        self.AR = sb("AR", [128, AR_COLS], BF16)
        self.WET = sb("WET", [128, 2048], BF16)
        self.AB = sb("AB", [128, 20], F32)
        self.PBc = sb("PBc", [128, 8], F32)
        self.B1 = sb("B1", [128, 128], F32)
        self.B2 = sb("B2", [128, 128], F32)
        self.B3 = sb("B3", [128, 128], F32)
        self.SLP = sb("SLP", [128, 12], F32)
        self.WGT = sb("WGT", [128, 128], BF16)
        self.IDX = sb("IDX", [128, 56], I32)
        self.IDENT = sb("IDENT", [128, 128], BF16)
        self.PERMA = sb("PERMA", [128, 128], BF16)
        self.PERMB = sb("PERMB", [128, 128], BF16)
        self.ONESB = sb("ONESB", [128, 128], BF16)
        self.ONESF = sb("ONESF", [128, 128], F32)
        self.ONESC = sb("ONESC", [128, 1], BF16)
        self.CONE = sb("CONE", [128, 8], F32)
        self.PF = sb("PF", [128, 128], F32)
        self.JG = sb("JG", [128, 64], F32)
        self.IOC = sb("IOC", [128, 1], F32)
        self.CONSTF = sb("CONSTF", [128, 2], F32)
        nc.const_aps.register(F32, -BEXP, self.CONSTF[:, 0:1])
        self.QRAW = sb("QRAW", [128, 2048], BF16)
        self.QSTG = sb("QSTG", [128, 2048], BF16)
        self.LTRI = sb("LTRI", [128, 128], BF16)
        self.SLOTI = sb("SLOTI", [128, 64], I32)
        self.IDXS = sb("IDXS", [128, 2 * NCH], I32)
        self.TOKID = sb("TOKID", [128, 16], I32)
        self.ZI32 = sb("ZI32", [128, E * CAP // 128], I32)
        self.WSTR = sb("WSTR", [128, 2 * NCH], F32)
        self.ZF32 = sb("ZF32", [128, E * CAP // 128], F32)
        self.B2S = sb("B2S", [16, D], BF16)
        ps = lambda n, s, d: ctx.enter_context(nc.psum_tensor(n, s, d))
        self.pA = ps("pA", [128, 2048], F32)
        self.pB = ps("pB", [128, 1024], F32)
        self.pb6 = ps("pb6", [128, 512], F32)
        self.ptr = ps("ptr", [128, 1024], BF16)
        self.dbg = {}
        shapes = {"xn1": [D, TC], "qa": [D, TC], "aga": [512, TC], "r2": [D, TC],
                  "wfull": [128, 256], "r3": [D, TC], "agb": [3072, TC],
                  "r4": [D, TC], "ctxa": [D, TC], "ctxb": [1024, TC]}
        for t in self.taps:
            self.dbg[t] = self.nc.dram_tensor(
                "dbg_" + t, shapes[t], F32, kind="ExternalOutput").ap()

    def ar(self, off, cols, rows=128, r0=0):
        return self.AR[r0:r0 + rows, off:off + cols]

    def xnp(self, pt):
        return self.ar(OFF_XN + pt * 2048, 2048)

    def tap_sbuf(self, name, src_ap, waits=()):
        if name in self.dbg:
            self.tap_toks.append(self.b.dgp(
                "tap", self.dbg[name].rearrange("(pt p) t -> p pt t", p=128),
                v3(src_ap, 2048), waits=waits))

    def tap_dram(self, name, src_ap, waits=()):
        if name in self.dbg:
            self.tap_toks.append(self.b.dgp("tap", self.dbg[name], src_ap,
                                            waits=waits))

    def dbg_tap_R(self, name):
        self.tap_toks.append(self.b.dsp(
            "tap2", self.dbg[name].rearrange("(pt p) t -> p pt t", p=128),
            v3(self.R[:, :], 2048), waits=[self.b.tdve()]))

    def preamble(self):
        b = self.b
        self.tok_R = b.dsp("in", v3(self.R[:, :], 2048),
                           self.xin.rearrange("(pt p) t -> p pt t", p=128))
        b.dsp("in", self.AB[:, :], self.abias_d)
        b.dsp("in", self.PBc[:, :], self.pbias_d)
        b.dsp("in", self.SLP[:, 0:4], self.slopes_d)
        b.dsp("in", self.IDX[:, :], self.idxc_d)
        b.dsp("in", self.MS[0:1, 0:TC], self.posrow_d)
        b.dsp("in", v3(self.B1[:, :], 8), self.b1c_d.rearrange("e p c -> p e c"))
        b.dsp("in", v3(self.B2[:, :], 8), self.b2c_d.rearrange("e p c -> p e c"))
        self.tok_in = b.dsp("in", v3(self.B3[:, :], 8),
                            self.b3c_d.rearrange("e p c -> p e c"))
        self.tok_R = self.tok_in
        b.dgp("wg", self.B2S[:, :], self.b2s_d)
        self.tok_wg = b.dgp("wg", v3(self.WGT[:, :], 16),
                            self.wgT_d.rearrange("(dc p) e -> p dc e", p=128))
        P = b.pool
        P(lambda e: e.memset(self.IDENT[:, :], 1.0))
        P(lambda e: e.affine_select(self.IDENT[:, :], self.IDENT[:, :], [[1, 128]],
                                    ALU.is_equal, 0.0, base=0, channel_multiplier=-1))
        P(lambda e: e.memset(self.ONESB[:, :], 1.0))
        P(lambda e: e.memset(self.ONESF[:, :], 1.0))
        P(lambda e: e.memset(self.ONESC[:, :], 1.0))
        P(lambda e: e.memset(self.CONE[:, :], 1.0))
        P(lambda e: e.memset(self.CONSTF[:, 0:2], -BEXP))
        P(lambda e: e.iota(self.PF[:, :], [[-1, 128]], base=0, channel_multiplier=1,
                           allow_small_or_imprecise_dtypes=True))
        P(lambda e: e.iota(self.JG[:, :], [[128, 64]], base=0, channel_multiplier=1,
                           allow_small_or_imprecise_dtypes=True))
        P(lambda e: e.iota(self.IOC[:, :], [[1, 1]], base=0, channel_multiplier=1,
                           allow_small_or_imprecise_dtypes=True))
        P(lambda e: e.memset(self.LTRI[:, :], 1.0))
        P(lambda e: e.affine_select(self.LTRI[:, :], self.LTRI[:, :], [[1, 128]],
                                    ALU.is_gt, 0.0, base=0, channel_multiplier=-1))
        P(lambda e: e.memset(self.ZI32[:, :], 0.0))
        P(lambda e: e.memset(self.ZF32[:, :], 0.0))
        P(lambda e: e.iota(self.MS[:, 3000:3016], [[128, 16]], base=0,
                           channel_multiplier=1,
                           allow_small_or_imprecise_dtypes=True))
        P(lambda e: e.memset(self.PERMA[:, :], 0.0))
        P(lambda e: e.memset(self.PERMB[:, :], 0.0))
        for bp in (0, 64):
            for (mat, half, st, bs) in (
                    (self.PERMA, 0, -2, -1), (self.PERMA, 1, -2, 0),
                    (self.PERMB, 0, -1, -32), (self.PERMB, 1, -1, 0)):
                sl = mat[bp:bp + 64, bp + half * 32: bp + half * 32 + 32]
                P(lambda e, sl=sl: e.memset(sl, 1.0))
                P(lambda e, sl=sl, st=st, bs=bs: e.affine_select(
                    sl, sl, [[st, 32]], ALU.is_equal, 0.0, base=bs,
                    channel_multiplier=1))
        self.tok_consts = b.pool_mark()
        d = b.dve
        d(lambda e: e.tensor_scalar(self.SLP[:, 4:8], self.SLP[:, 0:4], 8.0, None,
                                    op0=ALU.mult), waits=[self.tok_in])
        d(lambda e: e.tensor_scalar(self.SLP[:, 8:12], self.SLP[:, 0:4], -8.0, None,
                                    op0=ALU.mult), inc=True)
        self.tok_slp = b.tdve()
        d(lambda e: e.tensor_scalar(self.TOKID[:, :], self.MS[:, 3000:3016], 1.0,
                                    None, op0=ALU.mult), waits=[self.tok_consts],
          inc=True)
        self.tok_tokid = b.tdve()

    def build_tables(self, reload_pos=False):
        b = self.b
        MS, pA = self.MS, self.pA
        wpos = [self.tok_in]
        if reload_pos:
            wpos = [b.dsp("pos", MS[0:1, 0:TC], self.posrow_d,
                          waits=[b.tpe()])]
        b.pool(lambda e: e.iota(MS[0:1, 2048:2112], [[0, 2], [1, 32]], base=0,
                                channel_multiplier=0,
                                allow_small_or_imprecise_dtypes=True),
               waits=wpos, inc=True)
        tp = b.tpool()
        b.act(lambda e: e.activation(MS[0:1, 2112:2176], MS[0:1, 2048:2112],
                                     AF.Exp, bias=0.0, scale=-LN10K / 32.0),
              waits=[tp] + wpos, inc=True)
        ta = b.tact()
        wA = [ta, b.tdve(), b.tpe()]
        for ch in range(4):
            for bp in (0, 64):
                b.pe(lambda e, ch=ch, bp=bp: e.matmul(
                    pA[bp:bp + 64, ch * 512:(ch + 1) * 512],
                    MS[0:1, 2112:2176],
                    MS[0:1, ch * 512:(ch + 1) * 512],
                    start=True, stop=True, tile_position=(0, bp)),
                    waits=wA if (ch == 0 and bp == 0) else ())
        tpe = b.pe_mark()
        d = b.dve
        d(lambda e: e.tensor_scalar(MS[:, 0:2048], pA[:, :], 1.0 / TWO_PI, None,
                                    op0=ALU.mult), waits=[tpe], inc=True)
        self.tok_tabu = b.tdve()
        d(lambda e: e.tensor_scalar(MS[:, 2048:4096], MS[:, 0:2048],
                                    0.25, MAGIC, op0=ALU.add, op1=ALU.add))
        d(lambda e: e.tensor_scalar(MS[:, 2048:4096], MS[:, 2048:4096], -MAGIC,
                                    None, op0=ALU.add))
        d(lambda e: e.scalar_tensor_tensor(MS[:, 2048:4096], MS[:, 0:2048], 0.25,
                                           MS[:, 2048:4096], op0=ALU.add,
                                           op1=ALU.subtract), inc=True)
        tcos = b.tdve()
        cosk = self.ar(OFF_TAB, 2048)
        b.act(lambda e: e.activation(cosk, MS[:, 2048:4096], AF.Sin, bias=0.0,
                                     scale=TWO_PI), waits=[tcos], inc=True)
        tck = b.tact()
        d(lambda e: e.tensor_scalar(MS[:, 2048:4096], MS[:, 0:2048], MAGIC, None,
                                    op0=ALU.add), waits=[tck])
        d(lambda e: e.tensor_scalar(MS[:, 2048:4096], MS[:, 2048:4096], -MAGIC,
                                    None, op0=ALU.add))
        d(lambda e: e.tensor_sub(MS[:, 0:2048], MS[:, 0:2048], MS[:, 2048:4096]),
          inc=True)
        tsin = b.tdve()
        sink = self.ar(OFF_TAB + 2048, 2048)
        b.act(lambda e: e.activation(sink, MS[:, 0:2048], AF.Sin, bias=0.0,
                                     scale=TWO_PI), waits=[tsin], inc=True)
        tsk = b.tact()
        d(lambda e: e.tensor_scalar(self.AR[0:32, 2048:4096],
                                    self.AR[0:32, 2048:4096], -1.0, None,
                                    op0=ALU.mult), waits=[tsk])
        d(lambda e: e.tensor_scalar(self.AR[64:96, 2048:4096],
                                    self.AR[64:96, 2048:4096], -1.0, None,
                                    op0=ALU.mult))
        d(lambda e: e.tensor_scalar(self.ar(OFF_TAB + 4096, 2048), cosk, 0.125,
                                    None, op0=ALU.mult))
        d(lambda e: e.tensor_scalar(self.ar(OFF_TAB + 6144, 2048), sink, 0.125,
                                    None, op0=ALU.mult), inc=True)
        self.tok_tab = b.tdve()

    def layernorm(self, sq_off, first=False, war_pe_tok=None):
        b = self.b
        MS, pA = self.MS, self.pA
        d, p = b.dve, b.pe
        pe_done = [None] * 8
        for pt in range(8):
            w = []
            if first:
                w.append(self.tok_R)
            if war_pe_tok is not None:
                w.append(war_pe_tok)
            d(lambda e, pt=pt: e.tensor_copy(self.xnp(pt),
                                             self.R[:, pt * 2048:(pt + 1) * 2048]),
              waits=w, inc=True)
            wsq = [pe_done[pt - 2]] if pt >= 2 else []
            sqb = self.ar(sq_off + (pt % 2) * 2048, 2048)
            d(lambda e, pt=pt, sqb=sqb: e.tensor_mul(sqb, self.xnp(pt), self.xnp(pt)),
              waits=wsq, inc=True)
            tdve = b.tdve()
            for ch in range(4):
                p(lambda e, pt=pt, ch=ch: e.matmul(
                    pA[0:1, ch * 512:(ch + 1) * 512], self.ONESC[:, 0:1],
                    self.xnp(pt)[:, ch * 512:(ch + 1) * 512],
                    start=(pt == 0), stop=(pt == 7)),
                    waits=[tdve, self.tok_consts] if ch == 0 else ())
                p(lambda e, ch=ch, sqb=sqb, pt=pt: e.matmul(
                    pA[64:65, ch * 512:(ch + 1) * 512], self.ONESC[:, 0:1],
                    sqb[:, ch * 512:(ch + 1) * 512],
                    start=(pt == 0), stop=(pt == 7), tile_position=(0, 64)))
            pe_done[pt] = b.pe_mark()
        tstat = pe_done[7]
        self.last_ln_pe = tstat
        d(lambda e: e.tensor_scalar(MS[0:1, 0:2048], pA[0:1, :], 1.0 / D, None,
                                    op0=ALU.mult), waits=[tstat])
        d(lambda e: e.tensor_mul(MS[0:1, 2048:4096], MS[0:1, 0:2048],
                                 MS[0:1, 0:2048]))
        d(lambda e: e.scalar_tensor_tensor(MS[0:1, 2048:4096], pA[64:65, :],
                                           1.0 / D, MS[0:1, 2048:4096],
                                           op0=ALU.mult, op1=ALU.subtract))
        d(lambda e: e.tensor_scalar(MS[0:1, 2048:4096], MS[0:1, 2048:4096], 1e-5,
                                    None, op0=ALU.add), inc=True)
        tvar = b.tdve()
        b.act(lambda e: e.activation(MS[0:1, 2048:4096], MS[0:1, 2048:4096], AF.Ln,
                                     bias=0.0, scale=1.0), waits=[tvar])
        b.act(lambda e: e.activation(MS[0:1, 2048:4096], MS[0:1, 2048:4096], AF.Exp,
                                     bias=0.0, scale=-0.5), inc=True)
        trstd = b.tact()
        mb = self.ar(OFF_SC, 2048)
        rb = self.ar(OFF_SC + 2048, 2048)
        prev = tvar
        for (col0, dst) in ((0, mb), (2048, rb)):
            for ch in range(4):
                wl = [trstd, prev]
                p(lambda e, col0=col0, ch=ch: e.matmul(
                    pA[:, ch * 512:(ch + 1) * 512], self.ONESF[0:1, :],
                    MS[0:1, col0 + ch * 512:col0 + (ch + 1) * 512],
                    start=True, stop=True), waits=wl if ch == 0 else ())
            tb = b.pe_mark()
            d(lambda e, dst=dst: e.tensor_copy(dst, pA[:, :]), waits=[tb], inc=True)
            prev = b.tdve()
        for pt in range(8):
            d(lambda e, pt=pt: e.tensor_sub(self.xnp(pt), self.xnp(pt), mb))
            d(lambda e, pt=pt: e.tensor_mul(self.xnp(pt), self.xnp(pt), rb),
              inc=(pt == 7))
        return b.tdve()

    def proj(self, wslot_off, wstride, oc_list, rope, sink, extra_wait=(),
             stg_war=()):
        b = self.b
        pA = self.pA
        d, p = b.dve, b.pe
        half_free = [None, None]
        stg_done = [None, None]
        stg_free = None
        stg_war = list(stg_war)
        last_evac = None
        for oc, bias in oc_list:
            rope_tok = None
            for half in range(2):
                w = list(extra_wait)
                if half_free[half] is not None:
                    w.append(half_free[half])
                for ch2 in range(2):
                    for dc in range(8):
                        lhsT = self.ar(wslot_off + dc * wstride + oc * 128, 128)
                        p(lambda e, half=half, ch2=ch2, dc=dc, lhsT=lhsT: e.matmul(
                            pA[:, half * 1024 + ch2 * 512:
                               half * 1024 + (ch2 + 1) * 512], lhsT,
                            self.xnp(dc)[:, half * 1024 + ch2 * 512:
                                         half * 1024 + (ch2 + 1) * 512],
                            start=(dc == 0), stop=(dc == 7)),
                            waits=w if (ch2 == 0 and dc == 0) else ())
                tmm = b.pe_mark()
                hs = slice(half * 1024, (half + 1) * 1024)
                pa_h = pA[:, hs]
                if rope is None:
                    wv = [tmm]
                    if half == 0:
                        if stg_free is not None:
                            wv.append(stg_free)
                        elif stg_war:
                            wv += stg_war
                    if bias is not None:
                        d(lambda e, bias=bias, hs=hs, pa_h=pa_h: e.tensor_scalar(
                            self.QSTG[:, hs], pa_h, bias, None, op0=ALU.add),
                          waits=wv, inc=True)
                    else:
                        d(lambda e, hs=hs, pa_h=pa_h: e.tensor_copy(
                            self.QSTG[:, hs], pa_h), waits=wv, inc=True)
                    half_free[half] = b.tdve()
                else:
                    perm, cos_off, sin_off = rope
                    if bias is not None:
                        d(lambda e, bias=bias, hs=hs, pa_h=pa_h: e.tensor_scalar(
                            self.QRAW[:, hs], pa_h, bias, None, op0=ALU.add),
                          waits=[tmm], inc=True)
                    else:
                        d(lambda e, hs=hs, pa_h=pa_h: e.tensor_copy(
                            self.QRAW[:, hs], pa_h), waits=[tmm], inc=True)
                    half_free[half] = b.tdve()
                    w2 = [half_free[half], self.tok_consts]
                    if last_evac is not None:
                        w2.append(last_evac)
                    for q2 in range(2):
                        p(lambda e, half=half, q2=q2, perm=perm: e.matmul(
                            self.pB[:, q2 * 512:(q2 + 1) * 512], perm,
                            self.QRAW[:, half * 1024 + q2 * 512:
                                      half * 1024 + (q2 + 1) * 512],
                            start=True, stop=True), waits=w2 if q2 == 0 else ())
                    tperm = b.pe_mark()
                    wv2 = [tperm]
                    if stg_done[half] is not None:
                        wv2.append(stg_done[half])
                    elif stg_war:
                        wv2 += stg_war
                    b.act(lambda e, hs=hs: e.activation(
                        self.QSTG[:, hs], self.pB[:, :], AF.Copy, bias=0.0,
                        scale=1.0), waits=wv2, inc=True)
                    evac_tok = b.tact()
                    last_evac = evac_tok
                    cos_ap = self.ar(cos_off + half * 1024, 1024)
                    sin_ap = self.ar(sin_off + half * 1024, 1024)
                    d(lambda e, hs=hs, cos_ap=cos_ap: e.tensor_mul(
                        self.QRAW[:, hs], self.QRAW[:, hs], cos_ap),
                      waits=[tperm])
                    d(lambda e, hs=hs, sin_ap=sin_ap: e.tensor_mul(
                        self.QSTG[:, hs], self.QSTG[:, hs], sin_ap),
                      waits=[evac_tok])
                    d(lambda e, hs=hs: e.tensor_add(
                        self.QSTG[:, hs], self.QSTG[:, hs], self.QRAW[:, hs]),
                      inc=True)
                    rope_tok = b.tdve()
                    half_free[half] = rope_tok
            if rope is None:
                stg_free = sink(oc, self.QSTG, half_free[1])
            else:
                tok = sink(oc, self.QSTG, rope_tok)
                stg_done = [tok, tok]
        return b.tpe()

    def vaug_build(self, vfm_off, vaug_off, tok_v):
        b = self.b
        p, d = b.pe, b.dve
        evac_tok = b.tdve()
        for bt in range(4):
            wt = [tok_v, evac_tok, self.tok_consts]
            for i in range(16):
                cc = bt * 16 + i
                p(lambda e, cc=cc, i=i, vfm_off=vfm_off: e.transpose(
                    self.ptr[:, i * 64:(i + 1) * 64],
                    self.ar(vfm_off + cc * 128, 128, rows=64),
                    self.IDENT[0:64, 0:64]),
                  waits=wt if i == 0 else ())
            tt = b.pe_mark()
            dst = self.AR[:, vaug_off + bt * 16 * VA_STRIDE:
                          vaug_off + (bt + 1) * 16 * VA_STRIDE].rearrange(
                "p (c v) -> p c v", v=VA_STRIDE)[:, :, 0:64]
            d(lambda e, dst=dst: e.tensor_copy(dst, self.ptr[:, :].rearrange(
                "p (c v) -> p c v", v=64)), waits=[tt], inc=True)
            evac_tok = b.tdve()
        ones = self.AR[:, vaug_off:vaug_off + 64 * VA_STRIDE].rearrange(
            "p (c v) -> p c v", v=VA_STRIDE)[:, :, 64:65]
        d(lambda e, ones=ones: e.memset(ones, 1.0), inc=True)
        return b.tdve(), b.tpe()

    def flash_head(self, kdup_off, qdup_off, vaug_off, parity, Q0,
                   wait_loads, exp_emit, ctx_sink, war_ctx_tok, war_store,
                   chunks=None):
        b = self.b
        p, d = b.pe, b.dve
        pA, pB = self.pA, self.pB
        if chunks is None:
            chunks = list(range(64))
        n = len(chunks)
        t_exp = [None] * n
        t_av = [None] * n

        def emit_S(i):
            tc = chunks[i]
            half = i % 2
            w = []
            if i < 2:
                w += list(wait_loads)
                if i == 0 and war_ctx_tok is not None:
                    w.append(war_ctx_tok)
            else:
                w.append(t_exp[i - 2])
            for hv in range(2):
                lhsT = self.AR[hv * 64:hv * 64 + 64,
                               kdup_off + tc * 128:kdup_off + (tc + 1) * 128]
                rhs = self.AR[hv * 64:hv * 64 + 64,
                              qdup_off + Q0 + hv * 512:qdup_off + Q0 + (hv + 1) * 512]
                p(lambda e, half=half, hv=hv, lhsT=lhsT, rhs=rhs: e.matmul(
                    pA[:, half * 1024 + hv * 512:half * 1024 + (hv + 1) * 512],
                    lhsT, rhs, start=True, stop=True, tile_position=(hv * 64, 0)),
                    waits=w if hv == 0 else ())
            return b.pe_mark()

        ts_tok = [None] * n
        ts_tok[0] = emit_S(0)
        if n > 1:
            ts_tok[1] = emit_S(1)
        for i in range(n):
            tc = chunks[i]
            half = i % 2
            ubuf = U0 if half == 0 else U1
            wav = t_av[i - 2] if i >= 2 else None
            ta, td = exp_emit(tc, ts_tok[i], wav, ubuf, half)
            t_exp[i] = ta
            if i + 2 < n:
                ts_tok[i + 2] = emit_S(i + 2)
            for hv in range(2):
                lhsT = self.AR[:, vaug_off + tc * VA_STRIDE:
                               vaug_off + tc * VA_STRIDE + 65]
                wv = [ta] + ([td] if td is not None else [])
                p(lambda e, hv=hv, i=i, lhsT=lhsT, ubuf=ubuf: e.matmul(
                    pB[0:65, hv * 512:(hv + 1) * 512], lhsT,
                    self.ar(ubuf, 1024)[:, hv * 512:(hv + 1) * 512],
                    start=(i == 0), stop=(i == n - 1)),
                    waits=wv if hv == 0 else ())
            t_av[i] = b.pe_mark()
        MS = self.MS
        slot = MS[0:65, 1024 + parity * 1024:2048 + parity * 1024]
        wm = [t_av[n - 1]] + ([war_store] if war_store is not None else [])
        d(lambda e, slot=slot: e.tensor_copy(slot[:, 0:1024], pB[0:65, :]),
          waits=wm, inc=True)
        tfree = b.tdve()
        d(lambda e, slot=slot: e.reciprocal(MS[64:65, 0:1024],
                                            slot[64:65, 0:1024]), inc=True)
        prev = b.tdve()
        for half in range(2):
            b.pe(lambda e, half=half: e.matmul(
                self.pb6[0:64, :], self.ONESF[64:65, 0:64],
                MS[64:65, half * 512:(half + 1) * 512], start=True, stop=True),
                waits=[prev])
            tb = b.pe_mark()
            d(lambda e, half=half: e.tensor_copy(
                MS[0:64, half * 512:(half + 1) * 512], self.pb6[0:64, :]),
              waits=[tb], inc=True)
            prev = b.tdve()
        ctxt = slot[0:64, 0:1024]
        d(lambda e, ctxt=ctxt: e.tensor_mul(ctxt, ctxt, MS[0:64, 0:1024]),
          inc=True)
        tctx = b.tdve()
        tok = ctx_sink(ctxt, tctx)
        return tfree, tok, b.tpe()

    def attn_a(self):
        b = self.b
        d, p = b.dve, b.pe
        tok_wq = b.dgp("wq", v3(self.ar(WB0, 8192), 1024),
                       self.wqT.rearrange("(dc p) o -> p dc o", p=128))
        tok_wk = b.dgp("wkv", v3(self.ar(WB1, 2048), 256),
                       self.wkT.rearrange("(dc p) o -> p dc o", p=128),
                       waits=[self.tok_ln1_pe])
        tok_wv = b.dgp("wkv", v3(self.ar(WB1 + 2048, 2048), 256),
                       self.wvT.rearrange("(dc p) o -> p dc o", p=128))

        def sink_q(oc, tile, tok):
            return b.dsp("qa%d" % (oc % 2),
                         self.qa_dram[oc * 128:(oc + 1) * 128, :], tile[:, :],
                         waits=[tok])

        def sink_k(oc, tile, tok):
            return b.dsp("aga", self.ag_a_in[oc * 128:(oc + 1) * 128, :],
                         tile[:, :], waits=[tok])

        def sink_v(oc, tile, tok):
            return b.dsp("aga", self.ag_a_in[256 + oc * 128:256 + (oc + 1) * 128, :],
                         tile[:, :], waits=[tok])

        rope_k = (self.PERMA[:, :], OFF_TAB, OFF_TAB + 2048)
        rope_q = (self.PERMA[:, :], OFF_TAB + 4096, OFF_TAB + 6144)
        self.proj(WB0, 1024, [(oc, self.AB[:, oc:oc + 1]) for oc in range(8)],
                  rope_q, sink_q,
                  extra_wait=[tok_wq, self.tok_xn1, self.tok_tab])
        gq0, gq1 = b.grp("qa0"), b.grp("qa1")
        self.proj(WB1, 256, [(oc, self.AB[:, 8 + oc:9 + oc]) for oc in range(2)],
                  rope_k, sink_k, extra_wait=[tok_wv],
                  stg_war=[(gq0, gq0.n), (gq1, gq1.n)])
        ga = b.grp("aga")
        self.proj(WB1 + 2048, 256,
                  [(oc, self.AB[:, 10 + oc:11 + oc]) for oc in range(2)],
                  None, sink_v, extra_wait=[], stg_war=[(ga, ga.n)])
        g_aga = b.grp("aga")
        tok_contrib = (g_aga, g_aga.n)
        g_qa0, g_qa1 = b.grp("qa0"), b.grp("qa1")
        tok_qa = [(g_qa0, g_qa0.n), (g_qa1, g_qa1.n)]
        self.tap_dram("qa", self.qa_dram, waits=tok_qa)
        self.tap_dram("aga", self.ag_a_in, waits=[tok_contrib])
        tok_aga = b.coll("a", [[0, 1, 2, 3], [4, 5, 6, 7]],
                         [self.ag_a_in], [self.ag_a_out],
                         waits=[tok_contrib])
        tok_wo = b.dgp("wo", v3(self.ar(A_WO, 8192), 1024),
                       self.woT.rearrange("(dc p) o -> p dc o", p=128))

        def exp_plain(tc, wpe, wav, ubuf, half):
            w = [wpe] + ([wav] if wav is not None else [])
            b.act(lambda e, ubuf=ubuf, half=half: e.activation(
                self.ar(ubuf, 1024), self.pA[:, half * 1024:(half + 1) * 1024],
                AF.Exp, bias=-BEXP, scale=1.0), waits=w, inc=True, drain=False)
            return b.tact(), None

        kload_war = None
        ctx_war = None
        ctx_tok = None
        self.ctx_store = [None, None]
        src = self.ag_a_out.rearrange("(r c) t -> c r t", c=512)
        for kv in range(4):
            w_k = [tok_aga]
            if kload_war is not None:
                w_k.append(kload_war)
            gkv = "kva%d" % (kv % 2)
            b.dsp(gkv, v3(self.ar(A_KDUP, 8192, rows=64), 2048),
                  src[kv * 64:(kv + 1) * 64, :, :], waits=w_k)
            b.dsp(gkv, v3(self.AR[64:128, A_KDUP:A_KDUP + 8192], 2048),
                  src[kv * 64:(kv + 1) * 64, :, :])
            tok_kv = b.dsp(gkv, v3(self.ar(A_VFM, 8192, rows=64), 2048),
                           src[256 + kv * 64:256 + (kv + 1) * 64, :, :])
            tok_vaug, _ = self.vaug_build(A_VFM, A_VAUG, tok_kv)
            for hi in range(4):
                h = kv * 4 + hi
                wq1 = [kload_war] if kload_war is not None else []
                gqd = "qda%d" % (h % 2)
                b.dsp(gqd, self.ar(A_QDUP, 2048, rows=64),
                      self.qa_dram[h * 64:(h + 1) * 64, :],
                      waits=wq1 + tok_qa)
                tq = b.dsp(gqd, self.AR[64:128, A_QDUP:A_QDUP + 2048],
                           self.qa_dram[h * 64:(h + 1) * 64, :])
                for qt in range(2):
                    wl = [tok_kv, tq, tok_vaug, self.tok_consts]
                    par = (2 * h + qt) % 2

                    def sink(ctxt, tok, h=h, qt=qt, par=par):
                        return b.dgp(
                            "ctx%d" % par,
                            self.ctxa_dram[h * 64:(h + 1) * 64,
                                           qt * 1024:(qt + 1) * 1024],
                            ctxt, waits=[tok])

                    ctx_war, ctx_tok, kload_war = self.flash_head(
                        A_KDUP, A_QDUP, A_VAUG, par, qt * 1024,
                        wl, exp_plain, sink, ctx_war, self.ctx_store[par])
                    self.ctx_store[par] = ctx_tok
        g_c0, g_c1 = b.grp("ctx0"), b.grp("ctx1")
        tok_ctxall = [(g_c0, g_c0.n), (g_c1, g_c1.n)]
        tok_ctx = b.dsp("ctxld", v3(self.AR[:, OFF_XN:OFF_XN + 16384], 2048),
                        self.ctxa_dram.rearrange("(pt p) t -> p pt t", p=128),
                        waits=tok_ctxall + [kload_war])
        self.tap_dram("ctxa", self.ctxa_dram, waits=tok_ctxall)
        pa_free = None
        for oc in range(8):
            w = [tok_wo, tok_ctx]
            if pa_free is not None:
                w.append(pa_free)
            for ch in range(4):
                for dc in range(8):
                    lhsT = self.ar(A_WO + dc * 1024 + oc * 128, 128)
                    p(lambda e, ch=ch, dc=dc, lhsT=lhsT: e.matmul(
                        self.pA[:, ch * 512:(ch + 1) * 512], lhsT,
                        self.xnp(dc)[:, ch * 512:(ch + 1) * 512],
                        start=(dc == 0), stop=(dc == 7)),
                        waits=w if (ch == 0 and dc == 0) else ())
            tmm = b.pe_mark()
            gt2 = b.grp("tap2")
            d(lambda e, oc=oc: e.scalar_tensor_tensor(
                self.R[:, oc * 2048:(oc + 1) * 2048], self.pA[:, :],
                self.AB[:, 12 + oc:13 + oc],
                self.R[:, oc * 2048:(oc + 1) * 2048],
                op0=ALU.add, op1=ALU.add),
              waits=[tmm] + ([(gt2, gt2.n)] if (oc == 0 and self.dbg) else []),
              inc=True)
            pa_free = b.tdve()
        self.tok_oproj_pe = b.tpe()
        return b.tdve()

    def routing(self, tok_xn, fi, qstg_war=()):
        b = self.b
        MS, pb6, pB = self.MS, self.pb6, self.pB
        d, p = b.dve, b.pe
        sM = lambda a, bb: MS[:, a:bb]
        sv = self.MS[:, 0:16].rearrange("p (a b) -> p a b", b=2)
        t1v = self.MS[:, 32:40].rearrange("p (a b) -> p a b", b=1)
        t_exp_prev = None
        tr_evac = None
        qstg_war = list(qstg_war)
        grps = [b.grp(n) for n in ("ix0", "ix1", "ws0", "ws1",
                                    "xg0", "xg1", "xg2", "xg3")]
        scg = b.grp("sc")
        wzi = [self.tok_consts] + ([(g, g.n) for g in grps] + [(scg, scg.n)]
                                   if fi == 2 else [])
        b.dsp("zi", self.invmap_d.rearrange("(a bb) o -> a (bb o)", a=128),
              self.ZI32[:, :], waits=wzi)
        tok_zi = b.dsp("zi",
                       self.wcomp_d.rearrange("(a bb) o -> a (bb o)", a=128),
                       self.ZF32[:, :])
        b.pool(lambda e: e.iota(MS[0:1, 3600:3616], [[1, 16]], base=0,
                                channel_multiplier=0,
                                allow_small_or_imprecise_dtypes=True),
               waits=[tok_xn], inc=True)
        tbp = b.tpool()
        d(lambda e: e.tensor_scalar(MS[0:1, 3600:3616], MS[0:1, 3600:3616],
                                    float(CAP), None, op0=ALU.mult), waits=[tbp])
        d(lambda e: e.tensor_scalar(MS[0:1, 3584:3600], self.ONESF[0:1, 0:16],
                                    0.0, None, op0=ALU.mult), inc=True)
        tpb = b.tdve()
        p(lambda e: e.matmul(pB[:, 48:64], self.ONESF[0:1, 0:128],
                             MS[0:1, 3600:3616], start=True, stop=True),
          waits=[tpb])
        tec = b.pe_mark()
        d(lambda e: e.tensor_copy(MS[:, 3616:3632], pB[:, 48:64]), waits=[tec],
          inc=True)
        tpb = b.tdve()
        for tc in range(16):
            w = [self.tok_wg, tok_xn]
            if t_exp_prev is not None:
                w.append(t_exp_prev)
            for dc in range(8):
                p(lambda e, tc=tc, dc=dc: e.matmul(
                    pb6[:, 0:16], self.xnp(dc)[:, tc * 128:(tc + 1) * 128],
                    self.WGT[:, dc * 16:(dc + 1) * 16],
                    start=(dc == 0), stop=(dc == 7)),
                    waits=w if dc == 0 else ())
            tmm = b.pe_mark()
            b.act(lambda e: e.activation(sM(16, 32), pb6[:, 0:16], AF.Exp,
                                         bias=0.0, scale=-1.0),
                  waits=[tmm], inc=True)
            t_exp_prev = b.tact()
            d(lambda e: e.tensor_scalar(sM(16, 32), sM(16, 32), 1.0, None,
                                        op0=ALU.add), waits=[t_exp_prev])
            d(lambda e: e.reciprocal(sM(0, 16), sM(16, 32)))
            d(lambda e: e.tensor_max(t1v[:, :, 0:1], sv[:, :, 0:1], sv[:, :, 1:2]))
            gv = self.MS[:, 32:40].rearrange("p (a b) -> p a b", b=2)
            d(lambda e: e.tensor_max(
                self.MS[:, 40:44].rearrange("p (a b) -> p a b", b=1),
                gv[:, :, 0:1], gv[:, :, 1:2]))
            d(lambda e: e.reduce_max(sM(44, 45), sM(40, 44), axis=AXX))
            d(lambda e: e.tensor_scalar(sM(48, 52), sM(40, 44), MS[:, 44:45], None,
                                        op0=ALU.is_ge))
            d(lambda e: e.scalar_tensor_tensor(sM(52, 56), sM(48, 52), -2.0,
                                               sM(40, 44), op0=ALU.mult, op1=ALU.add))
            d(lambda e: e.reduce_max(sM(44, 45), sM(52, 56), axis=AXX))
            d(lambda e: e.tensor_scalar(sM(112, 116), sM(52, 56), MS[:, 44:45], None,
                                        op0=ALU.is_ge))
            d(lambda e: e.tensor_add(sM(56, 60), sM(48, 52), sM(112, 116)))
            for gi in range(4):
                d(lambda e, gi=gi: e.tensor_scalar(
                    sM(64 + 4 * gi, 68 + 4 * gi), self.CONE[:, 0:4],
                    MS[:, 56 + gi:57 + gi], None, op0=ALU.mult))
            d(lambda e: e.tensor_scalar(sM(80, 96), sM(0, 16), 1.0, None, op0=ALU.add))
            d(lambda e: e.tensor_mul(sM(80, 96), sM(80, 96), sM(64, 80)))
            d(lambda e: e.tensor_scalar(sM(80, 96), sM(80, 96), -1.0, None,
                                        op0=ALU.add))
            for k in range(4):
                d(lambda e: e.reduce_max(sM(44, 45), sM(80, 96), axis=AXX))
                d(lambda e: e.tensor_scalar(sM(112, 128), sM(80, 96), MS[:, 44:45],
                                            None, op0=ALU.is_ge))
                if k == 0:
                    d(lambda e: e.tensor_copy(sM(96, 112), sM(112, 128)))
                else:
                    d(lambda e: e.tensor_add(sM(96, 112), sM(96, 112), sM(112, 128)))
                if k < 3:
                    d(lambda e: e.scalar_tensor_tensor(
                        sM(80, 96), sM(112, 128), -3.0, sM(80, 96),
                        op0=ALU.mult, op1=ALU.add))
            d(lambda e: e.tensor_mul(sM(112, 128), sM(0, 16), sM(96, 112)))
            d(lambda e: e.reduce_sum(sM(44, 45), sM(112, 128), axis=AXX))
            d(lambda e: e.reciprocal(sM(45, 46), sM(44, 45)))
            d(lambda e, tc=tc: e.tensor_scalar(
                sM(256 + 16 * tc, 272 + 16 * tc), sM(112, 128),
                MS[:, 45:46], 2.5, op0=ALU.mult, op1=ALU.mult))
            wq = qstg_war if tc == 0 else []
            d(lambda e, tc=tc: e.tensor_copy(
                self.QSTG[:, tc * 16:(tc + 1) * 16],
                sM(256 + 16 * tc, 272 + 16 * tc)), waits=wq, inc=True)
            tstg = b.tdve()
            selb = self.QSTG[:, 256 + 16 * tc:272 + 16 * tc]
            d(lambda e, selb=selb: e.tensor_copy(selb, sM(96, 112)), inc=True)
            tselb = b.tdve()
            d(lambda e: e.tensor_copy(sM(512, 528), sM(96, 112)))
            for (ksh, sa, da_) in ((1, 512, 528), (2, 528, 512),
                                   (4, 512, 528), (8, 528, 512)):
                d(lambda e, k=ksh, sa=sa, da_=da_: e.tensor_copy(
                    MS[:, da_:da_ + k], MS[:, sa:sa + k]))
                d(lambda e, k=ksh, sa=sa, da_=da_: e.tensor_add(
                    MS[:, da_ + k:da_ + 16], MS[:, sa + k:sa + 16],
                    MS[:, sa:sa + 16 - k]))
            d(lambda e: e.tensor_sub(sM(544, 560), sM(512, 528), sM(96, 112)),
              inc=True)
            texcl = b.tdve()
            p(lambda e, selb=selb: e.matmul(pB[:, 0:16], self.LTRI[:, :], selb,
                                            start=True, stop=True),
              waits=[tselb, tpb])
            p(lambda e: e.matmul(pB[:, 16:32], self.ONESF[0:1, 0:128],
                                 MS[0:1, 3584:3600], start=True, stop=True))
            p(lambda e, selb=selb: e.matmul(pB[0:1, 32:48], self.ONESB[:, 0:1],
                                            selb, start=True, stop=True))
            trk = b.pe_mark()
            d(lambda e: e.tensor_copy(sM(576, 592), pB[:, 0:16]), waits=[trk])
            d(lambda e: e.tensor_add(sM(576, 592), sM(576, 592), pB[:, 16:32]))
            d(lambda e: e.tensor_scalar(sM(576, 592), sM(576, 592),
                                        float(CAP - 1), None, op0=ALU.min))
            d(lambda e: e.tensor_add(MS[0:1, 3584:3600], MS[0:1, 3584:3600],
                                     pB[0:1, 32:48]), inc=True)
            tpb = b.tdve()
            d(lambda e: e.tensor_scalar(sM(600, 616), sM(576, 592),
                                        1.0 / 128.0, -63.5 / 128.0,
                                        op0=ALU.mult, op1=ALU.add))
            d(lambda e: e.tensor_scalar(sM(600, 616), sM(600, 616), MAGIC, None,
                                        op0=ALU.add))
            d(lambda e: e.tensor_scalar(sM(600, 616), sM(600, 616), -MAGIC, None,
                                        op0=ALU.add))
            d(lambda e: e.scalar_tensor_tensor(sM(616, 632), sM(600, 616),
                                               -128.0, sM(576, 592),
                                               op0=ALU.mult, op1=ALU.add))
            d(lambda e: e.scalar_tensor_tensor(sM(576, 592), sM(616, 632),
                                               float(NCH), sM(600, 616),
                                               op0=ALU.mult, op1=ALU.add))
            d(lambda e: e.tensor_add(sM(576, 592), sM(576, 592),
                                     MS[:, 3616:3632]))
            for s in range(4):
                d(lambda e, s=s: e.tensor_scalar(sM(560, 576), sM(544, 560),
                                                 float(s), None,
                                                 op0=ALU.is_equal),
                  waits=[texcl] if s == 0 else ())
                d(lambda e: e.tensor_mul(sM(560, 576), sM(560, 576), sM(96, 112)))
                d(lambda e: e.tensor_mul(sM(648, 664), sM(560, 576), sM(576, 592)))
                d(lambda e, s=s: e.reduce_sum(MS[:, 592 + s:593 + s], sM(648, 664),
                                              axis=AXX))
                d(lambda e, tc=tc: e.tensor_mul(sM(648, 664), sM(560, 576),
                                                sM(256 + 16 * tc, 272 + 16 * tc)))
                d(lambda e, s=s, tc=tc: e.reduce_sum(
                    MS[:, 3200 + 4 * tc + s:3201 + 4 * tc + s], sM(648, 664),
                    axis=AXX))
            d(lambda e, tc=tc: e.tensor_scalar(
                self.SLOTI[:, 4 * tc:4 * tc + 4], MS[:, 592:596], 1.0, None,
                op0=ALU.mult), inc=True)
            tslot = b.tdve()
            for s in range(4):
                w = [tslot] if s == 0 else []
                if tc == 0 and s == 0:
                    ws0, ws1 = b.grp("ws0"), b.grp("ws1")
                    w = [tslot, self.tok_tokid, tok_zi,
                         (ws0, ws0.n), (ws1, ws1.n)]
                b.dsis("sc", self.invmap_d,
                       self.SLOTI[:, 4 * tc + s:4 * tc + s + 1],
                       self.TOKID[:, tc:tc + 1], waits=w)
                b.dsis("sc", self.wcomp_d,
                       self.SLOTI[:, 4 * tc + s:4 * tc + s + 1],
                       MS[:, 3200 + 4 * tc + s:3201 + 4 * tc + s])
            wtr = [tstg, self.tok_consts]
            if tc == 0:
                wtr.append(self.tok_pl_ev)
            if tr_evac is not None:
                wtr.append(tr_evac)
            p(lambda e, tc=tc: e.transpose(
                self.ptr[0:16, (tc % 8) * 128:((tc % 8) + 1) * 128],
                self.QSTG[:, tc * 16:(tc + 1) * 16], self.IDENT[:, :]),
                waits=wtr)
            ttr = b.pe_mark()
            if tc % 8 == 7:
                d(lambda e, tc=tc: e.tensor_copy(
                    self.WET[0:16, (tc // 8) * 1024:(tc // 8 + 1) * 1024],
                    self.ptr[0:16, 0:1024]), waits=[ttr], inc=True)
                tr_evac = b.tdve()
        self.tap_dram("wfull", self.MS[:, 256:512], waits=[b.tdve()])
        self.tok_wet = tr_evac

    def payload(self, fi, tok_xn, ptr_war, qstg_war=()):
        b = self.b
        p = b.pe
        xstg = self.QSTG[:, 1024:2048]
        plg = b.grp("pl")
        pl_tok = None
        ev_prev = None
        for tc in range(16):
            for j in range(8):
                w = ()
                if j == 0:
                    w = [tok_xn] + ptr_war if tc == 0 else [ev_prev]
                p(lambda e, tc=tc, j=j: e.transpose(
                    self.ptr[:, j * 128:(j + 1) * 128],
                    self.xnp(j)[:, tc * 128:(tc + 1) * 128], self.IDENT[:, :]),
                  waits=w)
            tmm = b.pe_mark()
            wev = [tmm]
            if tc == 0:
                wev += list(qstg_war)
            if pl_tok is not None:
                wev.append(pl_tok)
            b.act(lambda e, xstg=xstg: e.activation(
                xstg, self.ptr[:, :], AF.Identity, bias=0.0, scale=1.0),
                waits=wev, inc=True, drain=False)
            ev_prev = b.tact()
            pl_tok = b.dsp("pl", self.xtok_d[tc * 128:(tc + 1) * 128, :], xstg,
                           waits=[ev_prev])
        self.tok_pl = pl_tok
        self.tok_pl_ev = ev_prev

    def ffn(self, tok_xn, fi, qstg_war=()):
        b = self.b
        MS, pB, pA = self.MS, self.pB, self.pA
        d, p = b.dve, b.pe
        ptr_war = [self.ptr_last] if self.ptr_last is not None else []
        self.payload(fi, tok_xn, ptr_war, qstg_war)
        self.routing(tok_xn, fi, qstg_war)
        wsrc = lambda w, e_: w[e_].rearrange("(dc p) i -> p dc i", p=128)
        g13 = lambda e_: "w13_%d" % (e_ % 2)
        g2 = lambda e_: "w2_%d" % (e_ % 2)
        war = [b.tpe()]
        b.dgp(g13(0), v3(self.ar(WB0, 8192), 1024), wsrc(self.w1T, 0), waits=war)
        tok_w13 = b.dgp(g13(0), v3(self.ar(WB1, 8192), 1024), wsrc(self.w3T, 0))
        tok_w2 = b.dgp(g2(0), v3(self.ar(WB2, 8192), 1024), wsrc(self.w2T, 0))
        gt = b.grp("tap")
        tap_war = [(gt, gt.n)] if self.dbg else []
        gt2 = b.grp("tap2")
        rtap_war = [(gt2, gt2.n)] if self.dbg else []
        first_radd = [True]
        scg = b.grp("sc")
        plg = b.grp("pl")
        ysg = [b.grp("ys0"), b.grp("ys1")]
        h_free = [None, None]
        silu_free = [None, None]
        h_last = [None, None]
        hwar = [None, None]
        pb_free = [None, None]
        ystg_war = [None, None]
        ring_pl = [False, True]
        ystg_ring = [0]
        tr_done = [None, None, None, None]
        tpe_entry = b.tpe()
        ptr_free = [self.tok_wet]
        winready = {}
        awin_pe = {}
        idx_tok = {}
        wst_tok = {}
        gather_last = {}
        prev_b_dve = [None]

        def emit_idx_we(e_):
            par = e_ % 2
            xgs = [b.grp("xg%d" % i) for i in range(4)]
            w = []
            if e_ == 0:
                w.append((scg, scg.n))
            else:
                w += [(g, g.n) for g in xgs]
            if prev_b_dve[0] is not None:
                w.append(prev_b_dve[0])
            idx_tok[e_] = b.dsp(
                "ix%d" % par, self.IDXS[:, par * NCH:(par + 1) * NCH],
                self.invmap_d[e_ * CAP:(e_ + 1) * CAP, :].rearrange(
                    "(a bb) o -> a (bb o)", bb=NCH), waits=w)
            wst_tok[e_] = b.dsp(
                "ws%d" % par, self.WSTR[:, par * NCH:(par + 1) * NCH],
                self.wcomp_d[e_ * CAP:(e_ + 1) * CAP, :].rearrange(
                    "(a bb) o -> a (bb o)", bb=NCH))

        gtok = {}

        def ring_slot(rc):
            if rc < 2:
                return self.ar(OFF_SC + 1024 + rc * 1024, 1024)
            return self.QRAW[:, (rc - 2) * 1024:(rc - 1) * 1024]

        def emit_gather(e_, c8):
            par = e_ % 2
            rc = c8 % 4
            w = []
            if c8 == 0:
                w.append(idx_tok[e_])
                if e_ == 0:
                    w.append(self.tok_pl)
            if e_ == 0 and c8 in (2, 3):
                w.append(tpe_entry)
            if tr_done[rc] is not None:
                w.append(tr_done[rc])
            gtok[(e_, c8)] = b.dsi(
                "xg%d" % rc, ring_slot(rc),
                self.xtok_d,
                self.IDXS[:, par * NCH + c8:par * NCH + c8 + 1], waits=w)

        def emit_tr_chunk(e_, c8, gtok):
            wi = CHWIN[c8]
            gw = e_ * NWIN + wi
            wslot = gw % 3
            cc = c8 - WINS[wi][0] // 128
            rc = c8 % 4
            src = ring_slot(rc)
            for j in range(8):
                w = [gtok, ptr_free[0]] if j == 0 else ()
                p(lambda e, j=j, src=src: e.transpose(
                    self.ptr[:, j * 128:(j + 1) * 128],
                    src[:, j * 128:(j + 1) * 128], self.IDENT[:, :]), waits=w)
            tmm = b.pe_mark()
            tr_done[rc] = tmm
            dst = self.ar(OFF_XN + wslot * 4096, 4096).rearrange(
                "p (a bb) -> p a bb", bb=512)[:, :, cc * 128:(cc + 1) * 128]
            wev = [tmm]
            if cc == 0 and (gw - 3) in awin_pe:
                wev.append(awin_pe[gw - 3])
            b.act(lambda e, dst=dst: e.activation(
                dst, v3(self.ptr[:, :], 128), AF.Identity, bias=0.0, scale=1.0),
                waits=wev, inc=True, drain=False)
            ptr_free[0] = b.tact()
            if c8 in (3, 7, NCH - 1):
                winready[gw] = b.tact()

        def a_window(e_, wi):
            ew = e_ * 8
            if True:
                ws_, wd = WINS[wi]
                gw = e_ * NWIN + wi
                hh = gw % 2
                for ic in range(8):
                    par = ic % 2
                    base = par * 1024
                    w = [tok_w13, winready[gw]] + tap_war
                    if h_free[par] is not None:
                        w.append(h_free[par])
                    if ic == 0 and hwar[hh] is not None:
                        w.append(hwar[hh])
                    for m3, woff in ((0, WB0), (1, WB1)):
                        for dc in range(8):
                            lhsT = self.ar(woff + dc * 1024 + ic * 128, 128)
                            rhs = self.ar(OFF_XN + (gw % 3) * 4096 + dc * 512, wd)
                            p(lambda e, base=base, m3=m3, lhsT=lhsT, rhs=rhs,
                              dc=dc, wd=wd: e.matmul(
                                  self.pA[:, base + m3 * 512:base + m3 * 512 + wd],
                                  lhsT, rhs, start=(dc == 0), stop=(dc == 7)),
                              waits=w if (m3 == 0 and dc == 0) else ())
                    tA = b.pe_mark()
                    hdst = self.ar(OFF_TAB + ic * 1024 + hh * 512, wd)
                    silu_bf = self.ar(OFF_SC + par * 512, wd)
                    ws2 = [tA]
                    if silu_free[par] is not None:
                        ws2.append(silu_free[par])
                    b.act(lambda e, ic=ic, ew=ew, base=base, silu_bf=silu_bf,
                          wd=wd: e.activation(
                              silu_bf, self.pA[:, base:base + wd], AF.Silu,
                              bias=self.B1[:, ew + ic:ew + ic + 1], scale=1.0),
                          waits=ws2, drain=False)
                    b.act(lambda e, ic=ic, ew=ew, base=base, hdst=hdst,
                          wd=wd: e.activation(
                              hdst, self.pA[:, base + 512:base + 512 + wd],
                              AF.Identity,
                              bias=self.B3[:, ew + ic:ew + ic + 1], scale=1.0),
                          inc=True, drain=False)
                    tsl = b.tact()
                    d(lambda e, hdst=hdst, silu_bf=silu_bf: e.tensor_mul(
                        hdst, hdst, silu_bf), waits=[tsl], inc=True)
                    h_free[par] = tsl
                    silu_free[par] = b.tdve()
                    h_last[hh] = b.tdve()
                awin_pe[gw] = b.tpe()

        def b_phase(e_, wi, pend_tr):
            gw = e_ * NWIN + wi
            hh = gw % 2
            ws_, wd = WINS[wi]
            for cc in range(wd // 128):
                c8 = ws_ // 128 + cc
                ring = ystg_ring[0] % 2
                for dh in (0, 1):
                    w = [h_last[hh], tok_w2]
                    if pb_free[dh] is not None:
                        w.append(pb_free[dh])
                    for k in range(8):
                        lhsT = self.ar(OFF_TAB + k * 1024 + hh * 512 + cc * 128,
                                       128)
                        rhs = self.ar(WB2 + k * 1024 + dh * 512, 512)
                        p(lambda e, dh=dh, lhsT=lhsT, rhs=rhs, k=k: e.matmul(
                            self.pB[:, dh * 512:(dh + 1) * 512], lhsT, rhs,
                            start=(k == 0), stop=(k == 7)),
                          waits=w if k == 0 else ())
                    tB = b.pe_mark()
                    wev = [tB]
                    if ystg_war[ring] is not None:
                        wev.append(ystg_war[ring])
                    if ring_pl[ring]:
                        wev.append((plg, plg.n))
                        ring_pl[ring] = False
                    if wi == 0 and cc == 0 and dh == 0:
                        wev.append(wst_tok[e_])
                    wecol = self.WSTR[:, (e_ % 2) * NCH + c8:
                                      (e_ % 2) * NCH + c8 + 1]
                    d(lambda e, ring=ring, dh=dh, wecol=wecol: e.tensor_scalar(
                        self.QSTG[:, ring * 1024 + dh * 512:
                                  ring * 1024 + dh * 512 + 512],
                        self.pB[:, dh * 512:(dh + 1) * 512], wecol, None,
                        op0=ALU.mult), waits=wev, inc=True)
                    pb_free[dh] = b.tdve()
                ydst = self.ycomp_d[e_ * CAP:(e_ + 1) * CAP, :].rearrange(
                    "(a bb) o -> a bb o", bb=NCH)[:, c8:c8 + 1, :]
                ystg_war[ring] = b.dsp(
                    "ys%d" % ring, ydst,
                    v3(self.QSTG[:, ring * 1024:ring * 1024 + 1024], 1024),
                    waits=[pb_free[1] if pb_free[1] is not None else pb_free[0]])
                ystg_ring[0] += 1
                if pend_tr:
                    en, c8n = pend_tr.pop(0)
                    emit_tr_chunk(en, c8n, gtok[(en, c8n)])
                    if c8n + 4 < NCH:
                        emit_gather(en, c8n + 4)
            hwar[hh] = b.tpe()

        emit_idx_we(0)
        for c8 in range(4):
            emit_gather(0, c8)
        for c8 in range(NCH):
            emit_tr_chunk(0, c8, gtok[(0, c8)])
            if c8 + 4 < NCH:
                emit_gather(0, c8 + 4)
        for e_ in range(E):
            a_window(e_, 0)
            pend_tr = []
            if e_ + 1 < E:
                emit_idx_we(e_ + 1)
                for c8 in range(4):
                    emit_gather(e_ + 1, c8)
                pend_tr = [(e_ + 1, c8) for c8 in range(NCH)]
            for wi in range(NWIN):
                if wi + 1 < NWIN:
                    a_window(e_, wi + 1)
                if wi == 1 and e_ + 1 < E:
                    wlast = [awin_pe[e_ * NWIN + 2]]
                    b.dgp(g13(e_ + 1), v3(self.ar(WB0, 8192), 1024),
                          wsrc(self.w1T, e_ + 1), waits=wlast)
                    tok_w13_next = b.dgp(g13(e_ + 1),
                                         v3(self.ar(WB1, 8192), 1024),
                                         wsrc(self.w3T, e_ + 1))
                b_phase(e_, wi, pend_tr)
            assert not pend_tr
            prev_b_dve[0] = b.tdve()
            if e_ + 1 < E:
                tok_w2 = b.dgp(g2(e_ + 1), v3(self.ar(WB2, 8192), 1024),
                               wsrc(self.w2T, e_ + 1), waits=[b.tpe()])
                tok_w13 = tok_w13_next
        fold_free = [None, None]
        for dc in range(8):
            for t2 in range(4):
                par = (dc * 4 + t2) % 2
                w = []
                if dc == 0 and t2 == 0:
                    w = [self.tok_wet, self.tok_wg, pb_free[0], pb_free[1]]
                if fold_free[par] is not None:
                    w = list(w) + [fold_free[par]]
                p(lambda e, dc=dc, t2=t2, par=par: e.matmul(
                    self.pB[:, par * 512:(par + 1) * 512],
                    self.B2S[0:16, dc * 128:(dc + 1) * 128],
                    self.WET[0:16, t2 * 512:(t2 + 1) * 512],
                    start=True, stop=True), waits=w)
                tf = b.pe_mark()
                wadd = [tf] + (rtap_war if first_radd[0] else [])
                first_radd[0] = False
                d(lambda e, dc=dc, t2=t2, par=par: e.tensor_add(
                    self.R[:, dc * 2048 + t2 * 512:dc * 2048 + (t2 + 1) * 512],
                    self.R[:, dc * 2048 + t2 * 512:dc * 2048 + (t2 + 1) * 512],
                    self.pB[:, par * 512:(par + 1) * 512]), waits=wadd, inc=True)
                fold_free[par] = b.tdve()
        awin15 = b.tpe()
        ygg = [b.grp("yg0"), b.grp("yg1")]
        ring_war = [None, None]
        radd_prev = None
        for tc in range(16):
            par = tc % 2
            w0 = []
            if tc == 0:
                w0 = [(ysg[0], ysg[0].n), (ysg[1], ysg[1].n), awin15]
            if ring_war[par] is not None:
                w0.append(ring_war[par])
            yt = None
            for s in range(4):
                yt = b.dsi("yg%d" % par,
                           self.ar(OFF_XN + par * 4096 + s * 1024, 1024),
                           self.ycomp_d,
                           self.SLOTI[:, 4 * tc + s:4 * tc + s + 1],
                           waits=w0 if s == 0 else ())
            ya = lambda s, par=par: self.ar(OFF_XN + par * 4096 + s * 1024, 1024)
            d(lambda e, ya=ya: e.tensor_add(self.ar(OFF_SC + 1024, 1024),
                                            ya(0), ya(1)), waits=[yt])
            d(lambda e, ya=ya: e.tensor_add(self.ar(OFF_SC + 2048, 1024),
                                            ya(2), ya(3)))
            d(lambda e, par=par: e.tensor_add(
                self.QSTG[:, par * 1024:(par + 1) * 1024],
                self.ar(OFF_SC + 1024, 1024), self.ar(OFF_SC + 2048, 1024)),
              inc=True)
            ysum_t = b.tdve()
            ring_war[par] = ysum_t
            for j in range(8):
                w = [ysum_t, ptr_free[0]] if j == 0 else ()
                p(lambda e, par=par, j=j: e.transpose(
                    self.ptr[:, j * 128:(j + 1) * 128],
                    self.QSTG[:, par * 1024 + j * 128:par * 1024 + (j + 1) * 128],
                    self.IDENT[:, :]), waits=w)
            tmm = b.pe_mark()
            for j in range(8):
                d(lambda e, j=j, tc=tc: e.tensor_add(
                    self.R[:, j * 2048 + tc * 128:j * 2048 + (tc + 1) * 128],
                    self.R[:, j * 2048 + tc * 128:j * 2048 + (tc + 1) * 128],
                    self.ptr[:, j * 128:(j + 1) * 128]),
                  waits=[tmm] if j == 0 else (), inc=(j == 7))
            radd_prev = b.tdve()
            ptr_free[0] = radd_prev
        self.tok_ffn_pe = b.tpe()
        self.ptr_last = radd_prev
        return b.tdve()

    def attn_b(self, sub=3):
        b = self.b
        MS = self.MS
        d, p = b.dve, b.pe
        tok_q = b.dgp("qkvq", v3(self.ar(WB0, 8192), 1024),
                      self.qkvT[:, 0:1024].rearrange("(dc p) o -> p dc o", p=128),
                      waits=[b.tpe()])
        tok_k = b.dgp("qkvk", v3(self.ar(WB1, 8192), 1024),
                      self.qkvT[:, 1024:2048].rearrange("(dc p) o -> p dc o", p=128),
                      waits=[self.last_ln_pe])

        def mk_sink(base):
            def sink(oc, tile, tok):
                h1, h2 = 2 * oc, 2 * oc + 1
                r1 = (h1 % 4) * 768 + base + (h1 // 4) * 64
                r2 = (h2 % 4) * 768 + base + (h2 // 4) * 64
                b.dsp("agb", self.ag_b_in[r1:r1 + 64, :], tile[0:64, :],
                      waits=[tok])
                return b.dsp("agb", self.ag_b_in[r2:r2 + 64, :], tile[64:128, :])
            return sink

        rope_q = (self.PERMB[:, :], OFF_TAB + 4096, OFF_TAB + 6144)
        rope_k = (self.PERMB[:, :], OFF_TAB, OFF_TAB + 2048)
        tq_pe = self.proj(WB0, 1024, [(oc, None) for oc in range(8)], rope_q,
                          mk_sink(0),
                          extra_wait=[tok_q, self.tok_xn3, self.tok_tab])
        tok_v = b.dgp("qkvw2", v3(self.ar(WB0, 8192), 1024),
                      self.qkvT[:, 2048:3072].rearrange("(dc p) o -> p dc o", p=128),
                      waits=[tq_pe])
        gab = b.grp("agb")
        self.proj(WB1, 1024, [(oc, None) for oc in range(8)], rope_k,
                  mk_sink(256), extra_wait=[tok_k],
                  stg_war=[(gab, gab.n)])
        self.proj(WB0, 1024, [(oc, None) for oc in range(8)], None,
                  mk_sink(512), extra_wait=[tok_v], stg_war=[(gab, gab.n)])
        g_agb = b.grp("agb")
        tok_contrib = (g_agb, g_agb.n)
        self.tap_dram("agb", self.ag_b_in, waits=[tok_contrib])
        tok_agb = b.coll("b", [[0, 1, 2, 3], [4, 5, 6, 7]],
                         [self.ag_b_in], [self.ag_b_out], waits=[tok_contrib])
        if sub < 2:
            self.tok_oprojb_pe = b.tpe()
            return b.tdve()
        EI = self.ar(B_EI, 1024)
        kload_war = None
        ctx_war = None
        self.ctx_store = [None, None]
        it = 0
        for hh in range(4):
            wk = [tok_agb]
            if kload_war is not None:
                wk.append(kload_war)
            ghb = "hb%d" % (hh % 2)
            b.dgi(ghb, v3(self.AR[:, B_KDUP:B_KDUP + 8192], 2048)[:, 0:1, :][:, 0, :]
                  if False else self.AR[:, B_KDUP + 0 * 2048:B_KDUP + 2048],
                  self.ag_b_out, self.IDX[:, hh * 4:hh * 4 + 1], waits=wk)
            for r in range(1, 4):
                b.dgi(ghb, self.AR[:, B_KDUP + r * 2048:B_KDUP + (r + 1) * 2048],
                      self.ag_b_out, self.IDX[:, hh * 4 + r:hh * 4 + r + 1])
            for r in range(4):
                b.dgi(ghb, self.AR[:, B_QDUP + r * 2048:B_QDUP + (r + 1) * 2048],
                      self.ag_b_out,
                      self.IDX[:, 16 + hh * 4 + r:17 + hh * 4 + r])
            for r in range(4):
                tok_load = b.dgi(
                    ghb, self.AR[0:64, B_VFM + r * 2048:B_VFM + (r + 1) * 2048],
                    self.ag_b_out, self.IDX[0:64, 32 + hh * 4 + r:33 + hh * 4 + r])
            tok_vaug, _ = self.vaug_build(B_VFM, B_VAUG, tok_load)
            b.pool(lambda e: e.iota(MS[64:65, 0:1024], [[1, 1024]], base=0,
                                    channel_multiplier=0,
                                    allow_small_or_imprecise_dtypes=True),
                   waits=[b.tpe(), b.tdve()], inc=True)
            tio = b.tpool()
            b.act(lambda e, hh=hh: e.activation(
                MS[64:65, 0:1024], MS[64:65, 0:1024], AF.Exp, bias=0.0,
                scale=self.SLP[64:65, 8 + hh:9 + hh]),
                waits=[tio, self.tok_slp], inc=True)
            tei = b.tact()
            tev = None
            for half in range(2):
                web = [tei, b.tdve()]
                if tev is not None:
                    web.append(tev)
                p(lambda e, half=half: e.matmul(
                    self.pb6[:, :], self.ONESF[64:65, :],
                    MS[64:65, half * 512:(half + 1) * 512],
                    start=True, stop=True), waits=web)
                tbm = b.pe_mark()
                d(lambda e, half=half: e.tensor_copy(
                    EI[:, half * 512:(half + 1) * 512], self.pb6[:, :]),
                  waits=[tbm], inc=True)
                tev = b.tdve()
            d(lambda e, hh=hh: e.tensor_scalar(
                MS[:, 3200:3327], self.PF[:, 0:127], 0.0,
                self.SLP[:, 4 + hh:5 + hh], op0=ALU.min, op1=ALU.mult))
            d(lambda e, hh=hh: e.tensor_scalar(
                MS[:, 3330:3331], self.IOC[:, 0:1], -127.0,
                self.SLP[:, 4 + hh:5 + hh], op0=ALU.add, op1=ALU.mult))
            d(lambda e: e.tensor_scalar(MS[:, 3331:3332], MS[:, 3330:3331], -BEXP,
                                        None, op0=ALU.add), inc=True)
            tok_hbias = b.tdve()
            s8_min = 8.0 * (0.7071067811865476 ** (4 + 4 * hh))
            wcap = (BEXP + 150.0) / s8_min
            for qt in range(8):
                Q0 = 1024 * qt
                chunks = [tc for tc in range(64)
                          if tc * 128 + 128 > Q0 - wcap or tc * 128 + 128 > Q0]
                d(lambda e, Q0=Q0: e.tensor_scalar(
                    MS[:, 3072:3136], self.JG[:, :], -float(Q0), None, op0=ALU.add),
                  waits=[b.tact()])
                d(lambda e, hh=hh: e.tensor_scalar(
                    MS[:, 3136:3200], MS[:, 3072:3136],
                    self.SLP[:, 4 + hh:5 + hh], -BEXP, op0=ALU.mult, op1=ALU.add),
                  inc=True)
                tok_cols = b.tdve()

                def expb(sc, wpe, wav, ubuf, Q0=Q0, tok_cols=tok_cols):
                    base_w = [wpe, tok_cols] + ([wav] if wav is not None else [])
                    tdve = None
                    ci = 0
                    while ci < 2:
                        tc = 2 * sc + ci
                        Bu = tc * 128
                        U = self.ar(ubuf, 2048)
                        if Bu >= Q0 + 1024:
                            if ci == 0 and (2 * sc + 1) * 128 >= Q0 + 1024:
                                b.act(lambda e, U=U: e.activation(
                                    U[:, 0:2048], self.pA[:, :], AF.Exp,
                                    bias=-BEXP, scale=1.0),
                                    waits=base_w, inc=True)
                                ci = 2
                                continue
                            b.act(lambda e, U=U, ci=ci: e.activation(
                                U[:, ci * 1024:(ci + 1) * 1024],
                                self.pA[:, ci * 1024:(ci + 1) * 1024], AF.Exp,
                                bias=-BEXP, scale=1.0), waits=base_w, inc=True)
                        elif Bu + 128 <= Q0:
                            b.act(lambda e, U=U, ci=ci, tc=tc: e.activation(
                                U[:, ci * 1024:(ci + 1) * 1024],
                                self.pA[:, ci * 1024:(ci + 1) * 1024], AF.Exp,
                                bias=MS[:, 3136 + tc:3137 + tc], scale=1.0),
                                waits=base_w, inc=True)
                            tpen = b.tact()
                            d(lambda e, U=U, ci=ci: e.tensor_mul(
                                U[:, ci * 1024:(ci + 1) * 1024],
                                U[:, ci * 1024:(ci + 1) * 1024], EI[:, :]),
                              waits=[tpen], inc=True)
                            tdve = b.tdve()
                        else:
                            fz = Bu - Q0
                            if fz > 0:
                                b.act(lambda e, U=U, ci=ci, fz=fz: e.activation(
                                    U[:, ci * 1024:ci * 1024 + fz],
                                    self.pA[:, ci * 1024:ci * 1024 + fz], AF.Exp,
                                    bias=-BEXP, scale=1.0), waits=base_w, inc=True)
                            sc0 = 3456 + ci * 128
                            d(lambda e, ci=ci, fz=fz, sc0=sc0: e.tensor_add(
                                MS[:, sc0:sc0 + 127],
                                self.pA[:, ci * 1024 + fz:ci * 1024 + fz + 127],
                                MS[:, 3200:3327]),
                              waits=base_w + [b.tact()], inc=True)
                            tband = b.tdve()
                            b.act(lambda e, U=U, ci=ci, fz=fz, sc0=sc0: e.activation(
                                U[:, ci * 1024 + fz:ci * 1024 + fz + 127],
                                MS[:, sc0:sc0 + 127], AF.Exp, bias=-BEXP, scale=1.0),
                                waits=[tband] + base_w, inc=True)
                            ps_ = fz + 127
                            if ps_ < 1024:
                                b.act(lambda e, U=U, ci=ci, ps_=ps_: e.activation(
                                    U[:, ci * 1024 + ps_:(ci + 1) * 1024],
                                    self.pA[:, ci * 1024 + ps_:(ci + 1) * 1024],
                                    AF.Exp, bias=MS[:, 3331:3332], scale=1.0),
                                    waits=base_w, inc=True)
                                tps = b.tact()
                                d(lambda e, U=U, ci=ci, ps_=ps_: e.tensor_mul(
                                    U[:, ci * 1024 + ps_:(ci + 1) * 1024],
                                    U[:, ci * 1024 + ps_:(ci + 1) * 1024],
                                    EI[:, 0:1024 - ps_]),
                                  waits=[tps], inc=True)
                                tdve = b.tdve()
                        ci += 1
                    return b.tact(), tdve

                wl = [tok_load, tok_vaug, tok_hbias, tev, self.tok_consts]
                par = it % 2
                it += 1

                def sinkb(ctxt, tok, hh=hh, qt=qt, par=par):
                    return b.dgp(
                        "bctx%d" % par,
                        self.ag_c_in[(qt // 2) * 256 + hh * 64:
                                     (qt // 2) * 256 + hh * 64 + 64,
                                     (qt % 2) * 1024:((qt % 2) + 1) * 1024],
                        ctxt, waits=[tok])

                def expb(tc, wpe, wav, ubuf, half, Q0=Q0, tok_cols=tok_cols):
                    base_w = [wpe, tok_cols] + ([wav] if wav is not None else [])
                    tdve = None
                    Bu = tc * 128
                    U = self.ar(ubuf, 1024)
                    pa_h = self.pA[:, half * 1024:(half + 1) * 1024]
                    if Bu >= Q0 + 1024:
                        b.act(lambda e, U=U, pa_h=pa_h: e.activation(
                            U[:, :], pa_h, AF.Exp, bias=-BEXP, scale=1.0),
                            waits=base_w, inc=True, drain=False)
                    elif Bu + 128 <= Q0:
                        b.act(lambda e, U=U, pa_h=pa_h, tc=tc: e.activation(
                            U[:, :], pa_h, AF.Exp,
                            bias=MS[:, 3136 + tc:3137 + tc], scale=1.0),
                            waits=base_w, inc=True, drain=False)
                        tpen = b.tact()
                        d(lambda e, U=U: e.tensor_mul(U[:, :], U[:, :], EI[:, :]),
                          waits=[tpen], inc=True)
                        tdve = b.tdve()
                    else:
                        fz = Bu - Q0
                        if fz > 0:
                            b.act(lambda e, U=U, pa_h=pa_h, fz=fz: e.activation(
                                U[:, 0:fz], pa_h[:, 0:fz], AF.Exp, bias=-BEXP,
                                scale=1.0), waits=base_w, inc=True, drain=False)
                        sc0 = 3456 + half * 128
                        d(lambda e, pa_h=pa_h, fz=fz, sc0=sc0: e.tensor_add(
                            MS[:, sc0:sc0 + 127], pa_h[:, fz:fz + 127],
                            MS[:, 3200:3327]), waits=base_w + [b.tact()], inc=True)
                        tband = b.tdve()
                        b.act(lambda e, U=U, fz=fz, sc0=sc0: e.activation(
                            U[:, fz:fz + 127], MS[:, sc0:sc0 + 127], AF.Exp,
                            bias=-BEXP, scale=1.0), waits=[tband] + base_w, inc=True)
                        ps_ = fz + 127
                        if ps_ < 1024:
                            b.act(lambda e, U=U, pa_h=pa_h, ps_=ps_: e.activation(
                                U[:, ps_:1024], pa_h[:, ps_:1024], AF.Exp,
                                bias=MS[:, 3331:3332], scale=1.0),
                                waits=base_w, inc=True, drain=False)
                            tps = b.tact()
                            d(lambda e, U=U, ps_=ps_: e.tensor_mul(
                                U[:, ps_:1024], U[:, ps_:1024],
                                EI[:, 0:1024 - ps_]), waits=[tps], inc=True)
                            tdve = b.tdve()
                    return b.tact(), tdve

                ctx_war, ctx_tok, kload_war = self.flash_head(
                    B_KDUP, B_QDUP, B_VAUG, par, Q0, wl, expb, sinkb,
                    ctx_war, self.ctx_store[par], chunks=chunks)
                self.ctx_store[par] = ctx_tok
        g_b0, g_b1 = b.grp("bctx0"), b.grp("bctx1")
        tok_bctx = [(g_b0, g_b0.n), (g_b1, g_b1.n)]
        self.tap_dram("ctxb", self.ag_c_in, waits=tok_bctx)
        tok_agc = b.coll("c", [[0, 1, 2, 3], [4, 5, 6, 7]],
                         [self.ag_c_in], [self.ag_c_out], waits=tok_bctx)
        if sub < 3:
            self.tok_oprojb_pe = b.tpe()
            return b.tdve()
        tok_proj = b.dgp("pw", v3(self.ar(WB0, 8192), 1024),
                         self.projT.rearrange("(dc p) o -> p dc o", p=128),
                         waits=[kload_war])
        for pt in range(8):
            wg = [tok_agc, kload_war] if pt == 0 else ()
            tok_ctx = b.dgi("cg", self.AR[:, OFF_XN + pt * 2048:
                                          OFF_XN + (pt + 1) * 2048],
                            self.ag_c_out, self.IDX[:, 48 + pt:49 + pt], waits=wg)
        pa_free = None
        for oc in range(8):
            w = [tok_proj, tok_ctx]
            if pa_free is not None:
                w.append(pa_free)
            for ch in range(4):
                for dc in range(8):
                    lhsT = self.ar(WB0 + dc * 1024 + oc * 128, 128)
                    p(lambda e, ch=ch, dc=dc, lhsT=lhsT: e.matmul(
                        self.pA[:, ch * 512:(ch + 1) * 512], lhsT,
                        self.xnp(dc)[:, ch * 512:(ch + 1) * 512],
                        start=(dc == 0), stop=(dc == 7)),
                        waits=w if (ch == 0 and dc == 0) else ())
            tmm = b.pe_mark()
            gt2b = b.grp("tap2")
            d(lambda e, oc=oc: e.scalar_tensor_tensor(
                self.R[:, oc * 2048:(oc + 1) * 2048], self.pA[:, :],
                self.PBc[:, oc:oc + 1], self.R[:, oc * 2048:(oc + 1) * 2048],
                op0=ALU.add, op1=ALU.add),
              waits=[tmm] + ([(gt2b, gt2b.n)] if (oc == 0 and self.dbg) else []),
              inc=True)
            pa_free = b.tdve()
        self.tok_oprojb_pe = b.tpe()
        return b.tdve()

    def build(self):
        st = self.stage
        b = self.b
        self.preamble()
        self.build_tables()
        self.tok_xn1 = self.layernorm(WB1, first=True)
        self.tok_ln1_pe = self.last_ln_pe
        if st >= 1:
            self.tap_sbuf("xn1", self.AR[:, OFF_XN:OFF_XN + 16384],
                          waits=[self.tok_xn1])
        if st >= 2:
            self.attn_a()
            if "r2" in self.dbg:
                self.dbg_tap_R("r2")
        if st >= 4:
            tok_xn2 = self.layernorm(OFF_TAB, war_pe_tok=self.tok_oproj_pe)
            qw = [(g, g.n) for g in (b.grp("qa0"), b.grp("qa1"), b.grp("aga"))]
            self.ffn(tok_xn2, 1, qstg_war=qw)
            if "r3" in self.dbg:
                self.dbg_tap_R("r3")
        if st >= 42:
            self.tok_xn3 = self.layernorm(OFF_TAB, war_pe_tok=self.tok_ffn_pe)
            self.build_tables(reload_pos=True)
            if st >= 45:
                self.attn_b(sub=(1 if st < 48 else (2 if st < 59 else 3)))
            if "r4" in self.dbg:
                self.dbg_tap_R("r4")
        if st >= 60:
            tok_xn4 = self.layernorm(OFF_TAB, war_pe_tok=self.tok_oprojb_pe)
            qw = [(b.grp("agb"), b.grp("agb").n)]
            self.ffn(tok_xn4, 2, qstg_war=qw)
        tok_out = b.dsp("out", self.xout.rearrange("(pt p) t -> p pt t", p=128),
                        v3(self.R[:, :], 2048), waits=[b.tdve()])
        self.tap_toks.append(tok_out)
        fin = {}
        for sem, val in self.tap_toks:
            if val > fin.get(id(sem), (None, -1))[1]:
                fin[id(sem)] = (sem, val)
        b.on("sp", lambda e: e.nop(), waits=list(fin.values()), drain=False)
        return self.finalize()

    def finalize(self):
        b = self.b
        with self.ctx:
            block = self.ctx.enter_context(self.nc.Block())
            block.sync(lambda e: b.replay("sp", e))
            block.tensor(lambda e: b.replay("pe", e))
            block.scalar(lambda e: b.replay("act", e))
            block.vector(lambda e: b.replay("dve", e))
            block.gpsimd(lambda e: b.replay("pool", e))
        return self.nc


def kernel(**inputs):
    in_maps = host_prepare(inputs)
    mk = MK(stage=99)
    nc = mk.build()
    res = run_bass_kernel_spmd(nc, in_maps, core_ids=list(range(8)))
    return host_assemble(res.results)


# version 09
# speedup vs baseline: 1.0114x; 1.0037x over previous
import numpy as np
import concourse.bass as bass
import concourse.mybir as mybir
from concourse.bass import IndirectOffsetOnAxis
from concourse.bass_utils import run_bass_kernel_spmd
from contextlib import ExitStack

F32 = mybir.dt.float32
BF16 = mybir.dt.bfloat16
I32 = mybir.dt.int32
AF = mybir.ActivationFunctionType
ALU = mybir.AluOpType
AXX = mybir.AxisListType.X

D = 1024
TC = 2048
T = 8192
HD = 64
NH = 16
E = 16
BEXP = 20.0
MAGIC = float(np.float32(1.5 * 2.0 ** 23))
LN10K = float(np.log(10000.0))
TWO_PI = float(2.0 * np.pi)

AR_COLS = 53248
OFF_TAB = 0
OFF_XN = 8192
OFF_WB = 24576
OFF_SC = 49152
WB0 = OFF_WB
WB1 = OFF_WB + 8192
WB2 = OFF_WB + 16384
A_QDUP = OFF_XN + 0
A_KDUP = OFF_XN + 2048
A_VAUG = OFF_XN + 10240
A_CTX = OFF_XN + 14464
A_VFM = WB0
A_WO = WB1
B_QDUP = OFF_XN + 0
B_KDUP = OFF_XN + 8192
B_VFM = WB0
B_VAUG = WB1
B_CTX = WB1 + 4224
B_EI = WB1 + 5248
U0 = OFF_SC
U1 = OFF_SC + 2048
VA_STRIDE = 66
SER_FFN = False
CAP = 1280
NCH = CAP // 128
WINS = [(0, 512), (512, 512), (1024, 256)]
NWIN = len(WINS)
CHWIN = [0, 0, 0, 0, 1, 1, 1, 1, 2, 2]


def host_prepare(inputs):
    f = np.ascontiguousarray
    x = np.asarray(inputs["L0_attn_a_hidden_states"])
    xT = f(np.asarray(x).transpose(0, 2, 1))
    wqT = f(np.asarray(inputs["L0_attn_a_Wq"]).T)
    wkT = f(np.asarray(inputs["L0_attn_a_Wk"]).T)
    wvT = f(np.asarray(inputs["L0_attn_a_Wv"]).T)
    woT = f(np.asarray(inputs["L0_attn_a_Wo"]).T)
    bq = np.asarray(inputs["L0_attn_a_bq"]); bk = np.asarray(inputs["L0_attn_a_bk"])
    bv = np.asarray(inputs["L0_attn_a_bv"]); bo = np.asarray(inputs["L0_attn_a_bo"])
    abias = np.zeros((128, 20), np.float32)
    for oc in range(8):
        abias[:, oc] = bq[oc * 128:(oc + 1) * 128]
        abias[:, 12 + oc] = bo[oc * 128:(oc + 1) * 128]
    for oc in range(2):
        abias[:, 8 + oc] = bk[oc * 128:(oc + 1) * 128]
        abias[:, 10 + oc] = bv[oc * 128:(oc + 1) * 128]
    qkvT = f(np.asarray(inputs["L0_attn_b_qkv_w"]).T)
    hperm = [r + 4 * m for r in range(4) for m in range(4)]
    projP = np.asarray(inputs["L0_attn_b_proj_w"]).reshape(D, NH, HD)[:, hperm, :].reshape(D, D)
    projT = f(projP.T)
    pb = np.asarray(inputs["L0_attn_b_proj_b"])
    pbias = np.zeros((128, 8), np.float32)
    for oc in range(8):
        pbias[:, oc] = pb[oc * 128:(oc + 1) * 128]
    slopes = np.asarray(inputs["L0_attn_b_slopes"])
    wgT = f(np.asarray(inputs["L0_ffn_Wg"]).T)
    w1T = f(np.asarray(inputs["L0_ffn_W1"]).transpose(0, 2, 1))
    w3T = f(np.asarray(inputs["L0_ffn_W3"]).transpose(0, 2, 1))
    w2T = f(np.asarray(inputs["L0_ffn_W2"]).transpose(0, 2, 1))
    b1c = f(np.asarray(inputs["L0_ffn_b1"]).reshape(E, 8, 128).transpose(0, 2, 1))
    b2c = f(np.asarray(inputs["L0_ffn_b2"]).reshape(E, 8, 128).transpose(0, 2, 1))
    b3c = f(np.asarray(inputs["L0_ffn_b3"]).reshape(E, 8, 128).transpose(0, 2, 1))
    b2s = np.asarray(inputs["L0_ffn_b2"]).astype(np.float32)
    shared = dict(wqT=wqT, wkT=wkT, wvT=wvT, woT=woT, abias=abias,
                  qkvT=qkvT, projT=projT, pbias=pbias,
                  wgT=wgT, w1T=w1T, w3T=w3T, w2T=w2T, b1c=b1c, b2c=b2c, b3c=b3c,
                  b2s=b2s)
    in_maps = []
    p = np.arange(128)
    for c in range(8):
        seq, g = c // 4, c % 4
        m = dict(shared)
        m["xin"] = f(xT[seq][:, g * TC:(g + 1) * TC]).astype(np.float32)
        m["posrow"] = np.arange(g * TC, (g + 1) * TC, dtype=np.float32).reshape(1, TC)
        m["slopes_rep"] = np.broadcast_to(
            slopes[[g, g + 4, g + 8, g + 12]].astype(np.float32), (128, 4)).copy()
        idx = np.zeros((128, 56), np.int32)
        col = 0
        for hh in range(4):
            for r in range(4):
                idx[:, col] = r * 3072 + g * 768 + 256 + hh * 64 + (p % 64); col += 1
        for hh in range(4):
            for r in range(4):
                idx[:, col] = r * 3072 + g * 768 + 0 + hh * 64 + (p % 64); col += 1
        for hh in range(4):
            for r in range(4):
                idx[:, col] = r * 3072 + g * 768 + 512 + hh * 64 + (p % 64); col += 1
        for pt in range(8):
            r = pt // 2
            mem = 2 * (pt % 2) + p // 64
            idx[:, col] = r * 1024 + g * 256 + mem * 64 + (p % 64); col += 1
        m["idxc"] = idx
        in_maps.append(m)
    return in_maps


def host_assemble(results):
    out = np.zeros((2, T, D), np.float32)
    for c, r in enumerate(results):
        seq, g = c // 4, c % 4
        out[seq, g * TC:(g + 1) * TC, :] = np.asarray(r["xout"]).T
    return out



def v3(ap2d, b):
    return ap2d.rearrange("p (a b) -> p a b", b=b)


class Sem:
    def __init__(self, nc, ctx, name, is_group=False):
        self.h = ctx.enter_context(nc.semaphore(name))
        self.n = 0
        self.is_group = is_group


class Bld:
    ENGS = ("pe", "act", "dve", "pool", "sp")
    DRAIN = {"act", "dve", "pool"}

    def __init__(self, nc, ctx):
        self.nc = nc
        self.ctx = ctx
        self.prog = {e: [] for e in self.ENGS}
        self.spe = Sem(nc, ctx, "spe")
        self.sact = Sem(nc, ctx, "sact")
        self.sdve = Sem(nc, ctx, "sdve")
        self.spool = Sem(nc, ctx, "spool")
        self.groups = {}

    def grp(self, name):
        if name not in self.groups:
            self.groups[name] = Sem(self.nc, self.ctx, "g_" + name, is_group=True)
        return self.groups[name]

    def on(self, eng, fn, waits=(), inc=None, inc_amt=1, drain=None):
        if drain is None:
            drain = eng in self.DRAIN
        for sem, val in waits:
            assert not getattr(sem, "is_group", False) or val == sem.n, \
                f"stale group wait: {val} != {sem.n}"
        if inc is not None:
            inc.n += inc_amt
        self.prog[eng].append((fn, tuple(waits), inc, inc_amt, drain))

    def dsp(self, grp, out, in_, waits=(), **kw):
        g = self.grp(grp)
        fn = lambda e, out=out, in_=in_, kw=kw: e.dma_start(out=out, in_=in_, **kw)
        self.on("sp", fn, waits, inc=g, inc_amt=16, drain=False)
        return (g, g.n)

    def dgp(self, grp, out, in_, waits=(), **kw):
        g = self.grp(grp)
        fn = lambda e, out=out, in_=in_, kw=kw: e.dma_start(out=out, in_=in_, **kw)
        self.on("pool", fn, waits, inc=g, inc_amt=16, drain=False)
        return (g, g.n)

    def dgi(self, grp, out, in_, idx_ap, waits=()):
        g = self.grp(grp)
        fn = lambda e, out=out, in_=in_, idx_ap=idx_ap: e.indirect_dma_start(
            out, None, in_, IndirectOffsetOnAxis(idx_ap, 0))
        self.on("pool", fn, waits, inc=g, inc_amt=16, drain=False)
        return (g, g.n)

    def dsi(self, grp, out, in_, idx_ap, waits=()):
        g = self.grp(grp)
        fn = lambda e, out=out, in_=in_, idx_ap=idx_ap: e.indirect_dma_start(
            out, None, in_, IndirectOffsetOnAxis(idx_ap, 0))
        self.on("pool", fn, waits, inc=g, inc_amt=16, drain=False)
        return (g, g.n)

    def dsis(self, grp, out, idx_ap, in_, waits=()):
        g = self.grp(grp)
        fn = lambda e, out=out, in_=in_, idx_ap=idx_ap: e.indirect_dma_start(
            out, IndirectOffsetOnAxis(idx_ap, 0), in_, None)
        self.on("pool", fn, waits, inc=g, inc_amt=16, drain=False)
        return (g, g.n)

    def coll(self, name, groups, ins, outs, waits=()):
        g = self.grp("cc_" + name)
        fn = lambda e, groups=groups, ins=ins, outs=outs: e.collective_compute(
            "AllGather", ALU.bypass, replica_groups=groups, ins=ins, outs=outs)
        self.on("pool", fn, waits, inc=g, inc_amt=1, drain=False)
        return (g, g.n)

    def pe_mark(self):
        return (self.spe, self.mark("pe"))

    def pool_mark(self):
        return (self.spool, self.mark("pool"))

    def tpe(self):
        return (self.spe, self.spe.n)

    def tact(self):
        return (self.sact, self.sact.n)

    def tdve(self):
        return (self.sdve, self.sdve.n)

    def tpool(self):
        return (self.spool, self.spool.n)

    def pe(self, fn, waits=(), inc=False):
        self.on("pe", fn, waits, inc=self.spe if inc else None, drain=False)
        return self.spe.n

    def act(self, fn, waits=(), inc=False, drain=None):
        self.on("act", fn, waits, inc=self.sact if inc else None, drain=drain)
        return self.sact.n

    def dve(self, fn, waits=(), inc=False):
        self.on("dve", fn, waits, inc=self.sdve if inc else None)
        return self.sdve.n

    def pool(self, fn, waits=(), inc=False):
        self.on("pool", fn, waits, inc=self.spool if inc else None)
        return self.spool.n

    def mark(self, eng):
        sems = {"pe": self.spe, "act": self.sact, "dve": self.sdve,
                "pool": self.spool}
        sem = sems[eng]
        fn, waits, inc, amt, drain = self.prog[eng][-1]
        assert inc is None, "last entry already incs"
        sem.n += 1
        self.prog[eng][-1] = (fn, waits, sem, amt if inc else 1, drain)
        return sem.n

    def replay(self, key, engine):
        waited = {}
        for fn, waits, inc, inc_amt, drain in self.prog[key]:
            for sem, val in waits:
                if val > 0 and val > waited.get(id(sem), -1):
                    engine.wait_ge(sem.h, val)
                    waited[id(sem)] = val
            if drain:
                engine.drain()
            inst = fn(engine)
            if inc is not None:
                inst.then_inc(inc.h, inc_amt)


class MK:
    def __init__(self, stage=99, taps=()):
        self.stage = stage
        self.taps = set(taps)
        self.nc = bass.Bass("TRN2", target_bir_lowering=False)
        self.ctx = ExitStack()
        self._declare()
        self.b = Bld(self.nc, self.ctx)
        self.tap_toks = []
        self.ptr_last = None

    def _declare(self):
        nc, ctx = self.nc, self.ctx
        din = lambda n, s: nc.dram_tensor(n, s, F32, kind="ExternalInput").ap()
        self.xin = din("xin", [D, TC])
        self.posrow_d = din("posrow", [1, TC])
        self.slopes_d = din("slopes_rep", [128, 4])
        self.idxc_d = nc.dram_tensor("idxc", [128, 56], I32, kind="ExternalInput").ap()
        self.abias_d = din("abias", [128, 20])
        self.pbias_d = din("pbias", [128, 8])
        self.wqT = din("wqT", [D, D]); self.wkT = din("wkT", [D, 256])
        self.wvT = din("wvT", [D, 256]); self.woT = din("woT", [D, D])
        self.qkvT = din("qkvT", [D, 3 * D]); self.projT = din("projT", [D, D])
        self.wgT_d = din("wgT", [D, E])
        self.w1T = din("w1T", [E, D, D]); self.w3T = din("w3T", [E, D, D])
        self.w2T = din("w2T", [E, D, D])
        self.b1c_d = din("b1c", [E, 128, 8]); self.b2c_d = din("b2c", [E, 128, 8])
        self.b3c_d = din("b3c", [E, 128, 8])
        self.b2s_d = din("b2s", [E, D])
        self.xout = nc.dram_tensor("xout", [D, TC], F32, kind="ExternalOutput").ap()
        dbf = lambda n, s: nc.dram_tensor(n, s, BF16).ap()
        self.ag_a_in = dbf("ag_a_in", [512, TC])
        self.ag_a_out = dbf("ag_a_out", [2048, TC])
        self.ag_b_in = dbf("ag_b_in", [3072, TC])
        self.ag_b_out = dbf("ag_b_out", [12288, TC])
        self.ag_c_in = dbf("ag_c_in", [1024, TC])
        self.ag_c_out = dbf("ag_c_out", [4096, TC])
        self.qa_dram = dbf("qa_dram", [D, TC])
        self.ctxa_dram = dbf("ctxa_dram", [D, TC])
        self.xtok_d = dbf("xtok", [TC, D])
        self.ycomp_d = dbf("ycomp", [E * CAP, D])
        self.invmap_d = nc.dram_tensor("invmap", [E * CAP, 1], I32).ap()
        self.wcomp_d = nc.dram_tensor("wcomp", [E * CAP, 1], F32).ap()
        sb = lambda n, s, d: ctx.enter_context(nc.sbuf_tensor(n, s, d))
        self.R = sb("R", [128, 8 * TC], F32)
        self.MS = sb("MS", [128, 4096], F32)
        self.AR = sb("AR", [128, AR_COLS], BF16)
        self.WET = sb("WET", [128, 2048], BF16)
        self.AB = sb("AB", [128, 20], F32)
        self.PBc = sb("PBc", [128, 8], F32)
        self.B1 = sb("B1", [128, 128], F32)
        self.B2 = sb("B2", [128, 128], F32)
        self.B3 = sb("B3", [128, 128], F32)
        self.SLP = sb("SLP", [128, 12], F32)
        self.WGT = sb("WGT", [128, 128], BF16)
        self.IDX = sb("IDX", [128, 56], I32)
        self.IDENT = sb("IDENT", [128, 128], BF16)
        self.PERMA = sb("PERMA", [128, 128], BF16)
        self.PERMB = sb("PERMB", [128, 128], BF16)
        self.ONESB = sb("ONESB", [128, 128], BF16)
        self.ONESF = sb("ONESF", [128, 128], F32)
        self.ONESC = sb("ONESC", [128, 1], BF16)
        self.CONE = sb("CONE", [128, 8], F32)
        self.PF = sb("PF", [128, 128], F32)
        self.JG = sb("JG", [128, 64], F32)
        self.IOC = sb("IOC", [128, 1], F32)
        self.CONSTF = sb("CONSTF", [128, 2], F32)
        nc.const_aps.register(F32, -BEXP, self.CONSTF[:, 0:1])
        self.QRAW = sb("QRAW", [128, 2048], BF16)
        self.QSTG = sb("QSTG", [128, 2048], BF16)
        self.LTRI = sb("LTRI", [128, 128], BF16)
        self.SLOTI = sb("SLOTI", [128, 64], I32)
        self.IDXS = sb("IDXS", [128, 2 * NCH], I32)
        self.TOKID = sb("TOKID", [128, 16], I32)
        self.ZI32 = sb("ZI32", [128, E * CAP // 128], I32)
        self.WSTR = sb("WSTR", [128, 2 * NCH], F32)
        self.ZF32 = sb("ZF32", [128, E * CAP // 128], F32)
        self.B2S = sb("B2S", [16, D], BF16)
        ps = lambda n, s, d: ctx.enter_context(nc.psum_tensor(n, s, d))
        self.pA = ps("pA", [128, 2048], F32)
        self.pB = ps("pB", [128, 1024], F32)
        self.pb6 = ps("pb6", [128, 512], F32)
        self.ptr = ps("ptr", [128, 1024], BF16)
        self.dbg = {}
        shapes = {"xn1": [D, TC], "qa": [D, TC], "aga": [512, TC], "r2": [D, TC],
                  "wfull": [128, 256], "r3": [D, TC], "agb": [3072, TC],
                  "r4": [D, TC], "ctxa": [D, TC], "ctxb": [1024, TC]}
        for t in self.taps:
            self.dbg[t] = self.nc.dram_tensor(
                "dbg_" + t, shapes[t], F32, kind="ExternalOutput").ap()

    def ar(self, off, cols, rows=128, r0=0):
        return self.AR[r0:r0 + rows, off:off + cols]

    def xnp(self, pt):
        return self.ar(OFF_XN + pt * 2048, 2048)

    def tap_sbuf(self, name, src_ap, waits=()):
        if name in self.dbg:
            self.tap_toks.append(self.b.dgp(
                "tap", self.dbg[name].rearrange("(pt p) t -> p pt t", p=128),
                v3(src_ap, 2048), waits=waits))

    def tap_dram(self, name, src_ap, waits=()):
        if name in self.dbg:
            self.tap_toks.append(self.b.dgp("tap", self.dbg[name], src_ap,
                                            waits=waits))

    def dbg_tap_R(self, name):
        self.tap_toks.append(self.b.dsp(
            "tap2", self.dbg[name].rearrange("(pt p) t -> p pt t", p=128),
            v3(self.R[:, :], 2048), waits=[self.b.tdve()]))

    def preamble(self):
        b = self.b
        self.tok_R = b.dsp("in", v3(self.R[:, :], 2048),
                           self.xin.rearrange("(pt p) t -> p pt t", p=128))
        b.dsp("in", self.AB[:, :], self.abias_d)
        b.dsp("in", self.PBc[:, :], self.pbias_d)
        b.dsp("in", self.SLP[:, 0:4], self.slopes_d)
        b.dsp("in", self.IDX[:, :], self.idxc_d)
        b.dsp("in", self.MS[0:1, 0:TC], self.posrow_d)
        b.dsp("in", v3(self.B1[:, :], 8), self.b1c_d.rearrange("e p c -> p e c"))
        b.dsp("in", v3(self.B2[:, :], 8), self.b2c_d.rearrange("e p c -> p e c"))
        self.tok_in = b.dsp("in", v3(self.B3[:, :], 8),
                            self.b3c_d.rearrange("e p c -> p e c"))
        self.tok_R = self.tok_in
        b.dgp("wg", self.B2S[:, :], self.b2s_d)
        self.tok_wg = b.dgp("wg", v3(self.WGT[:, :], 16),
                            self.wgT_d.rearrange("(dc p) e -> p dc e", p=128))
        P = b.pool
        P(lambda e: e.memset(self.IDENT[:, :], 1.0))
        P(lambda e: e.affine_select(self.IDENT[:, :], self.IDENT[:, :], [[1, 128]],
                                    ALU.is_equal, 0.0, base=0, channel_multiplier=-1))
        P(lambda e: e.memset(self.ONESB[:, :], 1.0))
        P(lambda e: e.memset(self.ONESF[:, :], 1.0))
        P(lambda e: e.memset(self.ONESC[:, :], 1.0))
        P(lambda e: e.memset(self.CONE[:, :], 1.0))
        P(lambda e: e.memset(self.CONSTF[:, 0:2], -BEXP))
        P(lambda e: e.iota(self.PF[:, :], [[-1, 128]], base=0, channel_multiplier=1,
                           allow_small_or_imprecise_dtypes=True))
        P(lambda e: e.iota(self.JG[:, :], [[128, 64]], base=0, channel_multiplier=1,
                           allow_small_or_imprecise_dtypes=True))
        P(lambda e: e.iota(self.IOC[:, :], [[1, 1]], base=0, channel_multiplier=1,
                           allow_small_or_imprecise_dtypes=True))
        P(lambda e: e.memset(self.LTRI[:, :], 1.0))
        P(lambda e: e.affine_select(self.LTRI[:, :], self.LTRI[:, :], [[1, 128]],
                                    ALU.is_gt, 0.0, base=0, channel_multiplier=-1))
        P(lambda e: e.memset(self.ZI32[:, :], 0.0))
        P(lambda e: e.memset(self.ZF32[:, :], 0.0))
        P(lambda e: e.iota(self.MS[:, 3000:3016], [[128, 16]], base=0,
                           channel_multiplier=1,
                           allow_small_or_imprecise_dtypes=True))
        P(lambda e: e.memset(self.PERMA[:, :], 0.0))
        P(lambda e: e.memset(self.PERMB[:, :], 0.0))
        for bp in (0, 64):
            for (mat, half, st, bs) in (
                    (self.PERMA, 0, -2, -1), (self.PERMA, 1, -2, 0),
                    (self.PERMB, 0, -1, -32), (self.PERMB, 1, -1, 0)):
                sl = mat[bp:bp + 64, bp + half * 32: bp + half * 32 + 32]
                P(lambda e, sl=sl: e.memset(sl, 1.0))
                P(lambda e, sl=sl, st=st, bs=bs: e.affine_select(
                    sl, sl, [[st, 32]], ALU.is_equal, 0.0, base=bs,
                    channel_multiplier=1))
        self.tok_consts = b.pool_mark()
        d = b.dve
        d(lambda e: e.tensor_scalar(self.SLP[:, 4:8], self.SLP[:, 0:4], 8.0, None,
                                    op0=ALU.mult), waits=[self.tok_in])
        d(lambda e: e.tensor_scalar(self.SLP[:, 8:12], self.SLP[:, 0:4], -8.0, None,
                                    op0=ALU.mult), inc=True)
        self.tok_slp = b.tdve()
        d(lambda e: e.tensor_scalar(self.TOKID[:, :], self.MS[:, 3000:3016], 1.0,
                                    None, op0=ALU.mult), waits=[self.tok_consts],
          inc=True)
        self.tok_tokid = b.tdve()

    def build_tables(self, reload_pos=False):
        b = self.b
        MS, pA = self.MS, self.pA
        wpos = [self.tok_in]
        if reload_pos:
            wpos = [b.dsp("pos", MS[0:1, 0:TC], self.posrow_d,
                          waits=[b.tpe()])]
        b.pool(lambda e: e.iota(MS[0:1, 2048:2112], [[0, 2], [1, 32]], base=0,
                                channel_multiplier=0,
                                allow_small_or_imprecise_dtypes=True),
               waits=wpos, inc=True)
        tp = b.tpool()
        b.act(lambda e: e.activation(MS[0:1, 2112:2176], MS[0:1, 2048:2112],
                                     AF.Exp, bias=0.0, scale=-LN10K / 32.0),
              waits=[tp] + wpos, inc=True)
        ta = b.tact()
        wA = [ta, b.tdve(), b.tpe()]
        for ch in range(4):
            for bp in (0, 64):
                b.pe(lambda e, ch=ch, bp=bp: e.matmul(
                    pA[bp:bp + 64, ch * 512:(ch + 1) * 512],
                    MS[0:1, 2112:2176],
                    MS[0:1, ch * 512:(ch + 1) * 512],
                    start=True, stop=True, tile_position=(0, bp)),
                    waits=wA if (ch == 0 and bp == 0) else ())
        tpe = b.pe_mark()
        d = b.dve
        d(lambda e: e.tensor_scalar(MS[:, 0:2048], pA[:, :], 1.0 / TWO_PI, None,
                                    op0=ALU.mult), waits=[tpe], inc=True)
        self.tok_tabu = b.tdve()
        d(lambda e: e.tensor_scalar(MS[:, 2048:4096], MS[:, 0:2048],
                                    0.25, MAGIC, op0=ALU.add, op1=ALU.add))
        d(lambda e: e.tensor_scalar(MS[:, 2048:4096], MS[:, 2048:4096], -MAGIC,
                                    None, op0=ALU.add))
        d(lambda e: e.scalar_tensor_tensor(MS[:, 2048:4096], MS[:, 0:2048], 0.25,
                                           MS[:, 2048:4096], op0=ALU.add,
                                           op1=ALU.subtract), inc=True)
        tcos = b.tdve()
        cosk = self.ar(OFF_TAB, 2048)
        b.act(lambda e: e.activation(cosk, MS[:, 2048:4096], AF.Sin, bias=0.0,
                                     scale=TWO_PI), waits=[tcos], inc=True)
        tck = b.tact()
        d(lambda e: e.tensor_scalar(MS[:, 2048:4096], MS[:, 0:2048], MAGIC, None,
                                    op0=ALU.add), waits=[tck])
        d(lambda e: e.tensor_scalar(MS[:, 2048:4096], MS[:, 2048:4096], -MAGIC,
                                    None, op0=ALU.add))
        d(lambda e: e.tensor_sub(MS[:, 0:2048], MS[:, 0:2048], MS[:, 2048:4096]),
          inc=True)
        tsin = b.tdve()
        sink = self.ar(OFF_TAB + 2048, 2048)
        b.act(lambda e: e.activation(sink, MS[:, 0:2048], AF.Sin, bias=0.0,
                                     scale=TWO_PI), waits=[tsin], inc=True)
        tsk = b.tact()
        d(lambda e: e.tensor_scalar(self.AR[0:32, 2048:4096],
                                    self.AR[0:32, 2048:4096], -1.0, None,
                                    op0=ALU.mult), waits=[tsk])
        d(lambda e: e.tensor_scalar(self.AR[64:96, 2048:4096],
                                    self.AR[64:96, 2048:4096], -1.0, None,
                                    op0=ALU.mult))
        d(lambda e: e.tensor_scalar(self.ar(OFF_TAB + 4096, 2048), cosk, 0.125,
                                    None, op0=ALU.mult))
        d(lambda e: e.tensor_scalar(self.ar(OFF_TAB + 6144, 2048), sink, 0.125,
                                    None, op0=ALU.mult), inc=True)
        self.tok_tab = b.tdve()

    def layernorm(self, sq_off, first=False, war_pe_tok=None):
        b = self.b
        MS, pA = self.MS, self.pA
        d, p = b.dve, b.pe
        pe_done = [None] * 8
        for pt in range(8):
            w = []
            if first:
                w.append(self.tok_R)
            if war_pe_tok is not None:
                w.append(war_pe_tok)
            d(lambda e, pt=pt: e.tensor_copy(self.xnp(pt),
                                             self.R[:, pt * 2048:(pt + 1) * 2048]),
              waits=w, inc=True)
            wsq = [pe_done[pt - 2]] if pt >= 2 else []
            sqb = self.ar(sq_off + (pt % 2) * 2048, 2048)
            d(lambda e, pt=pt, sqb=sqb: e.tensor_mul(sqb, self.xnp(pt), self.xnp(pt)),
              waits=wsq, inc=True)
            tdve = b.tdve()
            for ch in range(4):
                p(lambda e, pt=pt, ch=ch: e.matmul(
                    pA[0:1, ch * 512:(ch + 1) * 512], self.ONESC[:, 0:1],
                    self.xnp(pt)[:, ch * 512:(ch + 1) * 512],
                    start=(pt == 0), stop=(pt == 7)),
                    waits=[tdve, self.tok_consts] if ch == 0 else ())
                p(lambda e, ch=ch, sqb=sqb, pt=pt: e.matmul(
                    pA[64:65, ch * 512:(ch + 1) * 512], self.ONESC[:, 0:1],
                    sqb[:, ch * 512:(ch + 1) * 512],
                    start=(pt == 0), stop=(pt == 7), tile_position=(0, 64)))
            pe_done[pt] = b.pe_mark()
        tstat = pe_done[7]
        self.last_ln_pe = tstat
        d(lambda e: e.tensor_scalar(MS[0:1, 0:2048], pA[0:1, :], 1.0 / D, None,
                                    op0=ALU.mult), waits=[tstat])
        d(lambda e: e.tensor_mul(MS[0:1, 2048:4096], MS[0:1, 0:2048],
                                 MS[0:1, 0:2048]))
        d(lambda e: e.scalar_tensor_tensor(MS[0:1, 2048:4096], pA[64:65, :],
                                           1.0 / D, MS[0:1, 2048:4096],
                                           op0=ALU.mult, op1=ALU.subtract))
        d(lambda e: e.tensor_scalar(MS[0:1, 2048:4096], MS[0:1, 2048:4096], 1e-5,
                                    None, op0=ALU.add), inc=True)
        tvar = b.tdve()
        b.act(lambda e: e.activation(MS[0:1, 2048:4096], MS[0:1, 2048:4096], AF.Ln,
                                     bias=0.0, scale=1.0), waits=[tvar])
        b.act(lambda e: e.activation(MS[0:1, 2048:4096], MS[0:1, 2048:4096], AF.Exp,
                                     bias=0.0, scale=-0.5), inc=True)
        trstd = b.tact()
        mb = self.ar(OFF_SC, 2048)
        rb = self.ar(OFF_SC + 2048, 2048)
        prev = tvar
        for (col0, dst) in ((0, mb), (2048, rb)):
            for ch in range(4):
                wl = [trstd, prev]
                p(lambda e, col0=col0, ch=ch: e.matmul(
                    pA[:, ch * 512:(ch + 1) * 512], self.ONESF[0:1, :],
                    MS[0:1, col0 + ch * 512:col0 + (ch + 1) * 512],
                    start=True, stop=True), waits=wl if ch == 0 else ())
            tb = b.pe_mark()
            d(lambda e, dst=dst: e.tensor_copy(dst, pA[:, :]), waits=[tb], inc=True)
            prev = b.tdve()
        for pt in range(8):
            d(lambda e, pt=pt: e.tensor_sub(self.xnp(pt), self.xnp(pt), mb))
            d(lambda e, pt=pt: e.tensor_mul(self.xnp(pt), self.xnp(pt), rb),
              inc=(pt == 7))
        return b.tdve()

    def proj(self, wslot_off, wstride, oc_list, rope, sink, extra_wait=(),
             stg_war=()):
        b = self.b
        pA = self.pA
        d, p = b.dve, b.pe
        half_free = [None, None]
        stg_done = [None, None]
        stg_free = None
        stg_war = list(stg_war)
        last_evac = None
        for oc, bias in oc_list:
            rope_tok = None
            for half in range(2):
                w = list(extra_wait)
                if half_free[half] is not None:
                    w.append(half_free[half])
                for ch2 in range(2):
                    for dc in range(8):
                        lhsT = self.ar(wslot_off + dc * wstride + oc * 128, 128)
                        p(lambda e, half=half, ch2=ch2, dc=dc, lhsT=lhsT: e.matmul(
                            pA[:, half * 1024 + ch2 * 512:
                               half * 1024 + (ch2 + 1) * 512], lhsT,
                            self.xnp(dc)[:, half * 1024 + ch2 * 512:
                                         half * 1024 + (ch2 + 1) * 512],
                            start=(dc == 0), stop=(dc == 7)),
                            waits=w if (ch2 == 0 and dc == 0) else ())
                tmm = b.pe_mark()
                hs = slice(half * 1024, (half + 1) * 1024)
                pa_h = pA[:, hs]
                if rope is None:
                    wv = [tmm]
                    if half == 0:
                        if stg_free is not None:
                            wv.append(stg_free)
                        elif stg_war:
                            wv += stg_war
                    if bias is not None:
                        d(lambda e, bias=bias, hs=hs, pa_h=pa_h: e.tensor_scalar(
                            self.QSTG[:, hs], pa_h, bias, None, op0=ALU.add),
                          waits=wv, inc=True)
                    else:
                        d(lambda e, hs=hs, pa_h=pa_h: e.tensor_copy(
                            self.QSTG[:, hs], pa_h), waits=wv, inc=True)
                    half_free[half] = b.tdve()
                else:
                    perm, cos_off, sin_off = rope
                    if bias is not None:
                        d(lambda e, bias=bias, hs=hs, pa_h=pa_h: e.tensor_scalar(
                            self.QRAW[:, hs], pa_h, bias, None, op0=ALU.add),
                          waits=[tmm], inc=True)
                    else:
                        d(lambda e, hs=hs, pa_h=pa_h: e.tensor_copy(
                            self.QRAW[:, hs], pa_h), waits=[tmm], inc=True)
                    half_free[half] = b.tdve()
                    w2 = [half_free[half], self.tok_consts]
                    if last_evac is not None:
                        w2.append(last_evac)
                    for q2 in range(2):
                        p(lambda e, half=half, q2=q2, perm=perm: e.matmul(
                            self.pB[:, q2 * 512:(q2 + 1) * 512], perm,
                            self.QRAW[:, half * 1024 + q2 * 512:
                                      half * 1024 + (q2 + 1) * 512],
                            start=True, stop=True), waits=w2 if q2 == 0 else ())
                    tperm = b.pe_mark()
                    wv2 = [tperm]
                    if stg_done[half] is not None:
                        wv2.append(stg_done[half])
                    elif stg_war:
                        wv2 += stg_war
                    b.act(lambda e, hs=hs: e.activation(
                        self.QSTG[:, hs], self.pB[:, :], AF.Copy, bias=0.0,
                        scale=1.0), waits=wv2, inc=True)
                    evac_tok = b.tact()
                    last_evac = evac_tok
                    cos_ap = self.ar(cos_off + half * 1024, 1024)
                    sin_ap = self.ar(sin_off + half * 1024, 1024)
                    d(lambda e, hs=hs, cos_ap=cos_ap: e.tensor_mul(
                        self.QRAW[:, hs], self.QRAW[:, hs], cos_ap),
                      waits=[tperm])
                    d(lambda e, hs=hs, sin_ap=sin_ap: e.tensor_mul(
                        self.QSTG[:, hs], self.QSTG[:, hs], sin_ap),
                      waits=[evac_tok])
                    d(lambda e, hs=hs: e.tensor_add(
                        self.QSTG[:, hs], self.QSTG[:, hs], self.QRAW[:, hs]),
                      inc=True)
                    rope_tok = b.tdve()
                    half_free[half] = rope_tok
            if rope is None:
                stg_free = sink(oc, self.QSTG, half_free[1])
            else:
                tok = sink(oc, self.QSTG, rope_tok)
                stg_done = [tok, tok]
        return b.tpe()

    def vaug_build(self, vfm_off, vaug_off, tok_v):
        b = self.b
        p, d = b.pe, b.dve
        evac_tok = b.tdve()
        for bt in range(4):
            wt = [tok_v, evac_tok, self.tok_consts]
            for i in range(16):
                cc = bt * 16 + i
                p(lambda e, cc=cc, i=i, vfm_off=vfm_off: e.transpose(
                    self.ptr[:, i * 64:(i + 1) * 64],
                    self.ar(vfm_off + cc * 128, 128, rows=64),
                    self.IDENT[0:64, 0:64]),
                  waits=wt if i == 0 else ())
            tt = b.pe_mark()
            dst = self.AR[:, vaug_off + bt * 16 * VA_STRIDE:
                          vaug_off + (bt + 1) * 16 * VA_STRIDE].rearrange(
                "p (c v) -> p c v", v=VA_STRIDE)[:, :, 0:64]
            d(lambda e, dst=dst: e.tensor_copy(dst, self.ptr[:, :].rearrange(
                "p (c v) -> p c v", v=64)), waits=[tt], inc=True)
            evac_tok = b.tdve()
        ones = self.AR[:, vaug_off:vaug_off + 64 * VA_STRIDE].rearrange(
            "p (c v) -> p c v", v=VA_STRIDE)[:, :, 64:65]
        d(lambda e, ones=ones: e.memset(ones, 1.0), inc=True)
        return b.tdve(), b.tpe()

    def flash_head(self, kdup_off, qdup_off, vaug_off, parity, Q0,
                   wait_loads, exp_emit, ctx_sink, war_ctx_tok, war_store,
                   chunks=None):
        b = self.b
        p, d = b.pe, b.dve
        pA, pB = self.pA, self.pB
        if chunks is None:
            chunks = list(range(64))
        n = len(chunks)
        t_exp = [None] * n
        t_av = [None] * n

        def emit_S(i):
            tc = chunks[i]
            half = i % 2
            w = []
            if i < 2:
                w += list(wait_loads)
                if i == 0 and war_ctx_tok is not None:
                    w.append(war_ctx_tok)
            else:
                w.append(t_exp[i - 2])
            for hv in range(2):
                lhsT = self.AR[hv * 64:hv * 64 + 64,
                               kdup_off + tc * 128:kdup_off + (tc + 1) * 128]
                rhs = self.AR[hv * 64:hv * 64 + 64,
                              qdup_off + Q0 + hv * 512:qdup_off + Q0 + (hv + 1) * 512]
                p(lambda e, half=half, hv=hv, lhsT=lhsT, rhs=rhs: e.matmul(
                    pA[:, half * 1024 + hv * 512:half * 1024 + (hv + 1) * 512],
                    lhsT, rhs, start=True, stop=True, tile_position=(hv * 64, 0)),
                    waits=w if hv == 0 else ())
            return b.pe_mark()

        ts_tok = [None] * n
        ts_tok[0] = emit_S(0)
        if n > 1:
            ts_tok[1] = emit_S(1)
        for i in range(n):
            tc = chunks[i]
            half = i % 2
            ubuf = U0 if half == 0 else U1
            wav = t_av[i - 2] if i >= 2 else None
            ta, td = exp_emit(tc, ts_tok[i], wav, ubuf, half)
            t_exp[i] = ta
            if i + 2 < n:
                ts_tok[i + 2] = emit_S(i + 2)
            for hv in range(2):
                lhsT = self.AR[:, vaug_off + tc * VA_STRIDE:
                               vaug_off + tc * VA_STRIDE + 65]
                wv = [ta] + ([td] if td is not None else [])
                p(lambda e, hv=hv, i=i, lhsT=lhsT, ubuf=ubuf: e.matmul(
                    pB[0:65, hv * 512:(hv + 1) * 512], lhsT,
                    self.ar(ubuf, 1024)[:, hv * 512:(hv + 1) * 512],
                    start=(i == 0), stop=(i == n - 1)),
                    waits=wv if hv == 0 else ())
            t_av[i] = b.pe_mark()
        MS = self.MS
        slot = MS[0:65, 1024 + parity * 1024:2048 + parity * 1024]
        wm = [t_av[n - 1]] + ([war_store] if war_store is not None else [])
        d(lambda e, slot=slot: e.tensor_copy(slot[:, 0:1024], pB[0:65, :]),
          waits=wm, inc=True)
        tfree = b.tdve()
        d(lambda e, slot=slot: e.reciprocal(MS[64:65, 0:1024],
                                            slot[64:65, 0:1024]), inc=True)
        prev = b.tdve()
        for half in range(2):
            b.pe(lambda e, half=half: e.matmul(
                self.pb6[0:64, :], self.ONESF[64:65, 0:64],
                MS[64:65, half * 512:(half + 1) * 512], start=True, stop=True),
                waits=[prev])
            tb = b.pe_mark()
            d(lambda e, half=half: e.tensor_copy(
                MS[0:64, half * 512:(half + 1) * 512], self.pb6[0:64, :]),
              waits=[tb], inc=True)
            prev = b.tdve()
        ctxt = slot[0:64, 0:1024]
        d(lambda e, ctxt=ctxt: e.tensor_mul(ctxt, ctxt, MS[0:64, 0:1024]),
          inc=True)
        tctx = b.tdve()
        tok = ctx_sink(ctxt, tctx)
        return tfree, tok, b.tpe()

    def attn_a(self):
        b = self.b
        d, p = b.dve, b.pe
        tok_wq = b.dgp("wq", v3(self.ar(WB0, 8192), 1024),
                       self.wqT.rearrange("(dc p) o -> p dc o", p=128))
        tok_wk = b.dgp("wkv", v3(self.ar(WB1, 2048), 256),
                       self.wkT.rearrange("(dc p) o -> p dc o", p=128),
                       waits=[self.tok_ln1_pe])
        tok_wv = b.dgp("wkv", v3(self.ar(WB1 + 2048, 2048), 256),
                       self.wvT.rearrange("(dc p) o -> p dc o", p=128))

        def sink_q(oc, tile, tok):
            return b.dsp("qa%d" % (oc % 2),
                         self.qa_dram[oc * 128:(oc + 1) * 128, :], tile[:, :],
                         waits=[tok])

        def sink_k(oc, tile, tok):
            return b.dsp("aga", self.ag_a_in[oc * 128:(oc + 1) * 128, :],
                         tile[:, :], waits=[tok])

        def sink_v(oc, tile, tok):
            return b.dsp("aga", self.ag_a_in[256 + oc * 128:256 + (oc + 1) * 128, :],
                         tile[:, :], waits=[tok])

        rope_k = (self.PERMA[:, :], OFF_TAB, OFF_TAB + 2048)
        rope_q = (self.PERMA[:, :], OFF_TAB + 4096, OFF_TAB + 6144)
        self.proj(WB0, 1024, [(oc, self.AB[:, oc:oc + 1]) for oc in range(8)],
                  rope_q, sink_q,
                  extra_wait=[tok_wq, self.tok_xn1, self.tok_tab])
        gq0, gq1 = b.grp("qa0"), b.grp("qa1")
        self.proj(WB1, 256, [(oc, self.AB[:, 8 + oc:9 + oc]) for oc in range(2)],
                  rope_k, sink_k, extra_wait=[tok_wv],
                  stg_war=[(gq0, gq0.n), (gq1, gq1.n)])
        ga = b.grp("aga")
        self.proj(WB1 + 2048, 256,
                  [(oc, self.AB[:, 10 + oc:11 + oc]) for oc in range(2)],
                  None, sink_v, extra_wait=[], stg_war=[(ga, ga.n)])
        g_aga = b.grp("aga")
        tok_contrib = (g_aga, g_aga.n)
        g_qa0, g_qa1 = b.grp("qa0"), b.grp("qa1")
        tok_qa = [(g_qa0, g_qa0.n), (g_qa1, g_qa1.n)]
        self.tap_dram("qa", self.qa_dram, waits=tok_qa)
        self.tap_dram("aga", self.ag_a_in, waits=[tok_contrib])
        tok_aga = b.coll("a", [[0, 1, 2, 3], [4, 5, 6, 7]],
                         [self.ag_a_in], [self.ag_a_out],
                         waits=[tok_contrib])
        tok_wo = b.dgp("wo", v3(self.ar(A_WO, 8192), 1024),
                       self.woT.rearrange("(dc p) o -> p dc o", p=128))

        def exp_plain(tc, wpe, wav, ubuf, half):
            w = [wpe] + ([wav] if wav is not None else [])
            b.act(lambda e, ubuf=ubuf, half=half: e.activation(
                self.ar(ubuf, 1024), self.pA[:, half * 1024:(half + 1) * 1024],
                AF.Exp, bias=-BEXP, scale=1.0), waits=w, inc=True, drain=False)
            return b.tact(), None

        kload_war = None
        ctx_war = None
        ctx_tok = None
        self.ctx_store = [None, None]
        src = self.ag_a_out.rearrange("(r c) t -> c r t", c=512)
        for kv in range(4):
            w_k = [tok_aga]
            if kload_war is not None:
                w_k.append(kload_war)
            gkv = "kva%d" % (kv % 2)
            b.dsp(gkv, v3(self.ar(A_KDUP, 8192, rows=64), 2048),
                  src[kv * 64:(kv + 1) * 64, :, :], waits=w_k)
            b.dsp(gkv, v3(self.AR[64:128, A_KDUP:A_KDUP + 8192], 2048),
                  src[kv * 64:(kv + 1) * 64, :, :])
            tok_kv = b.dsp(gkv, v3(self.ar(A_VFM, 8192, rows=64), 2048),
                           src[256 + kv * 64:256 + (kv + 1) * 64, :, :])
            tok_vaug, _ = self.vaug_build(A_VFM, A_VAUG, tok_kv)
            for hi in range(4):
                h = kv * 4 + hi
                wq1 = [kload_war] if kload_war is not None else []
                gqd = "qda%d" % (h % 2)
                b.dsp(gqd, self.ar(A_QDUP, 2048, rows=64),
                      self.qa_dram[h * 64:(h + 1) * 64, :],
                      waits=wq1 + tok_qa)
                tq = b.dsp(gqd, self.AR[64:128, A_QDUP:A_QDUP + 2048],
                           self.qa_dram[h * 64:(h + 1) * 64, :])
                for qt in range(2):
                    wl = [tok_kv, tq, tok_vaug, self.tok_consts]
                    par = (2 * h + qt) % 2

                    def sink(ctxt, tok, h=h, qt=qt, par=par):
                        return b.dgp(
                            "ctx%d" % par,
                            self.ctxa_dram[h * 64:(h + 1) * 64,
                                           qt * 1024:(qt + 1) * 1024],
                            ctxt, waits=[tok])

                    ctx_war, ctx_tok, kload_war = self.flash_head(
                        A_KDUP, A_QDUP, A_VAUG, par, qt * 1024,
                        wl, exp_plain, sink, ctx_war, self.ctx_store[par])
                    self.ctx_store[par] = ctx_tok
        g_c0, g_c1 = b.grp("ctx0"), b.grp("ctx1")
        tok_ctxall = [(g_c0, g_c0.n), (g_c1, g_c1.n)]
        tok_ctx = b.dsp("ctxld", v3(self.AR[:, OFF_XN:OFF_XN + 16384], 2048),
                        self.ctxa_dram.rearrange("(pt p) t -> p pt t", p=128),
                        waits=tok_ctxall + [kload_war])
        self.tap_dram("ctxa", self.ctxa_dram, waits=tok_ctxall)
        pa_free = None
        for oc in range(8):
            w = [tok_wo, tok_ctx]
            if pa_free is not None:
                w.append(pa_free)
            for ch in range(4):
                for dc in range(8):
                    lhsT = self.ar(A_WO + dc * 1024 + oc * 128, 128)
                    p(lambda e, ch=ch, dc=dc, lhsT=lhsT: e.matmul(
                        self.pA[:, ch * 512:(ch + 1) * 512], lhsT,
                        self.xnp(dc)[:, ch * 512:(ch + 1) * 512],
                        start=(dc == 0), stop=(dc == 7)),
                        waits=w if (ch == 0 and dc == 0) else ())
            tmm = b.pe_mark()
            gt2 = b.grp("tap2")
            d(lambda e, oc=oc: e.scalar_tensor_tensor(
                self.R[:, oc * 2048:(oc + 1) * 2048], self.pA[:, :],
                self.AB[:, 12 + oc:13 + oc],
                self.R[:, oc * 2048:(oc + 1) * 2048],
                op0=ALU.add, op1=ALU.add),
              waits=[tmm] + ([(gt2, gt2.n)] if (oc == 0 and self.dbg) else []),
              inc=True)
            pa_free = b.tdve()
        self.tok_oproj_pe = b.tpe()
        return b.tdve()

    def routing(self, tok_xn, fi, qstg_war=()):
        b = self.b
        MS, pb6, pB = self.MS, self.pb6, self.pB
        d, p = b.dve, b.pe
        sM = lambda a, bb: MS[:, a:bb]
        sv = self.MS[:, 0:16].rearrange("p (a b) -> p a b", b=2)
        t1v = self.MS[:, 32:40].rearrange("p (a b) -> p a b", b=1)
        t_exp_prev = None
        tr_evac = None
        qstg_war = list(qstg_war)
        grps = [b.grp(n) for n in ("ix0", "ix1", "ws0", "ws1",
                                    "xg0", "xg1", "xg2", "xg3")]
        scg = b.grp("sc")
        wzi = [self.tok_consts] + ([(g, g.n) for g in grps] + [(scg, scg.n)]
                                   if fi == 2 else [])
        b.dsp("zi", self.invmap_d.rearrange("(a bb) o -> a (bb o)", a=128),
              self.ZI32[:, :], waits=wzi)
        tok_zi = b.dsp("zi",
                       self.wcomp_d.rearrange("(a bb) o -> a (bb o)", a=128),
                       self.ZF32[:, :])
        b.pool(lambda e: e.iota(MS[0:1, 3600:3616], [[1, 16]], base=0,
                                channel_multiplier=0,
                                allow_small_or_imprecise_dtypes=True),
               waits=[tok_xn], inc=True)
        tbp = b.tpool()
        d(lambda e: e.tensor_scalar(MS[0:1, 3600:3616], MS[0:1, 3600:3616],
                                    float(CAP), None, op0=ALU.mult), waits=[tbp])
        d(lambda e: e.tensor_scalar(MS[0:1, 3584:3600], self.ONESF[0:1, 0:16],
                                    0.0, None, op0=ALU.mult), inc=True)
        tpb = b.tdve()
        p(lambda e: e.matmul(pB[:, 48:64], self.ONESF[0:1, 0:128],
                             MS[0:1, 3600:3616], start=True, stop=True),
          waits=[tpb])
        tec = b.pe_mark()
        d(lambda e: e.tensor_copy(MS[:, 3616:3632], pB[:, 48:64]), waits=[tec],
          inc=True)
        tpb = b.tdve()
        for tc in range(16):
            w = [self.tok_wg, tok_xn]
            if t_exp_prev is not None:
                w.append(t_exp_prev)
            for dc in range(8):
                p(lambda e, tc=tc, dc=dc: e.matmul(
                    pb6[:, 0:16], self.xnp(dc)[:, tc * 128:(tc + 1) * 128],
                    self.WGT[:, dc * 16:(dc + 1) * 16],
                    start=(dc == 0), stop=(dc == 7)),
                    waits=w if dc == 0 else ())
            tmm = b.pe_mark()
            b.act(lambda e: e.activation(sM(16, 32), pb6[:, 0:16], AF.Exp,
                                         bias=0.0, scale=-1.0),
                  waits=[tmm], inc=True)
            t_exp_prev = b.tact()
            d(lambda e: e.tensor_scalar(sM(16, 32), sM(16, 32), 1.0, None,
                                        op0=ALU.add), waits=[t_exp_prev])
            d(lambda e: e.reciprocal(sM(0, 16), sM(16, 32)))
            d(lambda e: e.tensor_max(t1v[:, :, 0:1], sv[:, :, 0:1], sv[:, :, 1:2]))
            gv = self.MS[:, 32:40].rearrange("p (a b) -> p a b", b=2)
            d(lambda e: e.tensor_max(
                self.MS[:, 40:44].rearrange("p (a b) -> p a b", b=1),
                gv[:, :, 0:1], gv[:, :, 1:2]))
            d(lambda e: e.reduce_max(sM(44, 45), sM(40, 44), axis=AXX))
            d(lambda e: e.tensor_scalar(sM(48, 52), sM(40, 44), MS[:, 44:45], None,
                                        op0=ALU.is_ge))
            d(lambda e: e.scalar_tensor_tensor(sM(52, 56), sM(48, 52), -2.0,
                                               sM(40, 44), op0=ALU.mult, op1=ALU.add))
            d(lambda e: e.reduce_max(sM(44, 45), sM(52, 56), axis=AXX))
            d(lambda e: e.tensor_scalar(sM(112, 116), sM(52, 56), MS[:, 44:45], None,
                                        op0=ALU.is_ge))
            d(lambda e: e.tensor_add(sM(56, 60), sM(48, 52), sM(112, 116)))
            for gi in range(4):
                d(lambda e, gi=gi: e.tensor_scalar(
                    sM(64 + 4 * gi, 68 + 4 * gi), self.CONE[:, 0:4],
                    MS[:, 56 + gi:57 + gi], None, op0=ALU.mult))
            d(lambda e: e.tensor_scalar(sM(80, 96), sM(0, 16), 1.0, None, op0=ALU.add))
            d(lambda e: e.tensor_mul(sM(80, 96), sM(80, 96), sM(64, 80)))
            d(lambda e: e.tensor_scalar(sM(80, 96), sM(80, 96), -1.0, None,
                                        op0=ALU.add))
            for k in range(4):
                d(lambda e: e.reduce_max(sM(44, 45), sM(80, 96), axis=AXX))
                d(lambda e: e.tensor_scalar(sM(112, 128), sM(80, 96), MS[:, 44:45],
                                            None, op0=ALU.is_ge))
                if k == 0:
                    d(lambda e: e.tensor_copy(sM(96, 112), sM(112, 128)))
                else:
                    d(lambda e: e.tensor_add(sM(96, 112), sM(96, 112), sM(112, 128)))
                if k < 3:
                    d(lambda e: e.scalar_tensor_tensor(
                        sM(80, 96), sM(112, 128), -3.0, sM(80, 96),
                        op0=ALU.mult, op1=ALU.add))
            d(lambda e: e.tensor_mul(sM(112, 128), sM(0, 16), sM(96, 112)))
            d(lambda e: e.reduce_sum(sM(44, 45), sM(112, 128), axis=AXX))
            d(lambda e: e.reciprocal(sM(45, 46), sM(44, 45)))
            d(lambda e, tc=tc: e.tensor_scalar(
                sM(256 + 16 * tc, 272 + 16 * tc), sM(112, 128),
                MS[:, 45:46], 2.5, op0=ALU.mult, op1=ALU.mult))
            wq = qstg_war if tc == 0 else []
            d(lambda e, tc=tc: e.tensor_copy(
                self.QSTG[:, tc * 16:(tc + 1) * 16],
                sM(256 + 16 * tc, 272 + 16 * tc)), waits=wq, inc=True)
            tstg = b.tdve()
            selb = self.QSTG[:, 256 + 16 * tc:272 + 16 * tc]
            d(lambda e, selb=selb: e.tensor_copy(selb, sM(96, 112)), inc=True)
            tselb = b.tdve()
            d(lambda e: e.tensor_copy(sM(512, 528), sM(96, 112)))
            for (ksh, sa, da_) in ((1, 512, 528), (2, 528, 512),
                                   (4, 512, 528), (8, 528, 512)):
                d(lambda e, k=ksh, sa=sa, da_=da_: e.tensor_copy(
                    MS[:, da_:da_ + k], MS[:, sa:sa + k]))
                d(lambda e, k=ksh, sa=sa, da_=da_: e.tensor_add(
                    MS[:, da_ + k:da_ + 16], MS[:, sa + k:sa + 16],
                    MS[:, sa:sa + 16 - k]))
            d(lambda e: e.tensor_sub(sM(544, 560), sM(512, 528), sM(96, 112)),
              inc=True)
            texcl = b.tdve()
            p(lambda e, selb=selb: e.matmul(pB[:, 0:16], self.LTRI[:, :], selb,
                                            start=True, stop=True),
              waits=[tselb, tpb])
            p(lambda e: e.matmul(pB[:, 16:32], self.ONESF[0:1, 0:128],
                                 MS[0:1, 3584:3600], start=True, stop=True))
            p(lambda e, selb=selb: e.matmul(pB[0:1, 32:48], self.ONESB[:, 0:1],
                                            selb, start=True, stop=True))
            trk = b.pe_mark()
            d(lambda e: e.tensor_copy(sM(576, 592), pB[:, 0:16]), waits=[trk])
            d(lambda e: e.tensor_add(sM(576, 592), sM(576, 592), pB[:, 16:32]))
            d(lambda e: e.tensor_scalar(sM(576, 592), sM(576, 592),
                                        float(CAP - 1), None, op0=ALU.min))
            d(lambda e: e.tensor_add(MS[0:1, 3584:3600], MS[0:1, 3584:3600],
                                     pB[0:1, 32:48]), inc=True)
            tpb = b.tdve()
            d(lambda e: e.tensor_scalar(sM(600, 616), sM(576, 592),
                                        1.0 / 128.0, -63.5 / 128.0,
                                        op0=ALU.mult, op1=ALU.add))
            d(lambda e: e.tensor_scalar(sM(600, 616), sM(600, 616), MAGIC, None,
                                        op0=ALU.add))
            d(lambda e: e.tensor_scalar(sM(600, 616), sM(600, 616), -MAGIC, None,
                                        op0=ALU.add))
            d(lambda e: e.scalar_tensor_tensor(sM(616, 632), sM(600, 616),
                                               -128.0, sM(576, 592),
                                               op0=ALU.mult, op1=ALU.add))
            d(lambda e: e.scalar_tensor_tensor(sM(576, 592), sM(616, 632),
                                               float(NCH), sM(600, 616),
                                               op0=ALU.mult, op1=ALU.add))
            d(lambda e: e.tensor_add(sM(576, 592), sM(576, 592),
                                     MS[:, 3616:3632]))
            for s in range(4):
                d(lambda e, s=s: e.tensor_scalar(sM(560, 576), sM(544, 560),
                                                 float(s), None,
                                                 op0=ALU.is_equal),
                  waits=[texcl] if s == 0 else ())
                d(lambda e: e.tensor_mul(sM(560, 576), sM(560, 576), sM(96, 112)))
                d(lambda e: e.tensor_mul(sM(648, 664), sM(560, 576), sM(576, 592)))
                d(lambda e, s=s: e.reduce_sum(MS[:, 592 + s:593 + s], sM(648, 664),
                                              axis=AXX))
                d(lambda e, tc=tc: e.tensor_mul(sM(648, 664), sM(560, 576),
                                                sM(256 + 16 * tc, 272 + 16 * tc)))
                d(lambda e, s=s, tc=tc: e.reduce_sum(
                    MS[:, 3200 + 4 * tc + s:3201 + 4 * tc + s], sM(648, 664),
                    axis=AXX))
            d(lambda e, tc=tc: e.tensor_scalar(
                self.SLOTI[:, 4 * tc:4 * tc + 4], MS[:, 592:596], 1.0, None,
                op0=ALU.mult), inc=True)
            tslot = b.tdve()
            for s in range(4):
                w = [tslot] if s == 0 else []
                if tc == 0 and s == 0:
                    ws0, ws1 = b.grp("ws0"), b.grp("ws1")
                    w = [tslot, self.tok_tokid, tok_zi,
                         (ws0, ws0.n), (ws1, ws1.n)]
                b.dsis("sc", self.invmap_d,
                       self.SLOTI[:, 4 * tc + s:4 * tc + s + 1],
                       self.TOKID[:, tc:tc + 1], waits=w)
                b.dsis("sc", self.wcomp_d,
                       self.SLOTI[:, 4 * tc + s:4 * tc + s + 1],
                       MS[:, 3200 + 4 * tc + s:3201 + 4 * tc + s])
            wtr = [tstg, self.tok_consts]
            if tc == 0:
                wtr.append(self.tok_pl_ev)
            if tr_evac is not None:
                wtr.append(tr_evac)
            p(lambda e, tc=tc: e.transpose(
                self.ptr[0:16, (tc % 8) * 128:((tc % 8) + 1) * 128],
                self.QSTG[:, tc * 16:(tc + 1) * 16], self.IDENT[:, :]),
                waits=wtr)
            ttr = b.pe_mark()
            if tc % 8 == 7:
                d(lambda e, tc=tc: e.tensor_copy(
                    self.WET[0:16, (tc // 8) * 1024:(tc // 8 + 1) * 1024],
                    self.ptr[0:16, 0:1024]), waits=[ttr], inc=True)
                tr_evac = b.tdve()
        self.tap_dram("wfull", self.MS[:, 256:512], waits=[b.tdve()])
        self.tok_wet = tr_evac

    def payload(self, fi, tok_xn, ptr_war, qstg_war=()):
        b = self.b
        p = b.pe
        xstg = self.QSTG[:, 1024:2048]
        plg = b.grp("pl")
        pl_tok = None
        ev_prev = None
        for tc in range(16):
            for j in range(8):
                w = ()
                if j == 0:
                    w = [tok_xn] + ptr_war if tc == 0 else [ev_prev]
                p(lambda e, tc=tc, j=j: e.transpose(
                    self.ptr[:, j * 128:(j + 1) * 128],
                    self.xnp(j)[:, tc * 128:(tc + 1) * 128], self.IDENT[:, :]),
                  waits=w)
            tmm = b.pe_mark()
            wev = [tmm]
            if tc == 0:
                wev += list(qstg_war)
            if pl_tok is not None:
                wev.append(pl_tok)
            b.act(lambda e, xstg=xstg: e.activation(
                xstg, self.ptr[:, :], AF.Identity, bias=0.0, scale=1.0),
                waits=wev, inc=True, drain=False)
            ev_prev = b.tact()
            pl_tok = b.dsp("pl", self.xtok_d[tc * 128:(tc + 1) * 128, :], xstg,
                           waits=[ev_prev])
        self.tok_pl = pl_tok
        self.tok_pl_ev = ev_prev

    def ffn(self, tok_xn, fi, qstg_war=()):
        b = self.b
        MS, pB, pA = self.MS, self.pB, self.pA
        d, p = b.dve, b.pe
        ptr_war = [self.ptr_last] if self.ptr_last is not None else []
        self.payload(fi, tok_xn, ptr_war, qstg_war)
        self.routing(tok_xn, fi, qstg_war)
        wsrc = lambda w, e_: w[e_].rearrange("(dc p) i -> p dc i", p=128)
        g13 = lambda e_: "w13_%d" % (e_ % 2)
        g2 = lambda e_: "w2_%d" % (e_ % 2)
        war = [b.tpe()]
        b.dgp(g13(0), v3(self.ar(WB0, 8192), 1024), wsrc(self.w1T, 0), waits=war)
        tok_w13 = b.dgp(g13(0), v3(self.ar(WB1, 8192), 1024), wsrc(self.w3T, 0))
        tok_w2 = b.dgp(g2(0), v3(self.ar(WB2, 8192), 1024), wsrc(self.w2T, 0))
        gt = b.grp("tap")
        tap_war = [(gt, gt.n)] if self.dbg else []
        gt2 = b.grp("tap2")
        rtap_war = [(gt2, gt2.n)] if self.dbg else []
        first_radd = [True]
        scg = b.grp("sc")
        plg = b.grp("pl")
        ysg = [b.grp("ys0"), b.grp("ys1")]
        h_free = [None, None]
        silu_free = [None, None]
        h_last = [None, None]
        hwar = [None, None]
        pb_free = [None, None]
        ystg_war = [None, None]
        ring_pl = [False, True]
        ystg_ring = [0]
        tr_done = [None, None, None, None]
        tpe_entry = b.tpe()
        ptr_free = [self.tok_wet]
        winready = {}
        awin_pe = {}
        idx_tok = {}
        wst_tok = {}
        gather_last = {}
        prev_b_dve = [None]

        def emit_idx_we(e_):
            par = e_ % 2
            xgs = [b.grp("xg%d" % i) for i in range(4)]
            w = []
            if e_ == 0:
                w.append((scg, scg.n))
            else:
                w += [(g, g.n) for g in xgs]
            if prev_b_dve[0] is not None:
                w.append(prev_b_dve[0])
            idx_tok[e_] = b.dsp(
                "ix%d" % par, self.IDXS[:, par * NCH:(par + 1) * NCH],
                self.invmap_d[e_ * CAP:(e_ + 1) * CAP, :].rearrange(
                    "(a bb) o -> a (bb o)", bb=NCH), waits=w)
            wst_tok[e_] = b.dsp(
                "ws%d" % par, self.WSTR[:, par * NCH:(par + 1) * NCH],
                self.wcomp_d[e_ * CAP:(e_ + 1) * CAP, :].rearrange(
                    "(a bb) o -> a (bb o)", bb=NCH))

        gtok = {}

        def ring_slot(rc):
            if rc < 2:
                return self.ar(OFF_SC + 1024 + rc * 1024, 1024)
            return self.QRAW[:, (rc - 2) * 1024:(rc - 1) * 1024]

        def emit_gather(e_, c8):
            par = e_ % 2
            rc = c8 % 4
            w = []
            if c8 == 0:
                w.append(idx_tok[e_])
                if e_ == 0:
                    w.append(self.tok_pl)
            if e_ == 0 and c8 in (2, 3):
                w.append(tpe_entry)
            if tr_done[rc] is not None:
                w.append(tr_done[rc])
            gtok[(e_, c8)] = b.dsi(
                "xg%d" % rc, ring_slot(rc),
                self.xtok_d,
                self.IDXS[:, par * NCH + c8:par * NCH + c8 + 1], waits=w)

        def emit_tr_chunk(e_, c8, gtok):
            wi = CHWIN[c8]
            gw = e_ * NWIN + wi
            wslot = gw % 3
            cc = c8 - WINS[wi][0] // 128
            rc = c8 % 4
            src = ring_slot(rc)
            for j in range(8):
                w = [gtok, ptr_free[0]] if j == 0 else ()
                p(lambda e, j=j, src=src: e.transpose(
                    self.ptr[:, j * 128:(j + 1) * 128],
                    src[:, j * 128:(j + 1) * 128], self.IDENT[:, :]), waits=w)
            tmm = b.pe_mark()
            tr_done[rc] = tmm
            dst = self.ar(OFF_XN + wslot * 4096, 4096).rearrange(
                "p (a bb) -> p a bb", bb=512)[:, :, cc * 128:(cc + 1) * 128]
            wev = [tmm]
            if cc == 0 and (gw - 3) in awin_pe:
                wev.append(awin_pe[gw - 3])
            b.act(lambda e, dst=dst: e.activation(
                dst, v3(self.ptr[:, :], 128), AF.Identity, bias=0.0, scale=1.0),
                waits=wev, inc=True, drain=False)
            ptr_free[0] = b.tact()
            if c8 in (3, 7, NCH - 1):
                winready[gw] = b.tact()

        def a_window(e_, wi):
            ew = e_ * 8
            if True:
                ws_, wd = WINS[wi]
                gw = e_ * NWIN + wi
                hh = gw % 2
                for ic in range(8):
                    par = ic % 2
                    base = par * 1024
                    w = [tok_w13, winready[gw]] + tap_war
                    if h_free[par] is not None:
                        w.append(h_free[par])
                    if ic == 0 and hwar[hh] is not None:
                        w.append(hwar[hh])
                    for m3, woff in ((0, WB0), (1, WB1)):
                        for dc in range(8):
                            lhsT = self.ar(woff + dc * 1024 + ic * 128, 128)
                            rhs = self.ar(OFF_XN + (gw % 3) * 4096 + dc * 512, wd)
                            p(lambda e, base=base, m3=m3, lhsT=lhsT, rhs=rhs,
                              dc=dc, wd=wd: e.matmul(
                                  self.pA[:, base + m3 * 512:base + m3 * 512 + wd],
                                  lhsT, rhs, start=(dc == 0), stop=(dc == 7)),
                              waits=w if (m3 == 0 and dc == 0) else ())
                    tA = b.pe_mark()
                    hdst = self.ar(OFF_TAB + ic * 1024 + hh * 512, wd)
                    silu_bf = self.ar(OFF_SC + par * 512, wd)
                    ws2 = [tA]
                    if silu_free[par] is not None:
                        ws2.append(silu_free[par])
                    b.act(lambda e, ic=ic, ew=ew, base=base, silu_bf=silu_bf,
                          wd=wd: e.activation(
                              silu_bf, self.pA[:, base:base + wd], AF.Silu,
                              bias=self.B1[:, ew + ic:ew + ic + 1], scale=1.0),
                          waits=ws2, drain=False)
                    b.act(lambda e, ic=ic, ew=ew, base=base, hdst=hdst,
                          wd=wd: e.activation(
                              hdst, self.pA[:, base + 512:base + 512 + wd],
                              AF.Identity,
                              bias=self.B3[:, ew + ic:ew + ic + 1], scale=1.0),
                          inc=True, drain=False)
                    tsl = b.tact()
                    d(lambda e, hdst=hdst, silu_bf=silu_bf: e.tensor_mul(
                        hdst, hdst, silu_bf), waits=[tsl], inc=True)
                    h_free[par] = tsl
                    silu_free[par] = b.tdve()
                    h_last[hh] = b.tdve()
                awin_pe[gw] = b.tpe()

        def b_phase(e_, wi, pend_tr):
            gw = e_ * NWIN + wi
            hh = gw % 2
            ws_, wd = WINS[wi]
            for cc in range(wd // 128):
                c8 = ws_ // 128 + cc
                ring = ystg_ring[0] % 2
                for dh in (0, 1):
                    w = [h_last[hh], tok_w2]
                    if pb_free[dh] is not None:
                        w.append(pb_free[dh])
                    for k in range(8):
                        lhsT = self.ar(OFF_TAB + k * 1024 + hh * 512 + cc * 128,
                                       128)
                        rhs = self.ar(WB2 + k * 1024 + dh * 512, 512)
                        p(lambda e, dh=dh, lhsT=lhsT, rhs=rhs, k=k: e.matmul(
                            self.pB[:, dh * 512:(dh + 1) * 512], lhsT, rhs,
                            start=(k == 0), stop=(k == 7)),
                          waits=w if k == 0 else ())
                    tB = b.pe_mark()
                    wev = [tB]
                    if ystg_war[ring] is not None:
                        wev.append(ystg_war[ring])
                    if ring_pl[ring]:
                        wev.append((plg, plg.n))
                        ring_pl[ring] = False
                    if wi == 0 and cc == 0 and dh == 0:
                        wev.append(wst_tok[e_])
                    wecol = self.WSTR[:, (e_ % 2) * NCH + c8:
                                      (e_ % 2) * NCH + c8 + 1]
                    d(lambda e, ring=ring, dh=dh, wecol=wecol: e.tensor_scalar(
                        self.QSTG[:, ring * 1024 + dh * 512:
                                  ring * 1024 + dh * 512 + 512],
                        self.pB[:, dh * 512:(dh + 1) * 512], wecol, None,
                        op0=ALU.mult), waits=wev, inc=True)
                    pb_free[dh] = b.tdve()
                ydst = self.ycomp_d[e_ * CAP:(e_ + 1) * CAP, :].rearrange(
                    "(a bb) o -> a bb o", bb=NCH)[:, c8:c8 + 1, :]
                ystg_war[ring] = b.dsp(
                    "ys%d" % ring, ydst,
                    v3(self.QSTG[:, ring * 1024:ring * 1024 + 1024], 1024),
                    waits=[pb_free[1] if pb_free[1] is not None else pb_free[0]])
                ystg_ring[0] += 1
                if pend_tr:
                    en, c8n = pend_tr.pop(0)
                    emit_tr_chunk(en, c8n, gtok[(en, c8n)])
                    if c8n + 4 < NCH:
                        emit_gather(en, c8n + 4)
            hwar[hh] = b.tpe()

        emit_idx_we(0)
        for c8 in range(4):
            emit_gather(0, c8)
        for c8 in range(NCH):
            emit_tr_chunk(0, c8, gtok[(0, c8)])
            if c8 + 4 < NCH:
                emit_gather(0, c8 + 4)
        for e_ in range(E):
            a_window(e_, 0)
            pend_tr = []
            if e_ + 1 < E:
                emit_idx_we(e_ + 1)
                for c8 in range(4):
                    emit_gather(e_ + 1, c8)
                pend_tr = [(e_ + 1, c8) for c8 in range(NCH)]
            for wi in range(NWIN):
                if wi + 1 < NWIN:
                    a_window(e_, wi + 1)
                if wi == 1 and e_ + 1 < E:
                    wlast = [awin_pe[e_ * NWIN + 2]]
                    b.dgp(g13(e_ + 1), v3(self.ar(WB0, 8192), 1024),
                          wsrc(self.w1T, e_ + 1), waits=wlast)
                    tok_w13_next = b.dgp(g13(e_ + 1),
                                         v3(self.ar(WB1, 8192), 1024),
                                         wsrc(self.w3T, e_ + 1))
                b_phase(e_, wi, pend_tr)
            assert not pend_tr
            prev_b_dve[0] = b.tdve()
            if e_ + 1 < E:
                tok_w2 = b.dgp(g2(e_ + 1), v3(self.ar(WB2, 8192), 1024),
                               wsrc(self.w2T, e_ + 1), waits=[b.tpe()])
                tok_w13 = tok_w13_next
        self.tok_wb01_free = awin_pe[(E - 1) * NWIN + NWIN - 1]
        fold_free = [None, None]
        for dc in range(8):
            for t2 in range(4):
                par = (dc * 4 + t2) % 2
                w = []
                if dc == 0 and t2 == 0:
                    w = [self.tok_wet, self.tok_wg, pb_free[0], pb_free[1]]
                if fold_free[par] is not None:
                    w = list(w) + [fold_free[par]]
                p(lambda e, dc=dc, t2=t2, par=par: e.matmul(
                    self.pB[:, par * 512:(par + 1) * 512],
                    self.B2S[0:16, dc * 128:(dc + 1) * 128],
                    self.WET[0:16, t2 * 512:(t2 + 1) * 512],
                    start=True, stop=True), waits=w)
                tf = b.pe_mark()
                wadd = [tf] + (rtap_war if first_radd[0] else [])
                first_radd[0] = False
                d(lambda e, dc=dc, t2=t2, par=par: e.tensor_add(
                    self.R[:, dc * 2048 + t2 * 512:dc * 2048 + (t2 + 1) * 512],
                    self.R[:, dc * 2048 + t2 * 512:dc * 2048 + (t2 + 1) * 512],
                    self.pB[:, par * 512:(par + 1) * 512]), waits=wadd, inc=True)
                fold_free[par] = b.tdve()
        awin15 = b.tpe()
        ygg = [b.grp("yg0"), b.grp("yg1")]
        ring_war = [None, None]
        radd_prev = None
        for tc in range(16):
            par = tc % 2
            w0 = []
            if tc == 0:
                w0 = [(ysg[0], ysg[0].n), (ysg[1], ysg[1].n), awin15]
            if ring_war[par] is not None:
                w0.append(ring_war[par])
            yt = None
            for s in range(4):
                yt = b.dsi("yg%d" % par,
                           self.ar(OFF_XN + par * 4096 + s * 1024, 1024),
                           self.ycomp_d,
                           self.SLOTI[:, 4 * tc + s:4 * tc + s + 1],
                           waits=w0 if s == 0 else ())
            ya = lambda s, par=par: self.ar(OFF_XN + par * 4096 + s * 1024, 1024)
            d(lambda e, ya=ya: e.tensor_add(self.ar(OFF_SC + 1024, 1024),
                                            ya(0), ya(1)), waits=[yt])
            d(lambda e, ya=ya: e.tensor_add(self.ar(OFF_SC + 2048, 1024),
                                            ya(2), ya(3)))
            d(lambda e, par=par: e.tensor_add(
                self.QSTG[:, par * 1024:(par + 1) * 1024],
                self.ar(OFF_SC + 1024, 1024), self.ar(OFF_SC + 2048, 1024)),
              inc=True)
            ysum_t = b.tdve()
            ring_war[par] = ysum_t
            for j in range(8):
                w = [ysum_t, ptr_free[0]] if j == 0 else ()
                p(lambda e, par=par, j=j: e.transpose(
                    self.ptr[:, j * 128:(j + 1) * 128],
                    self.QSTG[:, par * 1024 + j * 128:par * 1024 + (j + 1) * 128],
                    self.IDENT[:, :]), waits=w)
            tmm = b.pe_mark()
            for j in range(8):
                d(lambda e, j=j, tc=tc: e.tensor_add(
                    self.R[:, j * 2048 + tc * 128:j * 2048 + (tc + 1) * 128],
                    self.R[:, j * 2048 + tc * 128:j * 2048 + (tc + 1) * 128],
                    self.ptr[:, j * 128:(j + 1) * 128]),
                  waits=[tmm] if j == 0 else (), inc=(j == 7))
            radd_prev = b.tdve()
            ptr_free[0] = radd_prev
        self.tok_ffn_pe = b.tpe()
        self.ptr_last = radd_prev
        return b.tdve()

    def attn_b(self, sub=3):
        b = self.b
        MS = self.MS
        d, p = b.dve, b.pe
        tok_q = b.dgp("qkvq", v3(self.ar(WB0, 8192), 1024),
                      self.qkvT[:, 0:1024].rearrange("(dc p) o -> p dc o", p=128),
                      waits=[self.tok_wb01_free])
        tok_k = b.dgp("qkvk", v3(self.ar(WB1, 8192), 1024),
                      self.qkvT[:, 1024:2048].rearrange("(dc p) o -> p dc o", p=128),
                      waits=[self.tok_wb01_free])

        def mk_sink(base):
            def sink(oc, tile, tok):
                h1, h2 = 2 * oc, 2 * oc + 1
                r1 = (h1 % 4) * 768 + base + (h1 // 4) * 64
                r2 = (h2 % 4) * 768 + base + (h2 // 4) * 64
                b.dsp("agb", self.ag_b_in[r1:r1 + 64, :], tile[0:64, :],
                      waits=[tok])
                return b.dsp("agb", self.ag_b_in[r2:r2 + 64, :], tile[64:128, :])
            return sink

        rope_q = (self.PERMB[:, :], OFF_TAB + 4096, OFF_TAB + 6144)
        rope_k = (self.PERMB[:, :], OFF_TAB, OFF_TAB + 2048)
        tq_pe = self.proj(WB0, 1024, [(oc, None) for oc in range(8)], rope_q,
                          mk_sink(0),
                          extra_wait=[tok_q, self.tok_xn3, self.tok_tab])
        tok_v = b.dgp("qkvw2", v3(self.ar(WB0, 8192), 1024),
                      self.qkvT[:, 2048:3072].rearrange("(dc p) o -> p dc o", p=128),
                      waits=[tq_pe])
        gab = b.grp("agb")
        self.proj(WB1, 1024, [(oc, None) for oc in range(8)], rope_k,
                  mk_sink(256), extra_wait=[tok_k],
                  stg_war=[(gab, gab.n)])
        self.proj(WB0, 1024, [(oc, None) for oc in range(8)], None,
                  mk_sink(512), extra_wait=[tok_v], stg_war=[(gab, gab.n)])
        g_agb = b.grp("agb")
        tok_contrib = (g_agb, g_agb.n)
        self.tap_dram("agb", self.ag_b_in, waits=[tok_contrib])
        tok_agb = b.coll("b", [[0, 1, 2, 3], [4, 5, 6, 7]],
                         [self.ag_b_in], [self.ag_b_out], waits=[tok_contrib])
        if sub < 2:
            self.tok_oprojb_pe = b.tpe()
            return b.tdve()
        EI = self.ar(B_EI, 1024)
        kload_war = None
        ctx_war = None
        self.ctx_store = [None, None]
        it = 0
        for hh in range(4):
            wk = [tok_agb]
            if kload_war is not None:
                wk.append(kload_war)
            ghb = "hb%d" % (hh % 2)
            b.dgi(ghb, v3(self.AR[:, B_KDUP:B_KDUP + 8192], 2048)[:, 0:1, :][:, 0, :]
                  if False else self.AR[:, B_KDUP + 0 * 2048:B_KDUP + 2048],
                  self.ag_b_out, self.IDX[:, hh * 4:hh * 4 + 1], waits=wk)
            for r in range(1, 4):
                b.dgi(ghb, self.AR[:, B_KDUP + r * 2048:B_KDUP + (r + 1) * 2048],
                      self.ag_b_out, self.IDX[:, hh * 4 + r:hh * 4 + r + 1])
            for r in range(4):
                b.dgi(ghb, self.AR[:, B_QDUP + r * 2048:B_QDUP + (r + 1) * 2048],
                      self.ag_b_out,
                      self.IDX[:, 16 + hh * 4 + r:17 + hh * 4 + r])
            for r in range(4):
                tok_load = b.dgi(
                    ghb, self.AR[0:64, B_VFM + r * 2048:B_VFM + (r + 1) * 2048],
                    self.ag_b_out, self.IDX[0:64, 32 + hh * 4 + r:33 + hh * 4 + r])
            tok_vaug, _ = self.vaug_build(B_VFM, B_VAUG, tok_load)
            b.pool(lambda e: e.iota(MS[64:65, 0:1024], [[1, 1024]], base=0,
                                    channel_multiplier=0,
                                    allow_small_or_imprecise_dtypes=True),
                   waits=[b.tpe(), b.tdve()], inc=True)
            tio = b.tpool()
            b.act(lambda e, hh=hh: e.activation(
                MS[64:65, 0:1024], MS[64:65, 0:1024], AF.Exp, bias=0.0,
                scale=self.SLP[64:65, 8 + hh:9 + hh]),
                waits=[tio, self.tok_slp], inc=True)
            tei = b.tact()
            tev = None
            for half in range(2):
                web = [tei, b.tdve()]
                if tev is not None:
                    web.append(tev)
                p(lambda e, half=half: e.matmul(
                    self.pb6[:, :], self.ONESF[64:65, :],
                    MS[64:65, half * 512:(half + 1) * 512],
                    start=True, stop=True), waits=web)
                tbm = b.pe_mark()
                d(lambda e, half=half: e.tensor_copy(
                    EI[:, half * 512:(half + 1) * 512], self.pb6[:, :]),
                  waits=[tbm], inc=True)
                tev = b.tdve()
            d(lambda e, hh=hh: e.tensor_scalar(
                MS[:, 3200:3327], self.PF[:, 0:127], 0.0,
                self.SLP[:, 4 + hh:5 + hh], op0=ALU.min, op1=ALU.mult))
            d(lambda e, hh=hh: e.tensor_scalar(
                MS[:, 3330:3331], self.IOC[:, 0:1], -127.0,
                self.SLP[:, 4 + hh:5 + hh], op0=ALU.add, op1=ALU.mult))
            d(lambda e: e.tensor_scalar(MS[:, 3331:3332], MS[:, 3330:3331], -BEXP,
                                        None, op0=ALU.add), inc=True)
            tok_hbias = b.tdve()
            s8_min = 8.0 * (0.7071067811865476 ** (4 + 4 * hh))
            wcap = (BEXP + 150.0) / s8_min
            for qt in range(8):
                Q0 = 1024 * qt
                chunks = [tc for tc in range(64)
                          if tc * 128 + 128 > Q0 - wcap or tc * 128 + 128 > Q0]
                d(lambda e, Q0=Q0: e.tensor_scalar(
                    MS[:, 3072:3136], self.JG[:, :], -float(Q0), None, op0=ALU.add),
                  waits=[b.tact()])
                d(lambda e, hh=hh: e.tensor_scalar(
                    MS[:, 3136:3200], MS[:, 3072:3136],
                    self.SLP[:, 4 + hh:5 + hh], -BEXP, op0=ALU.mult, op1=ALU.add),
                  inc=True)
                tok_cols = b.tdve()

                def expb(sc, wpe, wav, ubuf, Q0=Q0, tok_cols=tok_cols):
                    base_w = [wpe, tok_cols] + ([wav] if wav is not None else [])
                    tdve = None
                    ci = 0
                    while ci < 2:
                        tc = 2 * sc + ci
                        Bu = tc * 128
                        U = self.ar(ubuf, 2048)
                        if Bu >= Q0 + 1024:
                            if ci == 0 and (2 * sc + 1) * 128 >= Q0 + 1024:
                                b.act(lambda e, U=U: e.activation(
                                    U[:, 0:2048], self.pA[:, :], AF.Exp,
                                    bias=-BEXP, scale=1.0),
                                    waits=base_w, inc=True)
                                ci = 2
                                continue
                            b.act(lambda e, U=U, ci=ci: e.activation(
                                U[:, ci * 1024:(ci + 1) * 1024],
                                self.pA[:, ci * 1024:(ci + 1) * 1024], AF.Exp,
                                bias=-BEXP, scale=1.0), waits=base_w, inc=True)
                        elif Bu + 128 <= Q0:
                            b.act(lambda e, U=U, ci=ci, tc=tc: e.activation(
                                U[:, ci * 1024:(ci + 1) * 1024],
                                self.pA[:, ci * 1024:(ci + 1) * 1024], AF.Exp,
                                bias=MS[:, 3136 + tc:3137 + tc], scale=1.0),
                                waits=base_w, inc=True)
                            tpen = b.tact()
                            d(lambda e, U=U, ci=ci: e.tensor_mul(
                                U[:, ci * 1024:(ci + 1) * 1024],
                                U[:, ci * 1024:(ci + 1) * 1024], EI[:, :]),
                              waits=[tpen], inc=True)
                            tdve = b.tdve()
                        else:
                            fz = Bu - Q0
                            if fz > 0:
                                b.act(lambda e, U=U, ci=ci, fz=fz: e.activation(
                                    U[:, ci * 1024:ci * 1024 + fz],
                                    self.pA[:, ci * 1024:ci * 1024 + fz], AF.Exp,
                                    bias=-BEXP, scale=1.0), waits=base_w, inc=True)
                            sc0 = 3456 + ci * 128
                            d(lambda e, ci=ci, fz=fz, sc0=sc0: e.tensor_add(
                                MS[:, sc0:sc0 + 127],
                                self.pA[:, ci * 1024 + fz:ci * 1024 + fz + 127],
                                MS[:, 3200:3327]),
                              waits=base_w + [b.tact()], inc=True)
                            tband = b.tdve()
                            b.act(lambda e, U=U, ci=ci, fz=fz, sc0=sc0: e.activation(
                                U[:, ci * 1024 + fz:ci * 1024 + fz + 127],
                                MS[:, sc0:sc0 + 127], AF.Exp, bias=-BEXP, scale=1.0),
                                waits=[tband] + base_w, inc=True)
                            ps_ = fz + 127
                            if ps_ < 1024:
                                b.act(lambda e, U=U, ci=ci, ps_=ps_: e.activation(
                                    U[:, ci * 1024 + ps_:(ci + 1) * 1024],
                                    self.pA[:, ci * 1024 + ps_:(ci + 1) * 1024],
                                    AF.Exp, bias=MS[:, 3331:3332], scale=1.0),
                                    waits=base_w, inc=True)
                                tps = b.tact()
                                d(lambda e, U=U, ci=ci, ps_=ps_: e.tensor_mul(
                                    U[:, ci * 1024 + ps_:(ci + 1) * 1024],
                                    U[:, ci * 1024 + ps_:(ci + 1) * 1024],
                                    EI[:, 0:1024 - ps_]),
                                  waits=[tps], inc=True)
                                tdve = b.tdve()
                        ci += 1
                    return b.tact(), tdve

                wl = [tok_load, tok_vaug, tok_hbias, tev, self.tok_consts]
                par = it % 2
                it += 1

                def sinkb(ctxt, tok, hh=hh, qt=qt, par=par):
                    return b.dgp(
                        "bctx%d" % par,
                        self.ag_c_in[(qt // 2) * 256 + hh * 64:
                                     (qt // 2) * 256 + hh * 64 + 64,
                                     (qt % 2) * 1024:((qt % 2) + 1) * 1024],
                        ctxt, waits=[tok])

                def expb(tc, wpe, wav, ubuf, half, Q0=Q0, tok_cols=tok_cols):
                    base_w = [wpe, tok_cols] + ([wav] if wav is not None else [])
                    tdve = None
                    Bu = tc * 128
                    U = self.ar(ubuf, 1024)
                    pa_h = self.pA[:, half * 1024:(half + 1) * 1024]
                    if Bu >= Q0 + 1024:
                        b.act(lambda e, U=U, pa_h=pa_h: e.activation(
                            U[:, :], pa_h, AF.Exp, bias=-BEXP, scale=1.0),
                            waits=base_w, inc=True, drain=False)
                    elif Bu + 128 <= Q0:
                        b.act(lambda e, U=U, pa_h=pa_h, tc=tc: e.activation(
                            U[:, :], pa_h, AF.Exp,
                            bias=MS[:, 3136 + tc:3137 + tc], scale=1.0),
                            waits=base_w, inc=True, drain=False)
                        tpen = b.tact()
                        d(lambda e, U=U: e.tensor_mul(U[:, :], U[:, :], EI[:, :]),
                          waits=[tpen], inc=True)
                        tdve = b.tdve()
                    else:
                        fz = Bu - Q0
                        if fz > 0:
                            b.act(lambda e, U=U, pa_h=pa_h, fz=fz: e.activation(
                                U[:, 0:fz], pa_h[:, 0:fz], AF.Exp, bias=-BEXP,
                                scale=1.0), waits=base_w, inc=True, drain=False)
                        sc0 = 3456 + half * 128
                        d(lambda e, pa_h=pa_h, fz=fz, sc0=sc0: e.tensor_add(
                            MS[:, sc0:sc0 + 127], pa_h[:, fz:fz + 127],
                            MS[:, 3200:3327]), waits=base_w + [b.tact()], inc=True)
                        tband = b.tdve()
                        b.act(lambda e, U=U, fz=fz, sc0=sc0: e.activation(
                            U[:, fz:fz + 127], MS[:, sc0:sc0 + 127], AF.Exp,
                            bias=-BEXP, scale=1.0), waits=[tband] + base_w, inc=True)
                        ps_ = fz + 127
                        if ps_ < 1024:
                            b.act(lambda e, U=U, pa_h=pa_h, ps_=ps_: e.activation(
                                U[:, ps_:1024], pa_h[:, ps_:1024], AF.Exp,
                                bias=MS[:, 3331:3332], scale=1.0),
                                waits=base_w, inc=True, drain=False)
                            tps = b.tact()
                            d(lambda e, U=U, ps_=ps_: e.tensor_mul(
                                U[:, ps_:1024], U[:, ps_:1024],
                                EI[:, 0:1024 - ps_]), waits=[tps], inc=True)
                            tdve = b.tdve()
                    return b.tact(), tdve

                ctx_war, ctx_tok, kload_war = self.flash_head(
                    B_KDUP, B_QDUP, B_VAUG, par, Q0, wl, expb, sinkb,
                    ctx_war, self.ctx_store[par], chunks=chunks)
                self.ctx_store[par] = ctx_tok
        g_b0, g_b1 = b.grp("bctx0"), b.grp("bctx1")
        tok_bctx = [(g_b0, g_b0.n), (g_b1, g_b1.n)]
        self.tap_dram("ctxb", self.ag_c_in, waits=tok_bctx)
        tok_agc = b.coll("c", [[0, 1, 2, 3], [4, 5, 6, 7]],
                         [self.ag_c_in], [self.ag_c_out], waits=tok_bctx)
        if sub < 3:
            self.tok_oprojb_pe = b.tpe()
            return b.tdve()
        tok_proj = b.dgp("pw", v3(self.ar(WB0, 8192), 1024),
                         self.projT.rearrange("(dc p) o -> p dc o", p=128),
                         waits=[kload_war])
        for pt in range(8):
            wg = [tok_agc, kload_war] if pt == 0 else ()
            tok_ctx = b.dgi("cg", self.AR[:, OFF_XN + pt * 2048:
                                          OFF_XN + (pt + 1) * 2048],
                            self.ag_c_out, self.IDX[:, 48 + pt:49 + pt], waits=wg)
        pa_free = None
        for oc in range(8):
            w = [tok_proj, tok_ctx]
            if pa_free is not None:
                w.append(pa_free)
            for ch in range(4):
                for dc in range(8):
                    lhsT = self.ar(WB0 + dc * 1024 + oc * 128, 128)
                    p(lambda e, ch=ch, dc=dc, lhsT=lhsT: e.matmul(
                        self.pA[:, ch * 512:(ch + 1) * 512], lhsT,
                        self.xnp(dc)[:, ch * 512:(ch + 1) * 512],
                        start=(dc == 0), stop=(dc == 7)),
                        waits=w if (ch == 0 and dc == 0) else ())
            tmm = b.pe_mark()
            gt2b = b.grp("tap2")
            d(lambda e, oc=oc: e.scalar_tensor_tensor(
                self.R[:, oc * 2048:(oc + 1) * 2048], self.pA[:, :],
                self.PBc[:, oc:oc + 1], self.R[:, oc * 2048:(oc + 1) * 2048],
                op0=ALU.add, op1=ALU.add),
              waits=[tmm] + ([(gt2b, gt2b.n)] if (oc == 0 and self.dbg) else []),
              inc=True)
            pa_free = b.tdve()
        self.tok_oprojb_pe = b.tpe()
        return b.tdve()

    def build(self):
        st = self.stage
        b = self.b
        self.preamble()
        self.build_tables()
        self.tok_xn1 = self.layernorm(WB1, first=True)
        self.tok_ln1_pe = self.last_ln_pe
        if st >= 1:
            self.tap_sbuf("xn1", self.AR[:, OFF_XN:OFF_XN + 16384],
                          waits=[self.tok_xn1])
        if st >= 2:
            self.attn_a()
            if "r2" in self.dbg:
                self.dbg_tap_R("r2")
        if st >= 4:
            tok_xn2 = self.layernorm(OFF_TAB, war_pe_tok=self.tok_oproj_pe)
            qw = [(g, g.n) for g in (b.grp("qa0"), b.grp("qa1"), b.grp("aga"))]
            self.ffn(tok_xn2, 1, qstg_war=qw)
            if "r3" in self.dbg:
                self.dbg_tap_R("r3")
        if st >= 42:
            self.tok_xn3 = self.layernorm(OFF_TAB, war_pe_tok=self.tok_ffn_pe)
            self.build_tables(reload_pos=True)
            if st >= 45:
                self.attn_b(sub=(1 if st < 48 else (2 if st < 59 else 3)))
            if "r4" in self.dbg:
                self.dbg_tap_R("r4")
        if st >= 60:
            tok_xn4 = self.layernorm(OFF_TAB, war_pe_tok=self.tok_oprojb_pe)
            qw = [(b.grp("agb"), b.grp("agb").n)]
            self.ffn(tok_xn4, 2, qstg_war=qw)
        tok_out = b.dsp("out", self.xout.rearrange("(pt p) t -> p pt t", p=128),
                        v3(self.R[:, :], 2048), waits=[b.tdve()])
        self.tap_toks.append(tok_out)
        fin = {}
        for sem, val in self.tap_toks:
            if val > fin.get(id(sem), (None, -1))[1]:
                fin[id(sem)] = (sem, val)
        b.on("sp", lambda e: e.nop(), waits=list(fin.values()), drain=False)
        return self.finalize()

    def finalize(self):
        b = self.b
        with self.ctx:
            block = self.ctx.enter_context(self.nc.Block())
            block.sync(lambda e: b.replay("sp", e))
            block.tensor(lambda e: b.replay("pe", e))
            block.scalar(lambda e: b.replay("act", e))
            block.vector(lambda e: b.replay("dve", e))
            block.gpsimd(lambda e: b.replay("pool", e))
        return self.nc


def kernel(**inputs):
    in_maps = host_prepare(inputs)
    mk = MK(stage=99)
    nc = mk.build()
    res = run_bass_kernel_spmd(nc, in_maps, core_ids=list(range(8)))
    return host_assemble(res.results)
